# version 03
# baseline (speedup 1.0000x reference)
import numpy as np
from collections import defaultdict
from contextlib import ExitStack

import ml_dtypes
import concourse.bass as bass
import concourse.mybir as mybir
from concourse.bass import IndirectOffsetOnAxis

F32 = mybir.dt.float32
BF16 = mybir.dt.bfloat16
U32 = mybir.dt.uint32
AF = mybir.ActivationFunctionType
ALU = mybir.AluOpType
AXX = mybir.AxisListType.X

D = 1024
H = 16
HP = 8
FF = 4096
VW = 132
VROW = HP * VW
LN_EPS = 1e-5
ABS_EPS = 1e-6
NEG = -60000.0


class Prog:
    def __init__(self, nc):
        self.nc = nc
        self.ops = {e: [] for e in ("tensor", "vector", "scalar", "gpsimd", "sync")}
        self.sems = {}
        self.cnt = defaultdict(int)

    def sem(self, name):
        if name not in self.sems:
            self.sems[name] = self.nc.alloc_semaphore(name)
        return self.sems[name]

    def emit(self, eng, fn, waits=(), inc=None, incval=1):
        if inc is not None:
            self.sem(inc)
            self.cnt[inc] += incval
        for (s, v) in waits:
            self.sem(s)
        self.ops[eng].append((fn, tuple(waits), inc, incval))
        return self.cnt[inc] if inc is not None else None

    def mark(self, name):
        return self.cnt[name]

    def fence(self, engines=("tensor", "vector", "scalar", "gpsimd", "sync")):
        waits = tuple((s, self.cnt[s]) for s in self.sems)
        for e in engines:
            self.ops[e].append((None, waits, None, 1))

    def realize(self):
        with self.nc.Block() as block:
            for eng_name in ("sync", "tensor", "scalar", "vector", "gpsimd"):
                ops = self.ops[eng_name]
                if not ops:
                    continue

                def body(e, ops=ops):
                    for fn, waits, inc, incval in ops:
                        for (s, v) in waits:
                            if v > 0:
                                e.wait_ge(self.sems[s], v)
                        if fn is None:
                            continue
                        ins = fn(e)
                        if inc is not None:
                            ins.then_inc(self.sems[inc], incval)
                getattr(block, eng_name)(body)


def build(SEQ=8192, debug=False):
    QTW = SEQ // 16
    NK = QTW // 128
    assert QTW % 128 == 0 and QTW <= 512
    OWN = 4 * QTW
    NT1 = SEQ // 512
    NAMB = 3 * NK
    NKS_T = SEQ // 128
    NT6 = OWN // 128
    HCN = FF // 128

    nc = bass.Bass("TRN2", target_bir_lowering=False)
    P = Prog(nc)
    E, M = P.emit, P.mark
    for s in ("pe", "dv", "ac", "gp"):
        P.sem(s)

    def inp(name, shape, dt=F32):
        return nc.dram_tensor(name, shape, dt, kind="ExternalInput").ap()

    x_kv = inp("x_kv", [SEQ, D])
    x_own = inp("x_own", [OWN, D])
    Wq_p = inp("Wq_p", [D, D])
    Wk_p = inp("Wk_p", [D, D])
    Wv_e = inp("Wv_e", [D, VROW])
    bq_p = inp("bq_p", [8, 128])
    bk_p = inp("bk_p", [8, 128])
    bv_e = inp("bv_e", [VROW])
    Wo_p = inp("Wo_p", [D, D])
    bo = inp("bo", [D])
    Wfc = inp("Wfc", [FF, D])
    Wpj = inp("Wpj", [D, FF])
    lnw = inp("lnw", [HCN, 128])
    lnb = inp("lnb", [HCN, 128])
    T1k = inp("T1k", [HP, 128, SEQ], BF16)
    T2k = inp("T2k", [HP, 128, SEQ], BF16)
    T1q = inp("T1q", [HP, 128, OWN], BF16)
    T2q = inp("T2q", [HP, 128, OWN], BF16)
    imat_i = inp("imat", [128, 64], BF16)
    ind_i = inp("ind", [HP, 128, 16], BF16)
    kidx_i = inp("kidx", [128, 32], U32)
    vidx_i = inp("vidx", [128, 32], U32)
    pflag_i = inp("pflag", [1, 4 * NAMB])

    y = nc.dram_tensor("y", [OWN, D], F32, kind="ExternalOutput").ap()

    K_dram = nc.dram_tensor("K_dram", [HP, 128, SEQ], BF16).ap()
    V_dram = nc.dram_tensor("V_dram", [HP, 128, NKS_T * VW], BF16).ap()
    xn_dram = nc.dram_tensor("xn_dram", [8, 128, OWN], BF16).ap()
    h_dram = nc.dram_tensor("h_dram", [HCN, 128, OWN], BF16).ap()
    wp_dram = nc.dram_tensor("wp_dram", [HCN, 128, D], BF16).ap()

    dbg = {}
    if debug:
        def dout(name, shape, dt=F32):
            dbg[name] = nc.dram_tensor("dbg_" + name, shape, dt, kind="ExternalOutput").ap()
        dout("K", [HP, 128, SEQ], BF16)
        dout("V", [HP, 128, NKS_T * VW], BF16)
        dout("Q", [128, HP * OWN], BF16)
        dout("xn", [8, 128, OWN], BF16)
        dout("ctx", [128, HP * OWN], BF16)
        dout("b16", [128, 16])
        dout("h", [HCN, 128, OWN], BF16)
        dout("wp", [HCN, 128, D], BF16)
        dout("gam", [128, 2])
        dout("c1", [1, D])
        dout("c0", [1, D])
        dout("stat", [3, OWN])

    esP = ExitStack()
    scope = [esP]
    sb = lambda name, shape, dt=F32: scope[0].enter_context(nc.sbuf_tensor(name, shape, dt))
    ps = lambda name, shape, dt=F32: esP.enter_context(nc.psum_tensor(name, shape, dt))

    ones_bf = sb("ones_bf", [128, 128], BF16)
    ones_f = sb("ones_f", [128, 128])
    identf = sb("identf", [128, 128])
    identb = sb("identb", [128, 128], BF16)
    ident64 = sb("ident64", [64, 64], BF16)
    epscol = sb("epscol", [128, 1])
    imat = sb("imat_s", [128, 64], BF16)
    ind16 = sb("ind16", [128, HP * 16], BF16)
    bqc = sb("bqc", [128, 8])
    bkc = sb("bkc", [128, 8])
    bvrow = sb("bvrow", [1, VROW], BF16)
    lnwc = sb("lnwc", [128, HCN])
    lnbc = sb("lnbc", [128, HCN])
    lnw_bf = sb("lnw_bf", [128, HCN], BF16)
    lnb_bf = sb("lnb_bf", [128, HCN], BF16)
    kidx = sb("kidx_s", [128, 32], U32)
    vidx = sb("vidx_s", [128, 32], U32)
    pflag = sb("pflag_s", [1, 4 * NAMB])
    bias16 = sb("bias16", [128, 16])
    biasamb = sb("biasamb", [128, 4 * NAMB * 16])
    uacc = sb("uacc", [16, 4])
    gam = sb("gam_s", [128, 2])
    c1row = sb("c1row", [1, D], BF16)
    c0row = sb("c0row", [1, D], BF16)
    statmu_b = sb("statmu_b", [1, OWN], BF16)
    statsv_b = sb("statsv_b", [1, OWN], BF16)
    statrs = sb("statrs", [1, OWN])
    Q_sb = sb("Q_sb", [128, HP * OWN], BF16)
    ctx_sb = sb("ctx_sb", [128, HP * OWN], BF16)
    urow_sc = sb("urow_sc", [16, 16])
    urow_r = sb("urow_r", [1, 16])
    pfcol = sb("pfcol", [128, 4 * NAMB])
    _rope_tab_gate = [("pe", 0)]

    E("gpsimd", lambda e: e.memset(ones_bf[:], 1.0))
    E("gpsimd", lambda e: e.memset(ones_f[:], 1.0))
    E("gpsimd", lambda e: e.memset(uacc[:], 0.0))
    E("gpsimd", lambda e: e.memset(urow_sc[:], 0.0))
    E("gpsimd", lambda e: e.memset(epscol[:], LN_EPS))
    for t, n in ((identf, 128), (identb, 128), (ident64, 64)):
        E("gpsimd", lambda e, t=t: e.memset(t[:], 0.0), inc="gp")
        E("gpsimd", lambda e, t=t, n=n: e.affine_select(
            out=t[:], in_=t[:], compare_op=ALU.not_equal, fill=1.0, base=0,
            pattern=[[-1, n]], channel_multiplier=1), waits=[("gp", M("gp"))], inc="gp")
    gp_consts = M("gp")

    E("sync", lambda e: e.dma_start(out=imat[:], in_=imat_i), inc="q_once", incval=16)
    E("sync", lambda e: e.dma_start(out=pflag[:], in_=pflag_i), inc="q_once", incval=16)
    E("sync", lambda e: e.dma_start(out=ind16[:].rearrange("p (h c) -> p h c", h=HP), in_=ind_i.rearrange("h p c -> p h c")), inc="q_once", incval=16)
    E("sync", lambda e: e.dma_start(out=kidx[:], in_=kidx_i), inc="q_once", incval=16)
    E("sync", lambda e: e.dma_start(out=vidx[:], in_=vidx_i), inc="q_once", incval=16)
    E("gpsimd", lambda e: e.dma_start(out=bvrow[:], in_=bv_e[None, :]), inc="q_onceg", incval=16)

    pb = [esP.enter_context(nc.psum_tensor(f"pb{i}", [128, 512], F32)) for i in range(8)]
    btmp = sb("btmp", [HCN, 128])
    trb = pb[0]
    for (src, dst, rows) in ((bq_p, bqc, 8), (bk_p, bkc, 8), (lnw, lnwc, HCN), (lnb, lnbc, HCN)):
        qv = E("sync", lambda e, s=src, r=rows: e.dma_start(out=btmp[0:r, :], in_=s), waits=[("pe", M("pe"))], inc="q_btmp", incval=16)
        pv = E("tensor", lambda e, r=rows: e.transpose(trb[:, 0:r], btmp[0:r, :], identf[0:r, 0:r]),
               waits=[("q_btmp", qv), ("gp", gp_consts), ("ac", M("ac"))], inc="pe")
        E("scalar", lambda e, d=dst, r=rows: e.copy(out=d[:, 0:r], in_=trb[:, 0:r]),
          waits=[("pe", pv)], inc="ac")
    v = E("vector", lambda e: e.tensor_copy(out=lnw_bf[:], in_=lnwc[:]), waits=[("ac", M("ac"))], inc="dv")
    v = E("vector", lambda e: e.tensor_copy(out=lnb_bf[:], in_=lnbc[:]), waits=[("ac", M("ac"))], inc="dv")
    dv_ln = M("dv")

    esA = ExitStack()
    scope[0] = esA
    Wq_sb = sb("Wq_sb", [128, 8 * D], BF16)
    Wk_sb = sb("Wk_sb", [128, 8 * D], BF16)
    Wv_sb = sb("Wv_sb", [128, 8 * VROW], BF16)
    for (dsb, dsrc) in ((Wq_sb, Wq_p), (Wk_sb, Wk_p), (Wv_sb, Wv_e)):
        E("gpsimd", lambda e, a=dsb, b=dsrc: e.dma_start(
            out=a[:].rearrange("p (c n) -> p c n", c=8), in_=b.rearrange("(c p) n -> p c n", p=128)),
            inc="q_onceg", incval=16)
    qg_w = M("q_onceg")

    esA0 = ExitStack()
    scope[0] = esA0
    gtile = [sb(f"gtile{i}", [128, D]) for i in range(2)]
    gabs = [sb(f"gabs{i}", [128, D], BF16) for i in range(2)]
    ginv = sb("ginv", [1, 4])
    gps_h = [pb[1], pb[2]]
    gbc = pb[3]

    def gamma_pass(view, nt, wi):
        pe_pair = [M("pe")]
        for t in range(nt):
            i = t % 2
            qv = E("sync", lambda e, t=t, i=i, view=view: e.dma_start(
                out=gtile[i][:], in_=view(t)), waits=[("pe", pe_pair[0])], inc=f"q_g{i}", incval=16)
            av = E("scalar", lambda e, i=i: e.activation(out=gabs[i][:], in_=gtile[i][:], func=AF.Abs),
                   waits=[(f"q_g{i}", qv)], inc="ac")
            for half in range(2):
                E("tensor", lambda e, i=i, t=t, half=half, nt=nt: e.matmul(
                    gps_h[half][0:1, 0:512],
                    ones_bf[:, 0:1], gabs[i][:, half * 512:(half + 1) * 512],
                    start=(t == 0), stop=(t == nt - 1)),
                    waits=[("ac", av)] + ([("dv", M("dv"))] if t == 0 else []), inc="pe")
            if t >= 1:
                pe_pair[0] = M("pe") - 2
        d = E("vector", lambda e, wi=wi: e.reduce_sum(
            ginv[0:1, wi:wi + 1], gps_h[0][0:1, 0:512], axis=AXX), waits=[("pe", M("pe"))], inc="dv")
        d = E("vector", lambda e, wi=wi: e.reduce_sum(
            ginv[0:1, wi + 2:wi + 3], gps_h[1][0:1, 0:512], axis=AXX), waits=[("pe", M("pe"))], inc="dv")
        return E("vector", lambda e, wi=wi: e.tensor_add(
            ginv[0:1, wi:wi + 1], ginv[0:1, wi:wi + 1], ginv[0:1, wi + 2:wi + 3]),
            waits=[("dv", d)], inc="dv")

    fcv = lambda t: Wfc[t * 128:(t + 1) * 128, :]
    pjv = lambda t: Wpj.rearrange("(g p) (t n) -> g t p n", p=128, n=D)[t // 4][t % 4]
    gamma_pass(fcv, FF // 128, 0)
    gamma_pass(pjv, (D // 128) * (FF // D), 1)
    dvv = E("vector", lambda e: e.tensor_scalar(out=ginv[0:1, 0:2], in0=ginv[0:1, 0:2],
                                                scalar1=1.0 / (D * FF),
                                                scalar2=ABS_EPS, op0=ALU.mult, op1=ALU.add),
            waits=[("dv", M("dv"))], inc="dv")
    dvv = E("vector", lambda e: e.reciprocal(out=ginv[0:1, 0:2], in_=ginv[0:1, 0:2]),
            waits=[("dv", dvv)], inc="dv")
    pv = E("tensor", lambda e: e.matmul(gbc[:, 0:2], ones_f[0:1, :], ginv[0:1, 0:2],
                                        start=True, stop=True), waits=[("dv", dvv)], inc="pe")
    ac_gam = E("scalar", lambda e: e.copy(out=gam[:], in_=gbc[:, 0:2]), waits=[("pe", pv)], inc="ac")
    if debug:
        E("sync", lambda e: e.dma_start(out=dbg["gam"], in_=gam[:]), waits=[("ac", ac_gam)],
          inc="q_dbg", incval=16)

    ac_tr_hold = [0]
    wp_hold = {0: 0, 1: 0}
    v4_hold = {0: 0, 1: 0}
    pj_f = [sb("pj_f0", [128, FF])] * 2
    pj_t = [sb(f"pj_t{i}", [128, FF], BF16) for i in range(2)]
    pj_s = sb("pj_s", [128, FF // 2], BF16)
    wpt = [sb(f"wpt{i}", [128, 128], BF16) for i in range(2)]
    wps = [sb(f"wps{i}", [128, 128], BF16) for i in range(2)]
    trn = pb[4][:].bitcast(BF16)
    c10 = [pb[5], pb[6]]
    c00 = [pb[7], pb[3]]
    qg_hold = [0]
    for dchunk in range(D // 128):
        i = dchunk % 2
        qv = E("sync", lambda e, d=dchunk, i=i: e.dma_start(
            out=pj_f[i][:], in_=Wpj[d * 128:(d + 1) * 128, :]),
            waits=[("dv", M("dv"))], inc=f"q_pj{i}", incval=16)
        for hf in range(2):
            c0_, c1_ = hf * (FF // 2), (hf + 1) * (FF // 2)
            v1 = E("vector", lambda e, i=i, c0_=c0_, c1_=c1_: e.tensor_scalar(
                out=pj_s[:], in0=pj_f[i][:, c0_:c1_], scalar1=gam[:, 1:2], scalar2=0.5,
                op0=ALU.mult, op1=ALU.is_gt),
                waits=[(f"q_pj{i}", qv), ("ac", ac_gam), ("dv", M("dv"))], inc="dv")
            v2 = E("vector", lambda e, i=i, c0_=c0_, c1_=c1_: e.tensor_scalar(
                out=pj_t[i][:, c0_:c1_], in0=pj_f[i][:, c0_:c1_], scalar1=gam[:, 1:2], scalar2=-0.5,
                op0=ALU.mult, op1=ALU.is_lt), waits=[(f"q_pj{i}", qv)], inc="dv")
            v3 = E("vector", lambda e, i=i, c0_=c0_, c1_=c1_: e.tensor_sub(
                pj_t[i][:, c0_:c1_], pj_s[:], pj_t[i][:, c0_:c1_]), waits=[("dv", v2)], inc="dv")
        for hc in range(HCN):
            j = hc % 2
            pv = E("tensor", lambda e, i=i, hc=hc, j=j: e.transpose(
                trn[:, j * 128:(j + 1) * 128], pj_t[i][:, hc * 128:(hc + 1) * 128], identb[:]),
                waits=[("dv", v3), ("ac", ac_tr_hold[0]), ("dv", dv_ln)], inc="pe")
            av = E("scalar", lambda e, j=j: e.copy(out=wpt[j][:], in_=trn[:, j * 128:(j + 1) * 128]),
                   waits=[("pe", pv), ("dv", v4_hold[j])], inc="ac")
            if hc >= 1:
                ac_tr_hold[0] = av - 1
            E("tensor", lambda e, j=j, hc=hc, d=dchunk: e.matmul(
                c10[d // 4][0:1, (d % 4) * 128:(d % 4 + 1) * 128], lnw_bf[:, hc:hc + 1], wpt[j][:],
                start=(hc == 0), stop=(hc == HCN - 1)), waits=[("ac", av)], inc="pe")
            E("tensor", lambda e, j=j, hc=hc, d=dchunk: e.matmul(
                c00[d // 4][0:1, (d % 4) * 128:(d % 4 + 1) * 128], lnb_bf[:, hc:hc + 1], wpt[j][:],
                start=(hc == 0), stop=(hc == HCN - 1)), inc="pe")
            v4 = E("vector", lambda e, j=j, hc=hc: e.tensor_scalar_mul(
                wps[j][:], wpt[j][:], lnwc[:, hc:hc + 1]),
                waits=[("ac", av), (f"q_wp{j}", wp_hold[j])], inc="dv")
            v4_hold[j] = v4
            qq = E("gpsimd", lambda e, j=j, hc=hc, d=dchunk: e.dma_start(
                out=wp_dram[hc, :, d * 128:(d + 1) * 128], in_=wps[j][:]),
                waits=[("dv", v4)], inc=f"q_wp{j}", incval=16)
            wp_hold[j] = qq
    pe_c = M("pe")
    av = E("scalar", lambda e: e.copy(out=c1row[0:1, 0:512], in_=c10[0][0:1, 0:512]),
           waits=[("pe", pe_c)], inc="ac")
    av = E("scalar", lambda e: e.copy(out=c1row[0:1, 512:1024], in_=c10[1][0:1, 0:512]), inc="ac")
    av = E("scalar", lambda e: e.copy(out=c0row[0:1, 0:512], in_=c00[0][0:1, 0:512]), inc="ac")
    av = E("scalar", lambda e: e.copy(out=c0row[0:1, 512:1024], in_=c00[1][0:1, 0:512]), inc="ac")
    ac_c10 = M("ac")
    if debug:
        cc1 = sb("cc1", [1, D])
        cc0 = sb("cc0", [1, D])
        E("vector", lambda e: e.tensor_copy(out=cc1[:], in_=c1row[:]), waits=[("ac", ac_c10)], inc="dv")
        E("vector", lambda e: e.tensor_copy(out=cc0[:], in_=c0row[:]), waits=[("ac", ac_c10)], inc="dv")
        E("sync", lambda e: e.dma_start(out=dbg["c1"], in_=cc1[:]), waits=[("dv", M("dv"))], inc="q_dbg", incval=16)
        E("sync", lambda e: e.dma_start(out=dbg["c0"], in_=cc0[:]), waits=[("dv", M("dv"))], inc="q_dbg", incval=16)
        E("sync", lambda e: e.dma_start(out=dbg["wp"], in_=wp_dram), waits=[("q_wp0", M("q_wp0")), ("q_wp1", M("q_wp1"))], inc="q_dbg", incval=16)

    esA0.close()
    scope[0] = esA
    xtile = [sb(f"xtile{i}", [128, D]) for i in range(2)]
    xnb = sb("xnb", [128, D], BF16)
    lncol = sb("lncol", [128, 8])
    x2s = sb("x2s", [128, D], BF16)
    xnT = [sb(f"xnT{i}", [128, 8 * 512], BF16) for i in range(2)]
    t1s = [sb("t1s0", [128, HP * 512], BF16)] * 2
    t2s = [sb("t2s0", [128, HP * 512], BF16)] * 2
    keo = sb("keo", [128, 512], BF16)
    m1 = sb("m1", [128, 512], BF16)
    m2 = sb("m2", [128, 512], BF16)
    kst = [sb(f"kst{i}", [128, 512], BF16) for i in range(2)]
    k2t = sb("k2t", [128, 512], BF16)
    vraw = [sb("vraw0", [128, 4 * VROW], BF16)] * 2
    tr_ps = pb[0][:].bitcast(BF16)
    eo_ps2 = [pb[1], pb[7]]
    ro_ps = pb[2]
    v_ps = [pb[3], pb[4], pb[5]]
    u_ps = pb[6]

    def ln_and_transpose(x_src, tt, xnT_buf, marks, tw, key):
        for sub in range(tw // 128):
            yield
            i = sub % 2
            r0 = tt * tw + sub * 128
            qv = E("sync", lambda e, i=i, r0=r0, x_src=x_src: e.dma_start(
                out=xtile[i][:], in_=x_src[r0:r0 + 128, :]),
                waits=[("ac", marks.get(f"x_reuse{i}", 0))], inc=f"q_x{i}", incval=16)
            av0 = E("scalar", lambda e, i=i: e.activation(
                out=x2s[:], in_=xtile[i][:], func=AF.Identity, accum_out=lncol[:, 0:1]),
                waits=[(f"q_x{i}", qv), ("dv", marks.get("x2_reuse", 0))], inc="ac")
            dv0 = E("vector", lambda e, i=i: e.scalar_tensor_tensor(
                out=x2s[:], in0=xtile[i][:], scalar=1.0, in1=xtile[i][:],
                op0=ALU.mult, op1=ALU.mult, accum_out=lncol[:, 1:2]),
                waits=[("ac", av0)], inc="dv")
            marks["x2_reuse"] = dv0
            dv1 = E("vector", lambda e: e.tensor_scalar_mul(lncol[:, 2:3], lncol[:, 0:1], 1.0 / D),
                    waits=[("dv", dv0)], inc="dv")
            dv2 = E("vector", lambda e: e.scalar_tensor_tensor(
                out=lncol[:, 3:4], in0=lncol[:, 2:3], scalar=-1.0, in1=lncol[:, 2:3],
                op0=ALU.mult, op1=ALU.mult), waits=[("dv", dv1)], inc="dv")
            dv3 = E("vector", lambda e: e.scalar_tensor_tensor(
                out=lncol[:, 3:4], in0=lncol[:, 1:2], scalar=1.0 / D, in1=lncol[:, 3:4],
                op0=ALU.mult, op1=ALU.add), waits=[("dv", dv2)], inc="dv")
            av1 = E("scalar", lambda e: e.activation(out=lncol[:, 4:5], in_=lncol[:, 3:4],
                                                     func=AF.Ln, bias=epscol[:]), waits=[("dv", dv3)], inc="ac")
            av2 = E("scalar", lambda e: e.activation(out=lncol[:, 4:5], in_=lncol[:, 4:5],
                                                     func=AF.Exp, scale=-0.5), waits=[("ac", av1)], inc="ac")
            dv4 = E("vector", lambda e: e.scalar_tensor_tensor(
                out=lncol[:, 5:6], in0=lncol[:, 2:3], scalar=-1.0, in1=lncol[:, 4:5],
                op0=ALU.mult, op1=ALU.mult), waits=[("ac", av2)], inc="dv")
            av3 = E("scalar", lambda e, i=i: e.activation(
                out=xnb[:], in_=xtile[i][:], func=AF.Identity,
                scale=lncol[:, 4:5], bias=lncol[:, 5:6]), waits=[("dv", dv4)], inc="ac")
            marks[f"x_reuse{i}"] = av3
            for fc in range(8):
                pv = E("tensor", lambda e, fc=fc: e.transpose(
                    tr_ps[:, fc * 128:(fc + 1) * 128], xnb[:, fc * 128:(fc + 1) * 128], identb[:]),
                    waits=[("ac", av3)] + ([("ac", marks.get("tr_evac", 0))] if fc == 0 else []),
                    inc="pe")
            av4 = E("scalar", lambda e, sub=sub, xnT_buf=xnT_buf, tw=tw: e.copy(
                out=xnT_buf.rearrange("p (f t) -> p f t", f=8)[:, :, sub * 128:(sub + 1) * 128],
                in_=tr_ps[:].rearrange("p (f t) -> p f t", f=8)),
                waits=[("pe", pv), ("pe", marks.get(key + "_pefree", 0))], inc="ac")
            marks["tr_evac"] = av4
            marks[key] = av4

    def rope_project(xnT_buf, W_sb, bias_c, T1buf, T2buf, dst_kind, tt, tw, marks, dbg_norm=None):
        def emit_proj(hp):
            ei = hp % 2
            for fc in range(8):
                E("tensor", lambda e, hp=hp, fc=fc, tw=tw, ei=ei: e.matmul(
                    eo_ps2[ei][:, 0:tw], W_sb[:, fc * D + hp * 128: fc * D + (hp + 1) * 128],
                    xnT_buf[:, fc * tw: fc * tw + tw], start=(fc == 0), stop=(fc == 7)),
                    waits=([("ac", marks.get(f"keo_reuse{ei}", 0)), ("ac", marks.get("tr_evac", 0))]
                           + ([("q_onceg", qg_w)] if hp == 0 else []) if fc == 0 else []),
                    inc="pe")
            return M("pe")

        pmark = {0: emit_proj(0)}
        filler = marks.pop("filler", None)
        for hp in range(HP):
            ei = hp % 2
            if hp + 1 < HP:
                pmark[hp + 1] = emit_proj(hp + 1)
            if filler is not None:
                filler(hp)
            av = E("scalar", lambda e, hp=hp, tw=tw, bias_c=bias_c, ei=ei: e.activation(
                out=keo[:, 0:tw], in_=eo_ps2[ei][:, 0:tw], func=AF.Identity,
                bias=bias_c[:, hp:hp + 1]),
                waits=[("pe", pmark[hp]), ("dv", marks.get("keo_dv", 0))], inc="ac")
            marks[f"keo_reuse{ei}"] = av
            d1 = E("vector", lambda e, hp=hp, tw=tw, T1buf=T1buf: e.tensor_mul(
                m1[:, 0:tw], keo[:, 0:tw], T1buf[:, hp * tw: hp * tw + tw]),
                waits=[("ac", av), ("pe", marks.get("m_reuse", 0)),
                       _rope_tab_gate[0]], inc="dv")
            d2 = E("vector", lambda e, hp=hp, tw=tw, T2buf=T2buf: e.tensor_mul(
                m2[:, 0:tw], keo[:, 0:tw], T2buf[:, hp * tw: hp * tw + tw]), inc="dv")
            marks["keo_dv"] = d2
            pv1 = E("tensor", lambda e, tw=tw: e.matmul(
                ro_ps[0:64, 0:tw], imat[:], m1[:, 0:tw], start=True, stop=True),
                waits=[("dv", d2), ("ac", marks.get("ro_reuse", 0))], inc="pe")
            pv2 = E("tensor", lambda e, tw=tw: e.matmul(
                ro_ps[64:128, 0:tw], imat[:], m2[:, 0:tw], start=True, stop=True,
                tile_position=(0, 64)), inc="pe")
            marks["m_reuse"] = pv2
            if dst_kind[0] == 'K':
                i = hp % 2
                av2 = E("scalar", lambda e, i=i, tw=tw: e.copy(out=kst[i][:, 0:tw], in_=ro_ps[:, 0:tw]),
                        waits=[("pe", pv2), (f"q_ko{i}", marks.get(f"kst_reuse{i}", 0))], inc="ac")
                marks["ro_reuse"] = av2
                qv = E("sync", lambda e, i=i, hp=hp, tt=tt, tw=tw: e.dma_start(
                    out=K_dram[hp, :, tt * tw:(tt + 1) * tw], in_=kst[i][:, 0:tw]),
                    waits=[("ac", av2)], inc=f"q_ko{i}", incval=16)
                marks[f"kst_reuse{i}"] = qv
                ksrc = kst[i]
            else:
                av2 = E("scalar", lambda e, hp=hp, tt=tt, tw=tw: e.copy(
                    out=Q_sb[:, hp * OWN + tt * tw: hp * OWN + (tt + 1) * tw], in_=ro_ps[:, 0:tw]),
                    waits=[("pe", pv2)], inc="ac")
                marks["ro_reuse"] = av2
                ksrc = None
            src = ksrc[:, 0:tw] if ksrc is not None else Q_sb[:, hp * OWN + tt * tw: hp * OWN + (tt + 1) * tw]
            d3 = E("vector", lambda e, src=src, tw=tw: e.tensor_mul(k2t[:, 0:tw], src, src),
                   waits=[("ac", av2), ("pe", marks.get("k2_reuse", 0))], inc="dv")
            pu = E("tensor", lambda e, hp=hp, tw=tw: e.matmul(
                u_ps[0:16, 0:tw], ind16[:, hp * 16:(hp + 1) * 16], k2t[:, 0:tw],
                start=(hp == 0), stop=(hp == HP - 1)),
                waits=[("dv", d3)] + ([("dv", marks.get("u_dv", 0))] if hp == 0 else []), inc="pe")
            marks["k2_reuse"] = pu
        ucol = 0 if dst_kind[0] == 'K' else 1
        dm = E("vector", lambda e, tw=tw: e.reduce_max(uacc[:, 3:4], u_ps[0:16, 0:tw], axis=AXX),
               waits=[("pe", M("pe"))], inc="dv")
        dm = E("vector", lambda e, ucol=ucol: e.tensor_max(uacc[:, ucol:ucol + 1],
                                                           uacc[:, ucol:ucol + 1], uacc[:, 3:4]),
               waits=[("dv", dm)], inc="dv")
        marks["u_dv"] = dm
        return marks

    def v_build_sub(xnT_buf, tt, tw, marks, subs):
        i = 0
        segs = [(0, 512), (512, 1024), (1024, VROW)]
        for sub in subs:
            for fc in range(8):
                for bi, (c0, c1) in enumerate(segs):
                    E("tensor", lambda e, fc=fc, bi=bi, c0=c0, c1=c1, sub=sub, tw=tw: e.matmul(
                        v_ps[bi][:, 0:c1 - c0],
                        xnT_buf[:, fc * tw + sub * 128: fc * tw + (sub + 1) * 128],
                        Wv_sb[:, fc * VROW + c0: fc * VROW + c1],
                        start=(fc == 0), stop=False),
                        waits=([("ac", marks.get("vps_reuse", 0)),
                                ("ac", marks.get("tr_evac", 0))] if fc == 0 and bi == 0 else []),
                        inc="pe")
            for bi, (c0, c1) in enumerate(segs):
                E("tensor", lambda e, bi=bi, c0=c0, c1=c1: e.matmul(
                    v_ps[bi][:, 0:c1 - c0], ones_bf[0:1, :], bvrow[0:1, c0:c1],
                    start=False, stop=True), inc="pe")
            pv = M("pe")
            for bi, (c0, c1) in enumerate(segs):
                E("scalar", lambda e, bi=bi, c0=c0, c1=c1, sub=sub, i=i: e.copy(
                    out=vraw[i][:, sub * VROW + c0: sub * VROW + c1], in_=v_ps[bi][:, 0:c1 - c0]),
                    waits=[("pe", pv), (f"q_vo{i}", marks.get(f"vraw_q{i}", 0))], inc="ac")
            marks["vps_reuse"] = M("ac")

    def v_build_out(tt, tw, marks):
        i = 0
        av = M("ac")
        nsub = tw // 128
        for hp in range(HP):
            qv = E("sync", lambda e, hp=hp, tt=tt, nsub=nsub, i=i: e.dma_start(
                out=V_dram[hp, :, tt * nsub * VW:(tt + 1) * nsub * VW]
                    .rearrange("p (s v) -> p s v", v=VW),
                in_=vraw[i][:].rearrange("p (s v) -> p s v", v=VROW)[:, 0:nsub, hp * VW:(hp + 1) * VW]),
                waits=[("ac", av)], inc=f"q_vo{i}", incval=16)
        marks[f"vraw_q{i}"] = M(f"q_vo{i}")
        return M("qs")

    P.fence()
    marks = {"tr_evac": M("ac"), "keo_reuse0": M("ac"), "keo_reuse1": M("ac"),
             "ro_reuse": M("ac"),
             "vps_reuse": M("ac"), "x2_reuse": M("dv"), "u_dv": M("dv")}

    def emit_tables(tt, src1, src2, tw):
        q1 = E("sync", lambda e, tt=tt, src1=src1, tw=tw: e.dma_start(
            out=t1s[0][:].rearrange("p (h t) -> p h t", h=HP)[:, :, 0:tw],
            in_=src1[:, :, tt * tw:(tt + 1) * tw].rearrange("h p t -> p h t")),
            waits=[("dv", marks.get("tab_reuse", 0))], inc="q_t0", incval=16)
        E("sync", lambda e, tt=tt, src2=src2, tw=tw: e.dma_start(
            out=t2s[0][:].rearrange("p (h t) -> p h t", h=HP)[:, :, 0:tw],
            in_=src2[:, :, tt * tw:(tt + 1) * tw].rearrange("h p t -> p h t")),
            inc="q_t0", incval=16)
        marks["tab_q"] = ("q_t0", M("q_t0"))

    gen = ln_and_transpose(x_kv, 0, xnT[0], marks, 512, "lnk0")
    for _ in gen:
        pass
    for tt in range(NT1):
        i = tt % 2
        ni = (tt + 1) % 2
        emit_tables(tt, T1k, T2k, 512)
        _rope_tab_gate[0] = marks["tab_q"]
        if tt + 1 < NT1:
            marks[f"lnk{tt+1}_pefree"] = M("pe")
            nxt = ln_and_transpose(x_kv, tt + 1, xnT[ni], marks, 512, f"lnk{tt+1}")
        else:
            nxt = iter(())

        def filler(hp, tt=tt, nxt=nxt):
            if hp % 2 == 1:
                v_build_sub(xnT[tt % 2][:], tt, 512, marks, [(hp - 1) // 2])
            next(nxt, None)

        marks["filler"] = filler
        rope_project(xnT[i][:], Wk_sb, bkc, t1s[0][:], t2s[0][:], ('K', tt), tt, 512, marks)
        marks["tab_reuse"] = M("dv")
        for _ in nxt:
            pass
        v_build_out(tt, 512, marks)
    p1_done_q = M("qs")
    p1_done_pe = M("pe")

    QW = min(512, OWN)
    for tt in range(OWN // QW):
        i = tt % 2
        emit_tables(tt, T1q, T2q, QW)
        _rope_tab_gate[0] = marks["tab_q"]
        marks[f"lnq{tt}_pefree"] = M("pe")
        for _ in ln_and_transpose(x_own, tt, xnT[i], marks, QW, f"lnq{tt}"):
            pass
        av = M("ac")
        for fc in range(8):
            E("sync", lambda e, fc=fc, tt=tt, i=i, QW=QW: e.dma_start(
                out=xn_dram[fc, :, tt * QW:(tt + 1) * QW],
                in_=xnT[i][:, fc * QW:(fc + 1) * QW]), waits=[("ac", av)], inc="q_xn", incval=16)
        rope_project(xnT[i][:], Wq_sb, bqc, t1s[0][:], t2s[0][:], ('Q', tt), tt, QW, marks)
        marks["tab_reuse"] = M("dv")
    p1b_done = M("ac")

    P.fence()
    d0 = E("vector", lambda e: e.tensor_mul(uacc[:, 2:3], uacc[:, 0:1], uacc[:, 1:2]),
           waits=[("dv", M("dv")), ("ac", p1b_done)], inc="dv")
    du = E("vector", lambda e: e.tensor_copy(out=urow_sc[:, 0:1], in_=uacc[:, 2:3]),
           waits=[("dv", d0)], inc="dv")
    pu = E("tensor", lambda e: e.transpose(pb[7][0:16, 0:16], urow_sc[0:16, 0:16],
                                           identf[0:16, 0:16]),
           waits=[("dv", du)], inc="pe")
    av = E("scalar", lambda e: e.copy(out=urow_r[0:1, 0:16], in_=pb[7][0:1, 0:16]),
           waits=[("pe", pu)], inc="ac")
    pu2 = E("tensor", lambda e: e.matmul(pb[7][:, 0:16], ones_f[0:1, :], urow_r[0:1, 0:16],
                                         start=True, stop=True), waits=[("ac", av)], inc="pe")
    av2 = E("scalar", lambda e: e.activation(out=bias16[:], in_=pb[7][:, 0:16], func=AF.Ln),
           waits=[("pe", pu2)], inc="ac")
    av3 = E("scalar", lambda e: e.activation(out=bias16[:], in_=bias16[:], func=AF.Exp, scale=0.5),
            waits=[("ac", av2)], inc="ac")
    d1 = E("vector", lambda e: e.tensor_scalar_mul(bias16[:], bias16[:], -0.125),
          waits=[("ac", av3)], inc="dv")
    pu3 = E("tensor", lambda e: e.matmul(pb[6][:, 0:4 * NAMB], ones_f[0:1, :], pflag[0:1, :],
                                         start=True, stop=True), inc="pe")
    av4 = E("scalar", lambda e: e.copy(out=pfcol[:], in_=pb[6][:, 0:4 * NAMB]),
           waits=[("pe", pu3)], inc="ac")
    for a in range(4 * NAMB):
        d2 = E("vector", lambda e, a=a: e.tensor_scalar_add(
            biasamb[:, a * 16:(a + 1) * 16], bias16[:], pfcol[:, a:a + 1]),
            waits=[("dv", d1), ("ac", av4)], inc="dv")
    u_done_dv = M("dv")
    if debug:
        E("sync", lambda e: e.dma_start(out=dbg["b16"], in_=bias16[:]), waits=[("dv", u_done_dv)],
          inc="q_dbg", incval=16)

    P.fence()
    esA.close()
    esB = ExitStack()
    scope[0] = esB
    KEYS = lambda s: (s + 1) * 4 * QTW
    Kbuf = [sb(f"Kbuf{j}", [128, 4 * QTW * 4], BF16) for j in range(2)]
    Vbuf = [sb(f"Vbuf{j}", [128, (4 * QTW * 4 // 128) * VW], BF16) for j in range(2)]
    Kd = [sb(f"Kd{j}", [128, NK * 128], BF16) for j in range(2)]
    Vd = [sb(f"Vd{j}", [128, NK * VW], BF16) for j in range(2)]
    Pbuf = [sb(f"Pbuf{j}", [128, 2 * QTW], BF16) for j in range(2)]
    rsc = sb("rsc", [128, 4 * QTW], BF16)
    ctmp = sb("ctmp", [128, 4 * QTW], BF16)
    sA = [pb[0], pb[1]]
    sB = [pb[2], pb[3]]
    cA, cB = pb[4], pb[5]
    dvl, dvh = pb[6], pb[7]

    def p3_loads(u, marks):
        s, hp = divmod(u, HP)
        j = u % 2
        kq = E("sync", lambda e, s=s, hp=hp, j=j: e.dma_start(
            out=Kbuf[j][:, 0:KEYS(s)], in_=K_dram[hp, :, 0:KEYS(s)]),
            waits=[("ac", marks.get(f"kbuf_free{j}", 0))] +
                  ([("q_ko0", M("q_ko0")), ("q_ko1", M("q_ko1"))] if (s, hp) == (0, 0) else []),
            inc=f"q_K{j}", incval=16)
        vq = E("scalar", lambda e, s=s, hp=hp, j=j: e.dma_start(
            out=Vbuf[j][:, 0:(KEYS(s) // 128) * VW], in_=V_dram[hp, :, 0:(KEYS(s) // 128) * VW]),
            waits=([("q_vo0", M("q_vo0")), ("q_vo1", M("q_vo1"))] if (s, hp) == (0, 0) else []),
            inc=f"q_V{j}", incval=16)
        col = s * HP + hp
        kdq = E("gpsimd", lambda e, col=col, j=j: e.indirect_dma_start(
            out=Kd[j][:], out_offset=None, in_=K_dram.rearrange("h p s -> (h p) s"),
            in_offset=IndirectOffsetOnAxis(ap=kidx[:, col:col + 1], axis=1)),
            waits=[("ac", marks.get(f"kbuf_free{j}", 0))], inc=f"q_d{j}", incval=16)
        vdq = E("gpsimd", lambda e, col=col, j=j: e.indirect_dma_start(
            out=Vd[j][:], out_offset=None, in_=V_dram.rearrange("h p s -> (h p) s"),
            in_offset=IndirectOffsetOnAxis(ap=vidx[:, col:col + 1], axis=1)),
            inc=f"q_d{j}", incval=16)
        marks[f"kload{j}"] = (kq, vq, kdq, vdq)

    NUNITS = 4 * HP
    m3 = {"exp": {}, "pv": {}, "ctx_free": 0, "dv_free": 0}
    p3_loads(0, m3)
    for u in range(NUNITS):
        s, hp = divmod(u, HP)
        j = u % 2
        if u + 1 < NUNITS:
            p3_loads(u + 1, m3)
        kq, vq, kdq, vdq = m3[f"kload{j}"]
        qcol = hp * OWN + s * QTW
        ntile = NK + (4 * s + 3) * NK

        def src_of(kc, j=j):
            if kc < NK:
                return (lambda kc=kc: Kd[j][0:64, kc * 128:(kc + 1) * 128],
                        lambda kc=kc: Kd[j][64:128, kc * 128:(kc + 1) * 128],
                        kc * VW, Vd[j])
            ki = kc - NK
            return (lambda ki=ki: Kbuf[j][0:64, ki * 128:(ki + 1) * 128],
                    lambda ki=ki: Kbuf[j][64:128, ki * 128:(ki + 1) * 128],
                    ki * VW, Vbuf[j])

        def emit_S(kc):
            it = kc % 2
            klo, khi, _, _ = src_of(kc)
            w_s = [("ac", m3["exp"].get((u, kc - 2), m3["exp"].get("prev_last", 0) if kc < 2 else 0))]
            if kc == 0:
                w_s += [(f"q_K{j}", kq), (f"q_d{j}", vdq), ("dv", m3["dv_free"])]
            E("tensor", lambda e, klo=klo, it=it, qcol=qcol: e.matmul(
                sA[it][:, 0:QTW], klo(), Q_sb[0:64, qcol:qcol + QTW], start=True, stop=True),
                waits=w_s, inc="pe")
            return E("tensor", lambda e, khi=khi, it=it, qcol=qcol: e.matmul(
                sB[it][:, 0:QTW], khi(), Q_sb[64:128, qcol:qcol + QTW], start=True, stop=True,
                tile_position=(64, 0)), inc="pe")

        m3["smark"] = {0: emit_S(0)}
        for kc in range(ntile):
            it = kc % 2
            diag = kc < NK
            _, _, vcb, vsrc = src_of(kc)
            if kc + 1 < ntile:
                m3["smark"][kc + 1] = emit_S(kc + 1)
            if diag or (kc - NK) < 4 * s * NK:
                blo = lambda hp=hp: bias16[:, hp:hp + 1]
                bhi = lambda hp=hp: bias16[:, 8 + hp:8 + hp + 1]
            else:
                a = s * NAMB + (kc - NK - 4 * s * NK)
                blo = lambda a=a, hp=hp: biasamb[:, a * 16 + hp:a * 16 + hp + 1]
                bhi = lambda a=a, hp=hp: biasamb[:, a * 16 + 8 + hp:a * 16 + 8 + hp + 1]
            wexp = [("pe", m3["smark"][kc])]
            if (u, kc - 2) in m3["pv"]:
                wexp.append(("pe", m3["pv"][(u, kc - 2)]))
            e1 = E("scalar", lambda e, it=it, blo=blo: e.activation(
                out=Pbuf[it][:, 0:QTW], in_=sA[it][:, 0:QTW], func=AF.Exp, scale=0.125,
                bias=blo()), waits=wexp, inc="ac")
            e2 = E("scalar", lambda e, it=it, bhi=bhi: e.activation(
                out=Pbuf[it][:, QTW:2 * QTW], in_=sB[it][:, 0:QTW], func=AF.Exp, scale=0.125,
                bias=bhi()), inc="ac")
            m3["exp"][(u, kc)] = e2
            wpv = [("ac", e2)]
            if diag:
                g1 = E("gpsimd", lambda e, it=it, kc=kc: e.affine_select(
                    out=Pbuf[it][:, 0:QTW], in_=Pbuf[it][:, 0:QTW], compare_op=ALU.is_ge,
                    fill=0.0, base=-(kc * 128), pattern=[[1, QTW]], channel_multiplier=-1),
                    waits=[("ac", e2)], inc="gp")
                g2 = E("gpsimd", lambda e, it=it, kc=kc: e.affine_select(
                    out=Pbuf[it][:, QTW:2 * QTW], in_=Pbuf[it][:, QTW:2 * QTW], compare_op=ALU.is_ge,
                    fill=0.0, base=-(kc * 128), pattern=[[1, QTW]], channel_multiplier=-1),
                    inc="gp")
                wpv = [("gp", g2)]
            if kc == 0:
                wpv.append((f"q_V{j}", vq))
                wpv.append(("dv", m3["ctx_free"]))
            p1_ = E("tensor", lambda e, vsrc=vsrc, vcb=vcb, it=it, kc=kc, ntile=ntile: e.matmul(
                cA[0:65, 0:QTW], vsrc[:, vcb:vcb + 65], Pbuf[it][:, 0:QTW],
                start=(kc == 0), stop=(kc == ntile - 1)), waits=wpv, inc="pe")
            p2_ = E("tensor", lambda e, vsrc=vsrc, vcb=vcb, it=it, kc=kc, ntile=ntile: e.matmul(
                cB[0:65, 0:QTW], vsrc[:, vcb + 65:vcb + 130], Pbuf[it][:, QTW:2 * QTW],
                start=(kc == 0), stop=(kc == ntile - 1)), inc="pe")
            m3["pv"][(u, kc)] = p2_
        m3["exp"]["prev_last"] = m3["exp"][(u, ntile - 1)]
        pv_done = m3["pv"][(u, ntile - 1)]
        a1 = E("scalar", lambda e: e.activation(out=rsc[64:65, 0:QTW], in_=cA[64:65, 0:QTW],
                                                func=AF.Ln), waits=[("pe", pv_done)], inc="ac")
        a2 = E("scalar", lambda e: e.activation(out=rsc[64:65, QTW:2 * QTW], in_=rsc[64:65, 0:QTW],
                                                func=AF.Exp, scale=-1.0), waits=[("ac", a1)], inc="ac")
        a3 = E("scalar", lambda e: e.activation(out=rsc[64:65, 2 * QTW:3 * QTW], in_=cB[64:65, 0:QTW],
                                                func=AF.Ln), waits=[("pe", pv_done)], inc="ac")
        a4 = E("scalar", lambda e: e.activation(out=rsc[64:65, 3 * QTW:4 * QTW], in_=rsc[64:65, 2 * QTW:3 * QTW],
                                                func=AF.Exp, scale=-1.0), waits=[("ac", a3)], inc="ac")
        pdl = E("tensor", lambda e: e.matmul(dvl[0:64, 0:QTW], ones_bf[64:65, 0:64],
                                             rsc[64:65, QTW:2 * QTW], start=True, stop=True,
                                             tile_position=(64, 0)),
                waits=[("ac", a2), ("dv", m3["dv_free"])], inc="pe")
        pdh = E("tensor", lambda e: e.matmul(dvh[0:64, 0:QTW], ones_bf[64:65, 0:64],
                                             rsc[64:65, 3 * QTW:4 * QTW], start=True, stop=True,
                                             tile_position=(64, 0)), waits=[("ac", a4)], inc="pe")
        d1 = E("vector", lambda e: e.tensor_copy(out=ctmp[0:64, 0:QTW], in_=cA[0:64, 0:QTW]),
               waits=[("pe", pdl)], inc="dv")
        d2 = E("vector", lambda e, qcol=qcol: e.tensor_mul(
            ctx_sb[0:64, qcol:qcol + QTW], ctmp[0:64, 0:QTW], dvl[0:64, 0:QTW]),
            waits=[("dv", d1)], inc="dv")
        d3 = E("vector", lambda e: e.tensor_copy(out=ctmp[0:64, QTW:2 * QTW], in_=cB[0:64, 0:QTW]),
               waits=[("pe", pdh)], inc="dv")
        d4 = E("vector", lambda e: e.tensor_mul(
            ctmp[0:64, 2 * QTW:3 * QTW], ctmp[0:64, QTW:2 * QTW], dvh[0:64, 0:QTW]),
            waits=[("dv", d3)], inc="dv")
        m3["ctx_free"] = d4
        ib = (ntile) % 2
        pid = E("tensor", lambda e, ib=ib: e.matmul(
            sA[ib][64:128, 0:QTW], ident64[:], ctmp[0:64, 2 * QTW:3 * QTW],
            start=True, stop=True, tile_position=(0, 64)),
            waits=[("dv", d4), ("ac", m3["exp"]["prev_last"])], inc="pe")
        a5 = E("scalar", lambda e, ib=ib, qcol=qcol: e.copy(
            out=ctx_sb[64:128, qcol:qcol + QTW], in_=sA[ib][64:128, 0:QTW]),
            waits=[("pe", pid)], inc="ac")
        m3["exp"]["prev_last"] = a5
        m3["dv_free"] = d4
        m3[f"kbuf_free{j}"] = a5
    p3_done_ac = M("ac")
    p3_done_dv = M("dv")
    if debug:
        E("sync", lambda e: e.dma_start(out=dbg["ctx"], in_=ctx_sb[:]),
          waits=[("ac", p3_done_ac), ("dv", p3_done_dv)], inc="q_dbg", incval=16)
        E("sync", lambda e: e.dma_start(out=dbg["Q"], in_=Q_sb[:]), inc="q_dbg", incval=16)
        E("sync", lambda e: e.dma_start(out=dbg["K"], in_=K_dram), inc="q_dbg", incval=16)
        E("sync", lambda e: e.dma_start(out=dbg["V"], in_=V_dram), inc="q_dbg", incval=16)
        E("sync", lambda e: e.dma_start(out=dbg["xn"], in_=xn_dram), inc="q_dbg", incval=16)

    P.fence()
    esB.close()
    esC = ExitStack()
    scope[0] = esC
    xn5t = [sb(f"xn5t{i}", [128, 8 * 512], BF16) for i in range(2)]
    wfcT = sb("wfcT", [128, FF], BF16)
    wfcT2 = sb("wfcT2", [128, 7 * FF], BF16)
    wf = [sb("wf0", [128, D])] * 2
    wfs = sb("wfs", [128, D // 2], BF16)
    wft = [sb(f"wft{i}", [128, D], BF16) for i in range(2)]
    hbuf = [sb(f"hbuf{i}", [128, 512], BF16) for i in range(2)]
    h2buf = sb("h2buf", [128, 512], BF16)
    srow = sb("srow", [1, 1536])
    statmu = sb("statmu", [1, OWN])
    statsv = sb("statsv", [1, OWN])

    def wfcT_tile(fc, hc):
        base = hc * 128
        if fc == 0:
            return wfcT[:, base:base + 128]
        return wfcT2[:, (fc - 1) * FF + base:(fc - 1) * FF + base + 128]


    trn5 = pb[0][:].bitcast(BF16)
    fcp = [pb[1], pb[2]]
    stp_s, stp_q = pb[3], pb[4]
    NT5 = OWN // 512
    m5 = {"wft_free": {0: 0, 1: 0}, "trn_free": 0, "hb_free": {0: 0, 1: 0}}
    for tt in range(NT5):
        i5 = tt % 2
        xq = E("sync", lambda e, tt=tt, i5=i5: e.dma_start(
            out=xn5t[i5][:].rearrange("p (f t) -> p f t", f=8),
            in_=xn_dram[:, :, tt * 512:(tt + 1) * 512].rearrange("f p t -> p f t")),
            waits=[("q_xn", M("q_xn"))] + ([("pe", m5[f"xn5_free{i5}"])] if f"xn5_free{i5}" in m5 else []),
            inc=f"q_xt{i5}", incval=16)
        xn5_q = ("q_xt%d" % i5, xq)
        for hc in range(HCN):
            i = hc % 2
            if tt == 0:
                qv = E("sync", lambda e, hc=hc, i=i: e.dma_start(
                    out=wf[i][:], in_=Wfc[hc * 128:(hc + 1) * 128, :]),
                    waits=[("dv", M("dv"))], inc="q_wf0", incval=16)
                for hf in range(2):
                    c0_, c1_ = hf * (D // 2), (hf + 1) * (D // 2)
                    v1 = E("vector", lambda e, i=i, c0_=c0_, c1_=c1_: e.tensor_scalar(
                        out=wfs[:], in0=wf[i][:, c0_:c1_], scalar1=gam[:, 0:1], scalar2=0.5,
                        op0=ALU.mult, op1=ALU.is_gt),
                        waits=[("q_wf0", qv), ("dv", M("dv"))], inc="dv")
                    v2 = E("vector", lambda e, i=i, c0_=c0_, c1_=c1_: e.tensor_scalar(
                        out=wft[i][:, c0_:c1_], in0=wf[i][:, c0_:c1_], scalar1=gam[:, 0:1],
                        scalar2=-0.5, op0=ALU.mult, op1=ALU.is_lt), waits=[("q_wf0", qv)], inc="dv")
                    v3 = E("vector", lambda e, i=i, c0_=c0_, c1_=c1_: e.tensor_sub(
                        wft[i][:, c0_:c1_], wfs[:], wft[i][:, c0_:c1_]), waits=[("dv", v2)], inc="dv")
                m5["wft_free"][i] = v3
                pts = []
                for fc in range(8):
                    pt = E("tensor", lambda e, i=i, fc=fc: e.transpose(
                        trn5[:, fc * 128:(fc + 1) * 128], wft[i][:, fc * 128:(fc + 1) * 128],
                        identb[:]),
                        waits=([("dv", v3), ("ac", m5["trn_free"])] if fc == 0 else []), inc="pe")
                av = E("scalar", lambda e, hc=hc: e.copy(out=wfcT[:, hc * 128:(hc + 1) * 128],
                                                         in_=trn5[:, 0:128]),
                       waits=[("pe", M("pe"))], inc="ac")
                for fc in range(1, 8):
                    av = E("scalar", lambda e, hc=hc, fc=fc: e.copy(
                        out=wfcT2[:, (fc - 1) * FF + hc * 128:(fc - 1) * FF + hc * 128 + 128],
                        in_=trn5[:, fc * 128:(fc + 1) * 128]), inc="ac")
                m5["trn_free"] = av
                m5[f"wfcT_ready{hc}"] = av
            fi = hc % 2
            wfcq = [("ac", m5[f"wfcT_ready{hc}"]), xn5_q]
            for fc in range(8):
                E("tensor", lambda e, fc=fc, hc=hc, tt=tt, fi=fi, i5=i5: e.matmul(
                    fcp[fi][:, 0:512], wfcT_tile(fc, hc), xn5t[i5][:, fc * 512:(fc + 1) * 512],
                    start=(fc == 0), stop=(fc == 7)),
                    waits=(wfcq + [("ac", m5[f"fcp_free{fi}"] if f"fcp_free{fi}" in m5 else 0)]) if fc == 0 else [],
                    inc="pe")
            pv = M("pe")
            hb = hc % 2
            a1 = E("scalar", lambda e, hb=hb, fi=fi: e.activation(
                out=hbuf[hb][:], in_=fcp[fi][:, 0:512], func=AF.Gelu),
                waits=[("pe", pv), (f"q_h{hb}", m5["hb_free"][hb]), ("dv", m5.get(f"h2_{hb}", 0))], inc="ac")
            m5[f"fcp_free{fi}"] = a1
            d1 = E("vector", lambda e, hb=hb: e.tensor_mul(h2buf[:], hbuf[hb][:], hbuf[hb][:]),
                   waits=[("ac", a1), ("pe", m5.get("st_pe", 0))], inc="dv")
            m5[f"h2_{hb}"] = d1
            E("tensor", lambda e, hc=hc, hb=hb: e.matmul(
                stp_s[0:1, 0:512], ones_bf[:, 0:1], hbuf[hb][:],
                start=(hc == 0), stop=(hc == HCN - 1)), waits=[("ac", a1)], inc="pe")
            E("tensor", lambda e, hc=hc: e.matmul(
                stp_q[0:1, 0:512], ones_bf[:, 0:1], h2buf[:],
                start=(hc == 0), stop=(hc == HCN - 1)), waits=[("dv", d1)], inc="pe")
            m5["st_pe"] = M("pe")
            qh = E("sync", lambda e, hc=hc, tt=tt, hb=hb: e.dma_start(
                out=h_dram[hc, :, tt * 512:(tt + 1) * 512], in_=hbuf[hb][:]),
                waits=[("ac", a1)], inc=f"q_h{hb}", incval=16)
            m5["hb_free"][hb] = qh
        m5[f"xn5_free{i5}"] = M("pe")
        pv = M("pe")
        a1 = E("scalar", lambda e, tt=tt: e.activation(
            out=statmu[0:1, tt * 512:(tt + 1) * 512], in_=stp_s[0:1, 0:512],
            func=AF.Identity, scale=-1.0 / FF), waits=[("pe", pv)], inc="ac")
        d1 = E("vector", lambda e, tt=tt: e.scalar_tensor_tensor(
            out=srow[0:1, 0:512], in0=statmu[0:1, tt * 512:(tt + 1) * 512], scalar=1.0,
            in1=statmu[0:1, tt * 512:(tt + 1) * 512], op0=ALU.mult, op1=ALU.mult),
            waits=[("ac", a1)], inc="dv")
        d2 = E("vector", lambda e, tt=tt: e.scalar_tensor_tensor(
            out=srow[0:1, 512:1024], in0=stp_q[0:1, 0:512].partition_broadcast(1)
            if False else stp_q[0:1, 0:512], scalar=1.0 / FF, in1=srow[0:1, 0:512],
            op0=ALU.mult, op1=ALU.subtract), waits=[("dv", d1), ("pe", pv)], inc="dv")
        a2 = E("scalar", lambda e: e.activation(out=srow[0:1, 1024:1536], in_=srow[0:1, 512:1024],
                                                func=AF.Ln, bias=epscol[0:1, 0:1]), waits=[("dv", d2)], inc="ac")
        a3 = E("scalar", lambda e, tt=tt: e.activation(
            out=statsv[0:1, tt * 512:(tt + 1) * 512], in_=srow[0:1, 1024:1536],
            func=AF.Exp, scale=0.5), waits=[("ac", a2)], inc="ac")
        a4 = E("scalar", lambda e, tt=tt: e.activation(
            out=statrs[0:1, tt * 512:(tt + 1) * 512], in_=srow[0:1, 1024:1536],
            func=AF.Exp, scale=-0.5), waits=[("ac", a2)], inc="ac")
        m5["st_pe_gate"] = M("ac")
    dvb = E("vector", lambda e: e.tensor_copy(out=statmu_b[:], in_=statmu[:]),
            waits=[("ac", M("ac"))], inc="dv")
    dvb = E("vector", lambda e: e.tensor_copy(out=statsv_b[:], in_=statsv[:]),
            waits=[("ac", M("ac"))], inc="dv")
    p5_done = M("dv")
    if debug:
        E("sync", lambda e: e.dma_start(out=dbg["h"], in_=h_dram),
          waits=[("q_h0", M("q_h0")), ("q_h1", M("q_h1")), ("dv", p5_done)],
          inc="q_dbg", incval=16)
        E("sync", lambda e: e.dma_start(out=dbg["stat"][0:1, :], in_=statmu[:]),
          waits=[("dv", p5_done), ("ac", M("ac"))], inc="q_dbg", incval=16)
        E("sync", lambda e: e.dma_start(out=dbg["stat"][1:2, :], in_=statsv[:]), inc="q_dbg", incval=16)
        E("sync", lambda e: e.dma_start(out=dbg["stat"][2:3, :], in_=statrs[:]), inc="q_dbg", incval=16)

    P.fence()
    esC.close()
    esD = ExitStack()
    scope[0] = esD
    wpp = sb("wpp", [128, HCN * D], BF16)
    borow = sb("borow", [1, D], BF16)
    wo_sb = sb("wo_sb", [128, 8 * D], BF16)
    hcol = [sb(f"hcol{i}", [128, HCN * 128], BF16) for i in range(2)]
    gbuf = sb("gbuf", [128, D], BF16)
    rstdc = sb("rstdc", [128, 1])
    xf6 = [sb(f"xf6{i}", [128, D]) for i in range(2)]
    out6 = sb("out6", [128, D])
    E("gpsimd", lambda e: e.dma_start(out=borow[:], in_=bo[None, :]), inc="q_w6", incval=16)
    qw = E("gpsimd", lambda e: e.dma_start(out=wpp[:].rearrange("p (c n) -> p c n", c=HCN),
                                           in_=wp_dram.rearrange("c p n -> p c n")),
           waits=[("q_wp0", M("q_wp0")), ("q_wp1", M("q_wp1"))], inc="q_w6", incval=16)
    qw2 = E("gpsimd", lambda e: e.dma_start(out=wo_sb[:].rearrange("p (c n) -> p c n", c=8),
                                            in_=Wo_p.rearrange("(c p) n -> p c n", p=128)),
            inc="q_w6", incval=16)
    pjp = [pb[0], pb[1]]
    wop = [pb[2], pb[3]]
    rsp = pb[4]
    m6 = {}
    for tc in range(NT6):
        i = tc % 2
        qh = E("sync", lambda e, tc=tc, i=i: e.dma_start(
            out=hcol[i][:].rearrange("p (c t) -> p c t", c=HCN),
            in_=h_dram[:, :, tc * 128:(tc + 1) * 128].rearrange("c p t -> p c t")),
            waits=[("pe", m6.get(f"hcol_free{i}", 0))] +
                  ([("q_h0", M("q_h0")), ("q_h1", M("q_h1"))] if tc < 2 else []),
            inc=f"q_c{i}", incval=16)
        qx = E("sync", lambda e, tc=tc, i=i: e.dma_start(
            out=xf6[i][:], in_=x_own[tc * 128:(tc + 1) * 128, :]),
            waits=[("dv", m6.get(f"x_free{i}", 0))], inc=f"q_f{i}", incval=16)
        pr = E("tensor", lambda e, tc=tc: e.matmul(
            rsp[:, 0:1], statrs[0:1, tc * 128:(tc + 1) * 128], ones_f[0:1, 0:1],
            start=True, stop=True), waits=[("ac", m6.get("rst_free", 0)), ("dv", p5_done)], inc="pe")
        ar = E("scalar", lambda e: e.copy(out=rstdc[:], in_=rsp[:, 0:1]), waits=[("pe", pr)], inc="ac")
        m6["rst_free"] = ar
        for half in range(2):
            for hc in range(HCN):
                E("tensor", lambda e, hc=hc, half=half, tc=tc, i=i: e.matmul(
                    pjp[half][:, 0:512], hcol[i][:, hc * 128:(hc + 1) * 128],
                    wpp[:, hc * D + half * 512:hc * D + (half + 1) * 512],
                    start=(hc == 0), stop=False),
                    waits=([(f"q_c{tc % 2}", qh), ("q_w6", 48), ("ac", m6.get(f"pj_free{half}", 0))]
                           if hc == 0 else []), inc="pe")
            E("tensor", lambda e, half=half, tc=tc: e.matmul(
                pjp[half][:, 0:512], statmu_b[0:1, tc * 128:(tc + 1) * 128],
                c1row[0:1, half * 512:(half + 1) * 512], start=False, stop=False),
                waits=[("dv", p5_done)], inc="pe")
            E("tensor", lambda e, half=half, tc=tc: e.matmul(
                pjp[half][:, 0:512], statsv_b[0:1, tc * 128:(tc + 1) * 128],
                c0row[0:1, half * 512:(half + 1) * 512], start=False, stop=True), inc="pe")
            pv = M("pe")
            ag = E("scalar", lambda e, half=half: e.activation(
                out=gbuf[:, half * 512:(half + 1) * 512], in_=pjp[half][:, 0:512],
                func=AF.Silu, scale=rstdc[:]), waits=[("pe", pv), ("ac", ar)], inc="ac")
            m6[f"pj_free{half}"] = ag
            for cc in range(8):
                E("tensor", lambda e, cc=cc, half=half, tc=tc: e.matmul(
                    wop[half][:, 0:512], ctx_sb[:, cc * OWN + tc * 128:cc * OWN + (tc + 1) * 128],
                    wo_sb[:, cc * D + half * 512:cc * D + (half + 1) * 512],
                    start=(cc == 0), stop=False),
                    waits=([("q_w6", 48), ("dv", m6.get(f"wo_free{half}", 0))] if cc == 0 else []),
                    inc="pe")
            E("tensor", lambda e, half=half: e.matmul(
                wop[half][:, 0:512], ones_bf[0:1, 0:128], borow[0:1, half * 512:(half + 1) * 512],
                start=False, stop=True), inc="pe")
            pv2 = M("pe")
            d1 = E("vector", lambda e, half=half: e.tensor_mul(
                out6[:, half * 512:(half + 1) * 512], wop[half][:, 0:512],
                gbuf[:, half * 512:(half + 1) * 512]),
                waits=[("pe", pv2), ("ac", ag), ("q_y", m6.get("y_q", 0))], inc="dv")
            m6[f"wo_free{half}"] = d1
            d2 = E("vector", lambda e, half=half, i=i: e.tensor_add(
                out6[:, half * 512:(half + 1) * 512], out6[:, half * 512:(half + 1) * 512],
                xf6[i][:, half * 512:(half + 1) * 512]),
                waits=[(f"q_f{i}", qx), ("dv", d1)], inc="dv")
            m6[f"x_free{i}"] = d2
        m6[f"hcol_free{i}"] = M("pe")
        m6["rst_free"] = M("ac")
        yq = E("sync", lambda e, tc=tc: e.dma_start(out=y[tc * 128:(tc + 1) * 128, :], in_=out6[:]),
               waits=[("dv", M("dv"))], inc="q_y", incval=16)
        m6["y_q"] = yq
    P.fence()
    P.realize()
    esD.close()
    esP.close()
    meta = dict(SEQ=SEQ, QTW=QTW, NK=NK, OWN=OWN, dbg=sorted(dbg.keys()))
    return nc, meta


BF = ml_dtypes.bfloat16




def eo_perm():
    perm = np.zeros(D, dtype=np.int64)
    for hp in range(HP):
        for i in range(64):
            perm[hp * 128 + i] = 128 * hp + 2 * i
            perm[hp * 128 + 64 + i] = 128 * hp + 2 * i + 1
    return perm


def wo_row_perm():
    perm = np.zeros(D, dtype=np.int64)
    for hp in range(HP):
        for i in range(64):
            perm[hp * 128 + i] = 64 * hp + i
            perm[hp * 128 + 64 + i] = 512 + 64 * hp + i
    return perm


def rope_tables(SEQ, positions):
    i = np.arange(D // 2, dtype=np.float32)
    fc = np.power(np.float32(10000.0), -2.0 * i / np.float32(D)).astype(np.float32)
    pos = positions.astype(np.float32)
    T1 = np.zeros((HP, 128, len(pos)), dtype=np.float32)
    T2 = np.zeros((HP, 128, len(pos)), dtype=np.float32)
    for hp in range(HP):
        f = fc[64 * hp:64 * hp + 64]
        ang = pos[None, :] * f[:, None]
        c, s = np.cos(ang), np.sin(ang)
        T1[hp, 0:64] = c
        T1[hp, 64:128] = s
        T2[hp, 0:64] = s
        T2[hp, 64:128] = c
    return T1.astype(BF), T2.astype(BF)


def own_rows(SEQ, m):
    QTW = SEQ // 16
    rows = []
    for s in range(4):
        t = m + 4 * s
        rows.append(np.arange(t * QTW, (t + 1) * QTW))
    return np.concatenate(rows)


def prep_core(inp, c, SEQ):
    QTW = SEQ // 16
    NK = QTW // 128
    NAMB = 3 * NK
    NKS_T = SEQ // 128
    b, m = divmod(c, 4)
    x = np.asarray(inp["L0_attn_x"], dtype=np.float32)
    rows = own_rows(SEQ, m)
    perm = eo_perm()
    woperm = wo_row_perm()

    Wq = np.asarray(inp["L0_attn_Wq"], np.float32)
    Wk = np.asarray(inp["L0_attn_Wk"], np.float32)
    Wv = np.asarray(inp["L0_attn_Wv"], np.float32)
    Wo = np.asarray(inp["L0_attn_Wo"], np.float32)
    bq = np.asarray(inp["L0_attn_bq"], np.float32)
    bk = np.asarray(inp["L0_attn_bk"], np.float32)
    bv = np.asarray(inp["L0_attn_bv"], np.float32)
    bo = np.asarray(inp["L0_attn_bo"], np.float32)

    Wv_e = np.zeros((D, VROW), np.float32)
    bv_e = np.zeros(VROW, np.float32)
    for hp in range(HP):
        blk = hp * VW
        Wv_e[:, blk:blk + 64] = Wv[:, 64 * hp:64 * hp + 64]
        Wv_e[:, blk + 65:blk + 129] = Wv[:, 512 + 64 * hp:512 + 64 * hp + 64]
        bv_e[blk:blk + 64] = bv[64 * hp:64 * hp + 64]
        bv_e[blk + 64] = 1.0
        bv_e[blk + 65:blk + 129] = bv[512 + 64 * hp:512 + 64 * hp + 64]
        bv_e[blk + 129] = 1.0

    T1k, T2k = rope_tables(SEQ, np.arange(SEQ))
    T1q, T2q = rope_tables(SEQ, rows)

    imat = np.vstack([np.eye(64), -np.eye(64)]).astype(BF)
    ind = np.zeros((HP, 128, 16), np.float32)
    for hp in range(HP):
        ind[hp, 0:64, hp] = 1.0
        ind[hp, 64:128, 8 + hp] = 1.0

    p = np.arange(128, dtype=np.int64)
    kidx = np.zeros((128, 32), np.uint32)
    vidx = np.zeros((128, 32), np.uint32)
    for s in range(4):
        t = m + 4 * s
        for hp in range(HP):
            kidx[:, s * HP + hp] = hp * 128 * SEQ + p * SEQ + t * QTW
            vidx[:, s * HP + hp] = hp * 128 * (NKS_T * VW) + p * (NKS_T * VW) + t * NK * VW
    pflag = np.zeros((1, 4 * NAMB), np.float32)
    for s in range(4):
        for a in range(NAMB):
            pflag[0, s * NAMB + a] = 0.0 if a < m * NK else -60000.0

    return {
        "x_kv": np.ascontiguousarray(x[b]),
        "x_own": np.ascontiguousarray(x[b][rows]),
        "Wq_p": np.ascontiguousarray(Wq[:, perm]),
        "Wk_p": np.ascontiguousarray(Wk[:, perm]),
        "Wv_e": Wv_e,
        "bq_p": np.ascontiguousarray(bq[perm]).reshape(8, 128),
        "bk_p": np.ascontiguousarray(bk[perm]).reshape(8, 128),
        "bv_e": bv_e,
        "Wo_p": np.ascontiguousarray(Wo[woperm, :]),
        "bo": bo,
        "Wfc": np.asarray(inp["L0_ffn_W_fc"], np.float32),
        "Wpj": np.asarray(inp["L0_ffn_W_proj"], np.float32),
        "lnw": np.asarray(inp["L0_ffn_ln_w"], np.float32).reshape(32, 128),
        "lnb": np.asarray(inp["L0_ffn_ln_b"], np.float32).reshape(32, 128),
        "T1k": T1k, "T2k": T2k, "T1q": T1q, "T2q": T2q,
        "imat": imat,
        "ind": ind.astype(BF),
        "kidx": kidx, "vidx": vidx,
        "pflag": pflag,
    }




_CACHE = {}


def _get_nc():
    if "nc" not in _CACHE:
        _CACHE["nc"] = build(SEQ=8192, debug=False)[0]
    return _CACHE["nc"]


def kernel(**inputs):
    from concourse.bass_utils import run_bass_kernel_spmd
    SEQ = 8192
    nc = _get_nc()
    in_maps = [prep_core(inputs, c, SEQ) for c in range(8)]
    res = run_bass_kernel_spmd(nc, in_maps, core_ids=list(range(8)))
    out = np.zeros((2, SEQ, D), np.float32)
    for c in range(8):
        b, m = divmod(c, 4)
        out[b][own_rows(SEQ, m)] = res.results[c]["y"]
    return out


# version 04
# speedup vs baseline: 1.0198x; 1.0198x over previous
import numpy as np
from collections import defaultdict
from contextlib import ExitStack

import ml_dtypes
import concourse.bass as bass
import concourse.mybir as mybir
from concourse.bass import IndirectOffsetOnAxis

F32 = mybir.dt.float32
BF16 = mybir.dt.bfloat16
U32 = mybir.dt.uint32
AF = mybir.ActivationFunctionType
ALU = mybir.AluOpType
AXX = mybir.AxisListType.X

D = 1024
H = 16
HP = 8
FF = 4096
VW = 132
VROW = HP * VW
LN_EPS = 1e-5
ABS_EPS = 1e-6
NEG = -60000.0


class Prog:
    def __init__(self, nc):
        self.nc = nc
        self.ops = {e: [] for e in ("tensor", "vector", "scalar", "gpsimd", "sync")}
        self.sems = {}
        self.cnt = defaultdict(int)

    def sem(self, name):
        if name not in self.sems:
            self.sems[name] = self.nc.alloc_semaphore(name)
        return self.sems[name]

    def emit(self, eng, fn, waits=(), inc=None, incval=1):
        if inc is not None:
            self.sem(inc)
            self.cnt[inc] += incval
        for (s, v) in waits:
            self.sem(s)
        self.ops[eng].append((fn, tuple(waits), inc, incval))
        return self.cnt[inc] if inc is not None else None

    def mark(self, name):
        return self.cnt[name]

    def fence(self, engines=("tensor", "vector", "scalar", "gpsimd", "sync")):
        waits = tuple((s, self.cnt[s]) for s in self.sems)
        for e in engines:
            self.ops[e].append((None, waits, None, 1))

    def realize(self):
        with self.nc.Block() as block:
            for eng_name in ("sync", "tensor", "scalar", "vector", "gpsimd"):
                ops = self.ops[eng_name]
                if not ops:
                    continue

                def body(e, ops=ops):
                    for fn, waits, inc, incval in ops:
                        for (s, v) in waits:
                            if v > 0:
                                e.wait_ge(self.sems[s], v)
                        if fn is None:
                            continue
                        ins = fn(e)
                        if inc is not None:
                            ins.then_inc(self.sems[inc], incval)
                getattr(block, eng_name)(body)


def build(SEQ=8192, debug=False):
    QTW = SEQ // 16
    NK = QTW // 128
    assert QTW % 128 == 0 and QTW <= 512
    OWN = 4 * QTW
    NT1 = SEQ // 512
    NAMB = 3 * NK
    NKS_T = SEQ // 128
    NT6 = OWN // 128
    HCN = FF // 128

    nc = bass.Bass("TRN2", target_bir_lowering=False)
    P = Prog(nc)
    E, M = P.emit, P.mark
    for s in ("pe", "dv", "ac", "gp"):
        P.sem(s)

    def inp(name, shape, dt=F32):
        return nc.dram_tensor(name, shape, dt, kind="ExternalInput").ap()

    x_kv = inp("x_kv", [SEQ, D])
    x_own = inp("x_own", [OWN, D])
    Wq_p = inp("Wq_p", [D, D])
    Wk_p = inp("Wk_p", [D, D])
    Wv_e = inp("Wv_e", [D, VROW])
    bq_p = inp("bq_p", [8, 128])
    bk_p = inp("bk_p", [8, 128])
    bv_e = inp("bv_e", [VROW])
    Wo_p = inp("Wo_p", [D, D])
    bo = inp("bo", [D])
    Wfc = inp("Wfc", [FF, D])
    Wpj = inp("Wpj", [D, FF])
    lnw = inp("lnw", [HCN, 128])
    lnb = inp("lnb", [HCN, 128])
    T1k = inp("T1k", [SEQ // 512, 128, HP * 512], BF16)
    T2k = inp("T2k", [SEQ // 512, 128, HP * 512], BF16)
    T1q = inp("T1q", [OWN // 512, 128, HP * 512], BF16)
    T2q = inp("T2q", [OWN // 512, 128, HP * 512], BF16)
    imat_i = inp("imat", [128, 64], BF16)
    ind_i = inp("ind", [HP, 128, 16], BF16)
    kidx_i = inp("kidx", [128, 32], U32)
    vidx_i = inp("vidx", [128, 32], U32)
    pflag_i = inp("pflag", [1, 4 * NAMB])

    y = nc.dram_tensor("y", [OWN, D], F32, kind="ExternalOutput").ap()

    K_dram = nc.dram_tensor("K_dram", [HP, 128, SEQ], BF16).ap()
    V_dram = nc.dram_tensor("V_dram", [HP, 128, NKS_T * VW], BF16).ap()
    h_dram = nc.dram_tensor("h_dram", [HCN, 128, OWN], BF16).ap()
    wp_dram = nc.dram_tensor("wp_dram", [HCN, 128, D], BF16).ap()
    statb_dram = nc.dram_tensor("statb_dram", [2, OWN], BF16).ap()
    statrs_dram = nc.dram_tensor("statrs_dram", [1, OWN], F32).ap()

    dbg = {}
    if debug:
        def dout(name, shape, dt=F32):
            dbg[name] = nc.dram_tensor("dbg_" + name, shape, dt, kind="ExternalOutput").ap()
        dout("K", [HP, 128, SEQ], BF16)
        dout("V", [HP, 128, NKS_T * VW], BF16)
        dout("Q", [128, HP * OWN], BF16)
        dout("xn", [8, 128, OWN], BF16)
        dout("ctx", [128, HP * OWN], BF16)
        dout("b16", [128, 16])
        dout("h", [HCN, 128, OWN], BF16)
        dout("wp", [HCN, 128, D], BF16)
        dout("gam", [128, 2])
        dout("c1", [1, D])
        dout("c0", [1, D])
        dout("stat", [3, OWN])

    esP = ExitStack()
    scope = [esP]
    sb = lambda name, shape, dt=F32: scope[0].enter_context(nc.sbuf_tensor(name, shape, dt))
    ps = lambda name, shape, dt=F32: esP.enter_context(nc.psum_tensor(name, shape, dt))

    ones_bf = sb("ones_bf", [128, 128], BF16)
    ones_f = sb("ones_f", [128, 128])
    identf = sb("identf", [128, 128])
    identb = sb("identb", [128, 128], BF16)
    ident64 = sb("ident64", [64, 64], BF16)
    epscol = sb("epscol", [128, 1])
    imat = sb("imat_s", [128, 64], BF16)
    ind16 = sb("ind16", [128, HP * 16], BF16)
    bqc = sb("bqc", [128, 8])
    bkc = sb("bkc", [128, 8])
    bvrow = sb("bvrow", [1, VROW], BF16)
    lnwc = sb("lnwc", [128, HCN])
    lnbc = sb("lnbc", [128, HCN])
    lnw_bf = sb("lnw_bf", [128, HCN], BF16)
    lnb_bf = sb("lnb_bf", [128, HCN], BF16)
    kidx = sb("kidx_s", [128, 32], U32)
    vidx = sb("vidx_s", [128, 32], U32)
    pflag = sb("pflag_s", [1, 4 * NAMB])
    bias16 = sb("bias16", [128, 8])
    biasamb = sb("biasamb", [128, 4 * NAMB * 8])
    uacc = sb("uacc", [16, 4])
    gam = sb("gam_s", [128, 2])
    c1row = sb("c1row", [1, D], BF16)
    c0row = sb("c0row", [1, D], BF16)
    Q_sb = sb("Q_sb", [128, HP * OWN], BF16)
    ctx_sb = sb("ctx_sb", [128, HP * OWN], BF16)
    xn_sb = sb("xn_sb", [128, 8 * OWN], BF16)
    urow_sc = sb("urow_sc", [16, 16])
    urow_r = sb("urow_r", [1, 16])
    pfcol = sb("pfcol", [128, 4 * NAMB])
    _rope_tab_gate = [("pe", 0)]

    E("gpsimd", lambda e: e.memset(ones_bf[:], 1.0))
    E("gpsimd", lambda e: e.memset(ones_f[:], 1.0))
    E("gpsimd", lambda e: e.memset(uacc[:], 0.0))
    E("gpsimd", lambda e: e.memset(urow_sc[:], 0.0))
    E("gpsimd", lambda e: e.memset(epscol[:], LN_EPS))
    for t, n in ((identf, 128), (identb, 128), (ident64, 64)):
        E("gpsimd", lambda e, t=t: e.memset(t[:], 0.0), inc="gp")
        E("gpsimd", lambda e, t=t, n=n: e.affine_select(
            out=t[:], in_=t[:], compare_op=ALU.not_equal, fill=1.0, base=0,
            pattern=[[-1, n]], channel_multiplier=1), waits=[("gp", M("gp"))], inc="gp")
    gp_consts = M("gp")

    E("sync", lambda e: e.dma_start(out=imat[:], in_=imat_i), inc="q_once", incval=16)
    E("sync", lambda e: e.dma_start(out=pflag[:], in_=pflag_i), inc="q_once", incval=16)
    E("sync", lambda e: e.dma_start(out=ind16[:].rearrange("p (h c) -> p h c", h=HP), in_=ind_i.rearrange("h p c -> p h c")), inc="q_once", incval=16)
    E("sync", lambda e: e.dma_start(out=kidx[:], in_=kidx_i), inc="q_once", incval=16)
    E("sync", lambda e: e.dma_start(out=vidx[:], in_=vidx_i), inc="q_once", incval=16)
    E("gpsimd", lambda e: e.dma_start(out=bvrow[:], in_=bv_e[None, :]), inc="q_onceg", incval=16)

    pd = [esP.enter_context(nc.psum_tensor(f"pd{i}", [128, 1024], F32)) for i in range(4)]
    pb = [pd[i // 2][:, (i % 2) * 512:(i % 2 + 1) * 512] for i in range(8)]
    btmp = sb("btmp", [HCN, 128])
    trb = pb[0]
    for (src, dst, rows) in ((bq_p, bqc, 8), (bk_p, bkc, 8), (lnw, lnwc, HCN), (lnb, lnbc, HCN)):
        qv = E("sync", lambda e, s=src, r=rows: e.dma_start(out=btmp[0:r, :], in_=s), waits=[("pe", M("pe"))], inc="q_btmp", incval=16)
        pv = E("tensor", lambda e, r=rows: e.transpose(trb[:, 0:r], btmp[0:r, :], identf[0:r, 0:r]),
               waits=[("q_btmp", qv), ("gp", gp_consts), ("ac", M("ac"))], inc="pe")
        E("scalar", lambda e, d=dst, r=rows: e.copy(out=d[:, 0:r], in_=trb[:, 0:r]),
          waits=[("pe", pv)], inc="ac")
    v = E("vector", lambda e: e.tensor_copy(out=lnw_bf[:], in_=lnwc[:]), waits=[("ac", M("ac"))], inc="dv")
    v = E("vector", lambda e: e.tensor_copy(out=lnb_bf[:], in_=lnbc[:]), waits=[("ac", M("ac"))], inc="dv")
    dv_ln = M("dv")

    esA = ExitStack()
    scope[0] = esA
    Wk_sb = sb("Wk_sb", [128, 8 * D], BF16)
    Wv_sb = sb("Wv_sb", [128, 8 * VROW], BF16)
    for (dsb, dsrc) in ((Wk_sb, Wk_p), (Wv_sb, Wv_e)):
        E("gpsimd", lambda e, a=dsb, b=dsrc: e.dma_start(
            out=a[:].rearrange("p (c n) -> p c n", c=8), in_=b.rearrange("(c p) n -> p c n", p=128)),
            inc="q_onceg", incval=16)
    qg_w_cell = [M("q_onceg")]

    esA0 = ExitStack()
    scope[0] = esA0
    gtile = [sb(f"gtile{i}", [128, D]) for i in range(2)]
    gabs = [sb(f"gabs{i}", [128, D], BF16) for i in range(2)]
    ginv = sb("ginv", [1, 4])
    gps_h = [pb[1], pb[2]]
    gbc = pb[3]

    def gamma_pass(view, nt, wi):
        pe_pair = [M("pe")]
        for t in range(nt):
            i = t % 2
            qv = E("sync", lambda e, t=t, i=i, view=view: e.dma_start(
                out=gtile[i][:], in_=view(t)), waits=[("pe", pe_pair[0])], inc=f"q_g{i}", incval=16)
            av = E("scalar", lambda e, i=i: e.activation(out=gabs[i][:], in_=gtile[i][:], func=AF.Abs),
                   waits=[(f"q_g{i}", qv)], inc="ac")
            for half in range(2):
                E("tensor", lambda e, i=i, t=t, half=half, nt=nt: e.matmul(
                    gps_h[half][0:1, 0:512],
                    ones_bf[:, 0:1], gabs[i][:, half * 512:(half + 1) * 512],
                    start=(t == 0), stop=(t == nt - 1)),
                    waits=[("ac", av)] + ([("dv", M("dv"))] if t == 0 else []), inc="pe")
            if t >= 1:
                pe_pair[0] = M("pe") - 2
        d = E("vector", lambda e, wi=wi: e.reduce_sum(
            ginv[0:1, wi:wi + 1], gps_h[0][0:1, 0:512], axis=AXX), waits=[("pe", M("pe"))], inc="dv")
        d = E("vector", lambda e, wi=wi: e.reduce_sum(
            ginv[0:1, wi + 2:wi + 3], gps_h[1][0:1, 0:512], axis=AXX), waits=[("pe", M("pe"))], inc="dv")
        return E("vector", lambda e, wi=wi: e.tensor_add(
            ginv[0:1, wi:wi + 1], ginv[0:1, wi:wi + 1], ginv[0:1, wi + 2:wi + 3]),
            waits=[("dv", d)], inc="dv")

    fcv = lambda t: Wfc[t * 128:(t + 1) * 128, :]
    pjv = lambda t: Wpj.rearrange("(g p) (t n) -> g t p n", p=128, n=D)[t // 4][t % 4]
    gamma_pass(fcv, FF // 128, 0)
    gamma_pass(pjv, (D // 128) * (FF // D), 1)
    dvv = E("vector", lambda e: e.tensor_scalar(out=ginv[0:1, 0:2], in0=ginv[0:1, 0:2],
                                                scalar1=1.0 / (D * FF),
                                                scalar2=ABS_EPS, op0=ALU.mult, op1=ALU.add),
            waits=[("dv", M("dv"))], inc="dv")
    dvv = E("vector", lambda e: e.reciprocal(out=ginv[0:1, 0:2], in_=ginv[0:1, 0:2]),
            waits=[("dv", dvv)], inc="dv")
    pv = E("tensor", lambda e: e.matmul(gbc[:, 0:2], ones_f[0:1, :], ginv[0:1, 0:2],
                                        start=True, stop=True), waits=[("dv", dvv)], inc="pe")
    ac_gam = E("scalar", lambda e: e.copy(out=gam[:], in_=gbc[:, 0:2]), waits=[("pe", pv)], inc="ac")
    if debug:
        E("sync", lambda e: e.dma_start(out=dbg["gam"], in_=gam[:]), waits=[("ac", ac_gam)],
          inc="q_dbg", incval=16)
    esA0.close()
    esA0 = ExitStack()
    scope[0] = esA0

    ac_tr_hold = [0]
    wp_hold = {0: 0, 1: 0}
    v4_hold = {0: 0, 1: 0}
    pj_f = [sb("pj_f0", [128, FF])] * 2
    pj_t = [sb("pj_t0", [128, FF], BF16)] * 2
    pj_s = sb("pj_s", [128, FF // 2], BF16)
    wpt = [sb(f"wpt{i}", [128, 128], BF16) for i in range(2)]
    wps = [sb(f"wps{i}", [128, 128], BF16) for i in range(2)]
    trn = pb[4].bitcast(BF16)
    c10 = [pb[5], pb[6]]
    c00 = [pb[7], pb[3]]
    qg_hold = [0]
    for dchunk in range(D // 128):
        i = dchunk % 2
        pe_gate = M("pe")
        qv = E("sync", lambda e, d=dchunk, i=i: e.dma_start(
            out=pj_f[i][:], in_=Wpj[d * 128:(d + 1) * 128, :]),
            waits=[("dv", M("dv"))], inc=f"q_pj{i}", incval=16)
        for hf in range(2):
            c0_, c1_ = hf * (FF // 2), (hf + 1) * (FF // 2)
            v1 = E("vector", lambda e, i=i, c0_=c0_, c1_=c1_: e.tensor_scalar(
                out=pj_s[:], in0=pj_f[i][:, c0_:c1_], scalar1=gam[:, 1:2], scalar2=0.5,
                op0=ALU.mult, op1=ALU.is_gt),
                waits=[(f"q_pj{i}", qv), ("ac", ac_gam), ("dv", M("dv"))], inc="dv")
            v2 = E("vector", lambda e, i=i, c0_=c0_, c1_=c1_: e.tensor_scalar(
                out=pj_t[i][:, c0_:c1_], in0=pj_f[i][:, c0_:c1_], scalar1=gam[:, 1:2], scalar2=-0.5,
                op0=ALU.mult, op1=ALU.is_lt),
                waits=[(f"q_pj{i}", qv), ("pe", pe_gate)], inc="dv")
            v3 = E("vector", lambda e, i=i, c0_=c0_, c1_=c1_: e.tensor_sub(
                pj_t[i][:, c0_:c1_], pj_s[:], pj_t[i][:, c0_:c1_]), waits=[("dv", v2)], inc="dv")
        for hc in range(HCN):
            j = hc % 2
            pv = E("tensor", lambda e, i=i, hc=hc, j=j: e.transpose(
                trn[:, j * 128:(j + 1) * 128], pj_t[i][:, hc * 128:(hc + 1) * 128], identb[:]),
                waits=[("dv", v3), ("ac", ac_tr_hold[0]), ("dv", dv_ln)], inc="pe")
            av = E("scalar", lambda e, j=j: e.copy(out=wpt[j][:], in_=trn[:, j * 128:(j + 1) * 128]),
                   waits=[("pe", pv), ("dv", v4_hold[j])], inc="ac")
            if hc >= 1:
                ac_tr_hold[0] = av - 1
            E("tensor", lambda e, j=j, hc=hc, d=dchunk: e.matmul(
                c10[d // 4][0:1, (d % 4) * 128:(d % 4 + 1) * 128], lnw_bf[:, hc:hc + 1], wpt[j][:],
                start=(hc == 0), stop=(hc == HCN - 1)), waits=[("ac", av)], inc="pe")
            E("tensor", lambda e, j=j, hc=hc, d=dchunk: e.matmul(
                c00[d // 4][0:1, (d % 4) * 128:(d % 4 + 1) * 128], lnb_bf[:, hc:hc + 1], wpt[j][:],
                start=(hc == 0), stop=(hc == HCN - 1)), inc="pe")
            v4 = E("vector", lambda e, j=j, hc=hc: e.tensor_scalar_mul(
                wps[j][:], wpt[j][:], lnwc[:, hc:hc + 1]),
                waits=[("ac", av), (f"q_wp{j}", wp_hold[j])], inc="dv")
            v4_hold[j] = v4
            qq = E("gpsimd", lambda e, j=j, hc=hc, d=dchunk: e.dma_start(
                out=wp_dram[hc, :, d * 128:(d + 1) * 128], in_=wps[j][:]),
                waits=[("dv", v4)], inc=f"q_wp{j}", incval=16)
            wp_hold[j] = qq
    pe_c = M("pe")
    av = E("scalar", lambda e: e.copy(out=c1row[0:1, 0:512], in_=c10[0][0:1, 0:512]),
           waits=[("pe", pe_c)], inc="ac")
    av = E("scalar", lambda e: e.copy(out=c1row[0:1, 512:1024], in_=c10[1][0:1, 0:512]), inc="ac")
    av = E("scalar", lambda e: e.copy(out=c0row[0:1, 0:512], in_=c00[0][0:1, 0:512]), inc="ac")
    av = E("scalar", lambda e: e.copy(out=c0row[0:1, 512:1024], in_=c00[1][0:1, 0:512]), inc="ac")
    ac_c10 = M("ac")
    if debug:
        cc1 = sb("cc1", [1, D])
        cc0 = sb("cc0", [1, D])
        E("vector", lambda e: e.tensor_copy(out=cc1[:], in_=c1row[:]), waits=[("ac", ac_c10)], inc="dv")
        E("vector", lambda e: e.tensor_copy(out=cc0[:], in_=c0row[:]), waits=[("ac", ac_c10)], inc="dv")
        E("sync", lambda e: e.dma_start(out=dbg["c1"], in_=cc1[:]), waits=[("dv", M("dv"))], inc="q_dbg", incval=16)
        E("sync", lambda e: e.dma_start(out=dbg["c0"], in_=cc0[:]), waits=[("dv", M("dv"))], inc="q_dbg", incval=16)
        E("sync", lambda e: e.dma_start(out=dbg["wp"], in_=wp_dram), waits=[("q_wp0", M("q_wp0")), ("q_wp1", M("q_wp1"))], inc="q_dbg", incval=16)

    esA0.close()
    scope[0] = esA
    xtile = [sb(f"xtile{i}", [128, D]) for i in range(2)]
    xnb = sb("xnb", [128, D], BF16)
    lncol = sb("lncol", [128, 8])
    x2s = sb("x2s", [128, D], BF16)
    xnT = [sb(f"xnT{i}", [128, 8 * 512], BF16) for i in range(2)]
    t1s = [sb("t1s0", [128, HP * 512], BF16)] * 2
    t2s = [sb("t2s0", [128, HP * 512], BF16)] * 2
    keo = sb("keo", [128, 512], BF16)
    m1 = sb("m1", [128, 512], BF16)
    m2 = sb("m2", [128, 512], BF16)
    kst = [sb(f"kst{i}", [128, 512], BF16) for i in range(2)]
    k2t = sb("k2t", [128, 512], BF16)
    vraw = [sb("vraw0", [128, 4 * VROW], BF16)] * 2
    tr_ps = pb[0].bitcast(BF16)
    eo_ps2 = [pb[1], pb[7]]
    ro_ps = pb[2]
    v_ps = [pb[3], pb[4], pb[5]]
    u_ps = pb[6]

    def ln_and_transpose(x_src, tt, xnT_buf, marks, tw, key):
        for sub in range(tw // 128):
            yield
            i = sub % 2
            r0 = tt * tw + sub * 128
            qv = E("sync", lambda e, i=i, r0=r0, x_src=x_src: e.dma_start(
                out=xtile[i][:], in_=x_src[r0:r0 + 128, :]),
                waits=[("ac", marks.get(f"x_reuse{i}", 0))], inc=f"q_x{i}", incval=16)
            av0 = E("scalar", lambda e, i=i: e.activation(
                out=x2s[:], in_=xtile[i][:], func=AF.Identity, accum_out=lncol[:, 0:1]),
                waits=[(f"q_x{i}", qv), ("dv", marks.get("x2_reuse", 0))], inc="ac")
            dv0 = E("vector", lambda e, i=i: e.scalar_tensor_tensor(
                out=x2s[:], in0=xtile[i][:], scalar=1.0, in1=xtile[i][:],
                op0=ALU.mult, op1=ALU.mult, accum_out=lncol[:, 1:2]),
                waits=[("ac", av0)], inc="dv")
            marks["x2_reuse"] = dv0
            dv1 = E("vector", lambda e: e.tensor_scalar_mul(lncol[:, 2:3], lncol[:, 0:1], 1.0 / D),
                    waits=[("dv", dv0)], inc="dv")
            dv2 = E("vector", lambda e: e.scalar_tensor_tensor(
                out=lncol[:, 3:4], in0=lncol[:, 2:3], scalar=-1.0, in1=lncol[:, 2:3],
                op0=ALU.mult, op1=ALU.mult), waits=[("dv", dv1)], inc="dv")
            dv3 = E("vector", lambda e: e.scalar_tensor_tensor(
                out=lncol[:, 3:4], in0=lncol[:, 1:2], scalar=1.0 / D, in1=lncol[:, 3:4],
                op0=ALU.mult, op1=ALU.add), waits=[("dv", dv2)], inc="dv")
            av1 = E("scalar", lambda e: e.activation(out=lncol[:, 4:5], in_=lncol[:, 3:4],
                                                     func=AF.Ln, bias=epscol[:]), waits=[("dv", dv3)], inc="ac")
            av2 = E("scalar", lambda e: e.activation(out=lncol[:, 4:5], in_=lncol[:, 4:5],
                                                     func=AF.Exp, scale=-0.5), waits=[("ac", av1)], inc="ac")
            dv4 = E("vector", lambda e: e.scalar_tensor_tensor(
                out=lncol[:, 5:6], in0=lncol[:, 2:3], scalar=-1.0, in1=lncol[:, 4:5],
                op0=ALU.mult, op1=ALU.mult), waits=[("ac", av2)], inc="dv")
            av3 = E("scalar", lambda e, i=i: e.activation(
                out=xnb[:], in_=xtile[i][:], func=AF.Identity,
                scale=lncol[:, 4:5], bias=lncol[:, 5:6]), waits=[("dv", dv4)], inc="ac")
            marks[f"x_reuse{i}"] = av3
            for fc in range(8):
                pv = E("tensor", lambda e, fc=fc: e.transpose(
                    tr_ps[:, fc * 128:(fc + 1) * 128], xnb[:, fc * 128:(fc + 1) * 128], identb[:]),
                    waits=[("ac", av3)] + ([("ac", marks.get("tr_evac", 0))] if fc == 0 else []),
                    inc="pe")
            av4 = E("scalar", lambda e, sub=sub, xnT_buf=xnT_buf, tw=tw: e.copy(
                out=xnT_buf.rearrange("p (f t) -> p f t", f=8)[:, :, sub * 128:(sub + 1) * 128],
                in_=tr_ps[:].rearrange("p (f t) -> p f t", f=8)),
                waits=[("pe", pv), ("pe", marks.get(key + "_pefree", 0)),
                       ("dv", marks.get(key + "_dvfree", 0))], inc="ac")
            marks["tr_evac"] = av4
            marks[key] = av4

    def rope_project(xnT_buf, W_sb, bias_c, T1buf, T2buf, dst_kind, tt, tw, marks, dbg_norm=None):
        def emit_proj(hp):
            ei = hp % 2
            for fc in range(8):
                E("tensor", lambda e, hp=hp, fc=fc, tw=tw, ei=ei: e.matmul(
                    eo_ps2[ei][:, 0:tw], W_sb[:, fc * D + hp * 128: fc * D + (hp + 1) * 128],
                    xnT_buf[:, fc * tw: fc * tw + tw], start=(fc == 0), stop=(fc == 7)),
                    waits=([("ac", marks.get(f"keo_reuse{ei}", 0)), ("ac", marks.get("tr_evac", 0))]
                           + ([("q_onceg", qg_w_cell[0])] if hp == 0 else []) if fc == 0 else []),
                    inc="pe")
            return M("pe")

        pmark = {0: emit_proj(0)}
        filler = marks.pop("filler", None)
        for hp in range(HP):
            ei = hp % 2
            if hp + 1 < HP:
                pmark[hp + 1] = emit_proj(hp + 1)
            if filler is not None:
                filler(hp)
            av = E("scalar", lambda e, hp=hp, tw=tw, bias_c=bias_c, ei=ei: e.activation(
                out=keo[:, 0:tw], in_=eo_ps2[ei][:, 0:tw], func=AF.Identity,
                bias=bias_c[:, hp:hp + 1]),
                waits=[("pe", pmark[hp]), ("dv", marks.get("keo_dv", 0))], inc="ac")
            marks[f"keo_reuse{ei}"] = av
            d1 = E("vector", lambda e, hp=hp, tw=tw, T1buf=T1buf: e.tensor_mul(
                m1[:, 0:tw], keo[:, 0:tw], T1buf[:, hp * tw: hp * tw + tw]),
                waits=[("ac", av), ("pe", marks.get("m_reuse", 0)),
                       _rope_tab_gate[0]], inc="dv")
            d2 = E("vector", lambda e, hp=hp, tw=tw, T2buf=T2buf: e.tensor_mul(
                m2[:, 0:tw], keo[:, 0:tw], T2buf[:, hp * tw: hp * tw + tw]), inc="dv")
            marks["keo_dv"] = d2
            pv1 = E("tensor", lambda e, tw=tw: e.matmul(
                ro_ps[0:64, 0:tw], imat[:], m1[:, 0:tw], start=True, stop=True),
                waits=[("dv", d2), ("ac", marks.get("ro_reuse", 0))], inc="pe")
            pv2 = E("tensor", lambda e, tw=tw: e.matmul(
                ro_ps[64:128, 0:tw], imat[:], m2[:, 0:tw], start=True, stop=True,
                tile_position=(0, 64)), inc="pe")
            marks["m_reuse"] = pv2
            if dst_kind[0] == 'K':
                i = hp % 2
                av2 = E("scalar", lambda e, i=i, tw=tw: e.copy(out=kst[i][:, 0:tw], in_=ro_ps[:, 0:tw]),
                        waits=[("pe", pv2), (f"q_ko{i}", marks.get(f"kst_reuse{i}", 0))], inc="ac")
                marks["ro_reuse"] = av2
                qv = E("sync", lambda e, i=i, hp=hp, tt=tt, tw=tw: e.dma_start(
                    out=K_dram[hp, :, tt * tw:(tt + 1) * tw], in_=kst[i][:, 0:tw]),
                    waits=[("ac", av2)], inc=f"q_ko{i}", incval=16)
                marks[f"kst_reuse{i}"] = qv
                ksrc = kst[i]
            else:
                av2 = E("scalar", lambda e, hp=hp, tt=tt, tw=tw: e.copy(
                    out=Q_sb[:, hp * OWN + tt * tw: hp * OWN + (tt + 1) * tw], in_=ro_ps[:, 0:tw]),
                    waits=[("pe", pv2)], inc="ac")
                marks["ro_reuse"] = av2
                ksrc = None
            src = ksrc[:, 0:tw] if ksrc is not None else Q_sb[:, hp * OWN + tt * tw: hp * OWN + (tt + 1) * tw]
            d3 = E("vector", lambda e, src=src, tw=tw: e.tensor_mul(k2t[:, 0:tw], src, src),
                   waits=[("ac", av2), ("pe", marks.get("k2_reuse", 0))], inc="dv")
            pu = E("tensor", lambda e, hp=hp, tw=tw: e.matmul(
                u_ps[0:16, 0:tw], ind16[:, hp * 16:(hp + 1) * 16], k2t[:, 0:tw],
                start=(hp == 0), stop=(hp == HP - 1)),
                waits=[("dv", d3)] + ([("dv", marks.get("u_dv", 0))] if hp == 0 else []), inc="pe")
            marks["k2_reuse"] = pu
        ucol = 0 if dst_kind[0] == 'K' else 1
        dm = E("vector", lambda e, tw=tw: e.reduce_max(uacc[:, 3:4], u_ps[0:16, 0:tw], axis=AXX),
               waits=[("pe", M("pe"))], inc="dv")
        dm = E("vector", lambda e, ucol=ucol: e.tensor_max(uacc[:, ucol:ucol + 1],
                                                           uacc[:, ucol:ucol + 1], uacc[:, 3:4]),
               waits=[("dv", dm)], inc="dv")
        marks["u_dv"] = dm
        return marks

    def v_build_sub(xnT_buf, tt, tw, marks, subs):
        i = 0
        segs = [(0, 512), (512, 1024), (1024, VROW)]
        for sub in subs:
            for fc in range(8):
                for bi, (c0, c1) in enumerate(segs):
                    E("tensor", lambda e, fc=fc, bi=bi, c0=c0, c1=c1, sub=sub, tw=tw: e.matmul(
                        v_ps[bi][:, 0:c1 - c0],
                        xnT_buf[:, fc * tw + sub * 128: fc * tw + (sub + 1) * 128],
                        Wv_sb[:, fc * VROW + c0: fc * VROW + c1],
                        start=(fc == 0), stop=False),
                        waits=([("ac", marks.get("vps_reuse", 0)),
                                ("ac", marks.get("tr_evac", 0))] if fc == 0 and bi == 0 else []),
                        inc="pe")
            for bi, (c0, c1) in enumerate(segs):
                E("tensor", lambda e, bi=bi, c0=c0, c1=c1: e.matmul(
                    v_ps[bi][:, 0:c1 - c0], ones_bf[0:1, :], bvrow[0:1, c0:c1],
                    start=False, stop=True), inc="pe")
            pv = M("pe")
            for bi, (c0, c1) in enumerate(segs):
                E("scalar", lambda e, bi=bi, c0=c0, c1=c1, sub=sub, i=i: e.copy(
                    out=vraw[i][:, sub * VROW + c0: sub * VROW + c1], in_=v_ps[bi][:, 0:c1 - c0]),
                    waits=[("pe", pv), (f"q_vo{i}", marks.get(f"vraw_q{i}", 0))], inc="ac")
            marks["vps_reuse"] = M("ac")

    def v_build_out(tt, tw, marks):
        i = 0
        av = M("ac")
        nsub = tw // 128
        for hp in range(HP):
            qv = E("sync", lambda e, hp=hp, tt=tt, nsub=nsub, i=i: e.dma_start(
                out=V_dram[hp, :, tt * nsub * VW:(tt + 1) * nsub * VW]
                    .rearrange("p (s v) -> p s v", v=VW),
                in_=vraw[i][:].rearrange("p (s v) -> p s v", v=VROW)[:, 0:nsub, hp * VW:(hp + 1) * VW]),
                waits=[("ac", av)], inc=f"q_vo{i}", incval=16)
        marks[f"vraw_q{i}"] = M(f"q_vo{i}")
        return M("qs")

    P.fence()
    marks = {"tr_evac": M("ac"), "keo_reuse0": M("ac"), "keo_reuse1": M("ac"),
             "ro_reuse": M("ac"),
             "vps_reuse": M("ac"), "x2_reuse": M("dv"), "u_dv": M("dv")}

    def emit_tables(tt, src1, src2, tw):
        q1 = E("sync", lambda e, tt=tt, src1=src1, tw=tw: e.dma_start(
            out=t1s[0][:, 0:HP * tw], in_=src1[tt]),
            waits=[("dv", marks.get("tab_reuse", 0))], inc="q_t0", incval=16)
        E("sync", lambda e, tt=tt, src2=src2, tw=tw: e.dma_start(
            out=t2s[0][:, 0:HP * tw], in_=src2[tt]),
            inc="q_t0", incval=16)
        marks["tab_q"] = ("q_t0", M("q_t0"))

    gen = ln_and_transpose(x_kv, 0, xnT[0], marks, 512, "lnk0")
    for _ in gen:
        pass
    for tt in range(NT1):
        i = tt % 2
        ni = (tt + 1) % 2
        emit_tables(tt, T1k, T2k, 512)
        _rope_tab_gate[0] = marks["tab_q"]
        if tt + 1 < NT1:
            marks[f"lnk{tt+1}_pefree"] = M("pe")
            nxt = ln_and_transpose(x_kv, tt + 1, xnT[ni], marks, 512, f"lnk{tt+1}")
        else:
            nxt = iter(())

        def filler(hp, tt=tt, nxt=nxt):
            if hp % 2 == 1:
                v_build_sub(xnT[tt % 2][:], tt, 512, marks, [(hp - 1) // 2])
            next(nxt, None)

        marks["filler"] = filler
        rope_project(xnT[i][:], Wk_sb, bkc, t1s[0][:], t2s[0][:], ('K', tt), tt, 512, marks)
        marks["tab_reuse"] = M("dv")
        for _ in nxt:
            pass
        v_build_out(tt, 512, marks)
    p1_done_q = M("qs")
    p1_done_pe = M("pe")

    E("gpsimd", lambda e: e.dma_start(
        out=Wk_sb[:].rearrange("p (c n) -> p c n", c=8),
        in_=Wq_p.rearrange("(c p) n -> p c n", p=128)),
        waits=[("pe", p1_done_pe)], inc="q_onceg", incval=16)
    qg_w_cell[0] = M("q_onceg")
    QW = min(512, OWN)
    for tt in range(OWN // QW):
        i = tt % 2
        emit_tables(tt, T1q, T2q, QW)
        _rope_tab_gate[0] = marks["tab_q"]
        marks[f"lnq{tt}_pefree"] = M("pe")
        for _ in ln_and_transpose(x_own, tt, xnT[i], marks, QW, f"lnq{tt}"):
            pass
        av = M("ac")
        for fc in range(8):
            E("vector", lambda e, fc=fc, tt=tt, i=i, QW=QW: e.tensor_copy(
                out=xn_sb[:, fc * OWN + tt * QW: fc * OWN + (tt + 1) * QW],
                in_=xnT[i][:, fc * QW:(fc + 1) * QW]), waits=([("ac", av)] if fc == 0 else []),
                inc="dv")
        marks[f"lnq{tt + 2}_dvfree"] = M("dv")
        rope_project(xnT[i][:], Wk_sb, bqc, t1s[0][:], t2s[0][:], ('Q', tt), tt, QW, marks)
        marks["tab_reuse"] = M("dv")
    p1b_done = M("ac")

    P.fence()
    d0 = E("vector", lambda e: e.tensor_mul(uacc[:, 2:3], uacc[:, 0:1], uacc[:, 1:2]),
           waits=[("dv", M("dv")), ("ac", p1b_done)], inc="dv")
    du = E("vector", lambda e: e.tensor_copy(out=urow_sc[:, 0:1], in_=uacc[:, 2:3]),
           waits=[("dv", d0)], inc="dv")
    pu = E("tensor", lambda e: e.transpose(pb[7][0:16, 0:16], urow_sc[0:16, 0:16],
                                           identf[0:16, 0:16]),
           waits=[("dv", du)], inc="pe")
    av = E("scalar", lambda e: e.copy(out=urow_r[0:1, 0:16], in_=pb[7][0:1, 0:16]),
           waits=[("pe", pu)], inc="ac")
    dmx = E("vector", lambda e: e.tensor_max(urow_r[0:1, 0:8], urow_r[0:1, 0:8],
                                             urow_r[0:1, 8:16]), waits=[("ac", av)], inc="dv")
    pu2 = E("tensor", lambda e: e.matmul(pb[7][:, 0:8], ones_f[0:1, :], urow_r[0:1, 0:8],
                                         start=True, stop=True), waits=[("dv", dmx)], inc="pe")
    av2 = E("scalar", lambda e: e.activation(out=bias16[:], in_=pb[7][:, 0:8], func=AF.Ln),
           waits=[("pe", pu2)], inc="ac")
    av3 = E("scalar", lambda e: e.activation(out=bias16[:], in_=bias16[:], func=AF.Exp, scale=0.5),
            waits=[("ac", av2)], inc="ac")
    d1 = E("vector", lambda e: e.tensor_scalar_mul(bias16[:], bias16[:], -0.125),
          waits=[("ac", av3)], inc="dv")
    pu3 = E("tensor", lambda e: e.matmul(pb[6][:, 0:4 * NAMB], ones_f[0:1, :], pflag[0:1, :],
                                         start=True, stop=True), inc="pe")
    av4 = E("scalar", lambda e: e.copy(out=pfcol[:], in_=pb[6][:, 0:4 * NAMB]),
           waits=[("pe", pu3)], inc="ac")
    for a in range(4 * NAMB):
        d2 = E("vector", lambda e, a=a: e.tensor_scalar_add(
            biasamb[:, a * 8:(a + 1) * 8], bias16[:], pfcol[:, a:a + 1]),
            waits=[("dv", d1), ("ac", av4)], inc="dv")
    u_done_dv = M("dv")
    if debug:
        E("sync", lambda e: e.dma_start(out=dbg["b16"], in_=bias16[:]), waits=[("dv", u_done_dv)],
          inc="q_dbg", incval=16)

    P.fence()
    esA.close()
    esB = ExitStack()
    scope[0] = esB
    KEYS = lambda s: (s + 1) * 4 * QTW
    Kbuf = [sb(f"Kbuf{j}", [128, 4 * QTW * 4], BF16) for j in range(2)]
    Vbuf = [sb(f"Vbuf{j}", [128, (4 * QTW * 4 // 128) * VW], BF16) for j in range(2)]
    Kd = [sb(f"Kd{j}", [128, NK * 128], BF16) for j in range(2)]
    Vd = [sb(f"Vd{j}", [128, NK * VW], BF16) for j in range(2)]
    Pbuf = [sb(f"Pbuf{j}", [128, 2 * QTW], BF16) for j in range(2)]
    rsc = sb("rsc", [128, 4 * QTW], BF16)
    ctmp = sb("ctmp", [128, 4 * QTW], BF16)
    s2 = [pd[0], pd[1]]
    cA, cB = pb[4], pb[5]
    dvl, dvh = pb[6], pb[7]

    def p3_loads(u, marks):
        s, hp = divmod(u, HP)
        j = u % 2
        kq = E("sync", lambda e, s=s, hp=hp, j=j: e.dma_start(
            out=Kbuf[j][:, 0:KEYS(s)], in_=K_dram[hp, :, 0:KEYS(s)]),
            waits=[("ac", marks.get(f"kbuf_free{j}", 0))] +
                  ([("q_ko0", M("q_ko0")), ("q_ko1", M("q_ko1"))] if (s, hp) == (0, 0) else []),
            inc=f"q_K{j}", incval=16)
        vq = E("scalar", lambda e, s=s, hp=hp, j=j: e.dma_start(
            out=Vbuf[j][:, 0:(KEYS(s) // 128) * VW], in_=V_dram[hp, :, 0:(KEYS(s) // 128) * VW]),
            waits=([("q_vo0", M("q_vo0")), ("q_vo1", M("q_vo1"))] if (s, hp) == (0, 0) else []),
            inc=f"q_V{j}", incval=16)
        col = s * HP + hp
        kdq = E("gpsimd", lambda e, col=col, j=j: e.indirect_dma_start(
            out=Kd[j][:], out_offset=None, in_=K_dram.rearrange("h p s -> (h p) s"),
            in_offset=IndirectOffsetOnAxis(ap=kidx[:, col:col + 1], axis=1)),
            waits=[("ac", marks.get(f"kbuf_free{j}", 0))], inc=f"q_d{j}", incval=16)
        vdq = E("gpsimd", lambda e, col=col, j=j: e.indirect_dma_start(
            out=Vd[j][:], out_offset=None, in_=V_dram.rearrange("h p s -> (h p) s"),
            in_offset=IndirectOffsetOnAxis(ap=vidx[:, col:col + 1], axis=1)),
            inc=f"q_d{j}", incval=16)
        marks[f"kload{j}"] = (kq, vq, kdq, vdq)

    NUNITS = 4 * HP
    m3 = {"exp": {}, "pv": {}, "ctx_free": 0, "dv_free": 0}
    p3_loads(0, m3)
    for u in range(NUNITS):
        s, hp = divmod(u, HP)
        j = u % 2
        if u + 1 < NUNITS:
            p3_loads(u + 1, m3)
        kq, vq, kdq, vdq = m3[f"kload{j}"]
        qcol = hp * OWN + s * QTW
        ntile = NK + (4 * s + 3) * NK

        def src_of(kc, j=j):
            if kc < NK:
                return (lambda kc=kc: Kd[j][0:64, kc * 128:(kc + 1) * 128],
                        lambda kc=kc: Kd[j][64:128, kc * 128:(kc + 1) * 128],
                        kc * VW, Vd[j])
            ki = kc - NK
            return (lambda ki=ki: Kbuf[j][0:64, ki * 128:(ki + 1) * 128],
                    lambda ki=ki: Kbuf[j][64:128, ki * 128:(ki + 1) * 128],
                    ki * VW, Vbuf[j])

        def emit_S(kc):
            it = kc % 2
            klo, khi, _, _ = src_of(kc)
            w_s = [("ac", m3["exp"].get((u, kc - 2), m3["exp"].get("prev_last", 0) if kc < 2 else 0))]
            if kc == 0:
                w_s += [(f"q_K{j}", kq), (f"q_d{j}", vdq), ("dv", m3["dv_free"])]
            E("tensor", lambda e, klo=klo, it=it, qcol=qcol: e.matmul(
                s2[it][:, 0:QTW], klo(), Q_sb[0:64, qcol:qcol + QTW], start=True, stop=True),
                waits=w_s, inc="pe")
            return E("tensor", lambda e, khi=khi, it=it, qcol=qcol: e.matmul(
                s2[it][:, QTW:2 * QTW], khi(), Q_sb[64:128, qcol:qcol + QTW], start=True, stop=True,
                tile_position=(64, 0)), inc="pe")

        m3["smark"] = {0: emit_S(0)}
        for kc in range(ntile):
            it = kc % 2
            diag = kc < NK
            _, _, vcb, vsrc = src_of(kc)
            if kc + 1 < ntile:
                m3["smark"][kc + 1] = emit_S(kc + 1)
            if diag or (kc - NK) < 4 * s * NK:
                blo = lambda hp=hp: bias16[:, hp:hp + 1]
            else:
                a = s * NAMB + (kc - NK - 4 * s * NK)
                blo = lambda a=a, hp=hp: biasamb[:, a * 8 + hp:a * 8 + hp + 1]
            wexp = [("pe", m3["smark"][kc])]
            if (u, kc - 2) in m3["pv"]:
                wexp.append(("pe", m3["pv"][(u, kc - 2)]))
            e2 = E("scalar", lambda e, it=it, blo=blo: e.activation(
                out=Pbuf[it][:, 0:2 * QTW], in_=s2[it][:, 0:2 * QTW], func=AF.Exp, scale=0.125,
                bias=blo()), waits=wexp, inc="ac")
            m3["exp"][(u, kc)] = e2
            wpv = [("ac", e2)]
            if diag:
                g1 = E("gpsimd", lambda e, it=it, kc=kc: e.affine_select(
                    out=Pbuf[it][:, 0:QTW], in_=Pbuf[it][:, 0:QTW], compare_op=ALU.is_ge,
                    fill=0.0, base=-(kc * 128), pattern=[[1, QTW]], channel_multiplier=-1),
                    waits=[("ac", e2)], inc="gp")
                g2 = E("gpsimd", lambda e, it=it, kc=kc: e.affine_select(
                    out=Pbuf[it][:, QTW:2 * QTW], in_=Pbuf[it][:, QTW:2 * QTW], compare_op=ALU.is_ge,
                    fill=0.0, base=-(kc * 128), pattern=[[1, QTW]], channel_multiplier=-1),
                    inc="gp")
                wpv = [("gp", g2)]
            if kc == 0:
                wpv.append((f"q_V{j}", vq))
                wpv.append(("dv", m3["ctx_free"]))
            p1_ = E("tensor", lambda e, vsrc=vsrc, vcb=vcb, it=it, kc=kc, ntile=ntile: e.matmul(
                cA[0:65, 0:QTW], vsrc[:, vcb:vcb + 65], Pbuf[it][:, 0:QTW],
                start=(kc == 0), stop=(kc == ntile - 1)), waits=wpv, inc="pe")
            p2_ = E("tensor", lambda e, vsrc=vsrc, vcb=vcb, it=it, kc=kc, ntile=ntile: e.matmul(
                cB[0:65, 0:QTW], vsrc[:, vcb + 65:vcb + 130], Pbuf[it][:, QTW:2 * QTW],
                start=(kc == 0), stop=(kc == ntile - 1)), inc="pe")
            m3["pv"][(u, kc)] = p2_
        m3["exp"]["prev_last"] = m3["exp"][(u, ntile - 1)]
        pv_done = m3["pv"][(u, ntile - 1)]
        a1 = E("scalar", lambda e: e.activation(out=rsc[64:65, 0:QTW], in_=cA[64:65, 0:QTW],
                                                func=AF.Ln), waits=[("pe", pv_done)], inc="ac")
        a2 = E("scalar", lambda e: e.activation(out=rsc[64:65, QTW:2 * QTW], in_=rsc[64:65, 0:QTW],
                                                func=AF.Exp, scale=-1.0), waits=[("ac", a1)], inc="ac")
        a3 = E("scalar", lambda e: e.activation(out=rsc[64:65, 2 * QTW:3 * QTW], in_=cB[64:65, 0:QTW],
                                                func=AF.Ln), waits=[("pe", pv_done)], inc="ac")
        a4 = E("scalar", lambda e: e.activation(out=rsc[64:65, 3 * QTW:4 * QTW], in_=rsc[64:65, 2 * QTW:3 * QTW],
                                                func=AF.Exp, scale=-1.0), waits=[("ac", a3)], inc="ac")
        pdl = E("tensor", lambda e: e.matmul(dvl[0:64, 0:QTW], ones_bf[64:65, 0:64],
                                             rsc[64:65, QTW:2 * QTW], start=True, stop=True,
                                             tile_position=(64, 0)),
                waits=[("ac", a2), ("dv", m3["dv_free"])], inc="pe")
        pdh = E("tensor", lambda e: e.matmul(dvh[0:64, 0:QTW], ones_bf[64:65, 0:64],
                                             rsc[64:65, 3 * QTW:4 * QTW], start=True, stop=True,
                                             tile_position=(64, 0)), waits=[("ac", a4)], inc="pe")
        d1 = E("vector", lambda e: e.tensor_copy(out=ctmp[0:64, 0:QTW], in_=cA[0:64, 0:QTW]),
               waits=[("pe", pdl)], inc="dv")
        d2 = E("vector", lambda e, qcol=qcol: e.tensor_mul(
            ctx_sb[0:64, qcol:qcol + QTW], ctmp[0:64, 0:QTW], dvl[0:64, 0:QTW]),
            waits=[("dv", d1)], inc="dv")
        d3 = E("vector", lambda e: e.tensor_copy(out=ctmp[0:64, QTW:2 * QTW], in_=cB[0:64, 0:QTW]),
               waits=[("pe", pdh)], inc="dv")
        d4 = E("vector", lambda e: e.tensor_mul(
            ctmp[0:64, 2 * QTW:3 * QTW], ctmp[0:64, QTW:2 * QTW], dvh[0:64, 0:QTW]),
            waits=[("dv", d3)], inc="dv")
        m3["ctx_free"] = d4
        ib = (ntile) % 2
        pid = E("tensor", lambda e, ib=ib: e.matmul(
            s2[ib][64:128, 0:QTW], ident64[:], ctmp[0:64, 2 * QTW:3 * QTW],
            start=True, stop=True, tile_position=(0, 64)),
            waits=[("dv", d4), ("ac", m3["exp"]["prev_last"])], inc="pe")
        a5 = E("scalar", lambda e, ib=ib, qcol=qcol: e.copy(
            out=ctx_sb[64:128, qcol:qcol + QTW], in_=s2[ib][64:128, 0:QTW]),
            waits=[("pe", pid)], inc="ac")
        m3["exp"]["prev_last"] = a5
        m3["dv_free"] = d4
        m3[f"kbuf_free{j}"] = a5
    p3_done_ac = M("ac")
    p3_done_dv = M("dv")
    if debug:
        E("sync", lambda e: e.dma_start(out=dbg["ctx"], in_=ctx_sb[:]),
          waits=[("ac", p3_done_ac), ("dv", p3_done_dv)], inc="q_dbg", incval=16)
        E("sync", lambda e: e.dma_start(out=dbg["Q"], in_=Q_sb[:]), inc="q_dbg", incval=16)
        E("sync", lambda e: e.dma_start(out=dbg["K"], in_=K_dram), inc="q_dbg", incval=16)
        E("sync", lambda e: e.dma_start(out=dbg["V"], in_=V_dram), inc="q_dbg", incval=16)

    P.fence()
    esB.close()
    esC = ExitStack()
    scope[0] = esC
    wfcT = sb("wfcT", [128, FF], BF16)
    wfcT2 = sb("wfcT2", [128, 7 * FF], BF16)
    wf = [sb("wf0", [128, D])] * 2
    wfs = sb("wfs", [128, D // 2], BF16)
    wft = [sb(f"wft{i}", [128, D], BF16) for i in range(2)]
    hbuf = [sb(f"hbuf{i}", [128, 512], BF16) for i in range(2)]
    h2buf = sb("h2buf", [128, 512], BF16)
    srow = sb("srow", [1, 1536])
    statmu = sb("statmu", [1, 512])
    statsv = sb("statsv", [1, 512])
    statrs = sb("statrs", [1, 512])
    statmu_b = sb("statmu_b", [1, OWN], BF16)
    statsv_b = sb("statsv_b", [1, OWN], BF16)

    def wfcT_tile(fc, hc):
        base = hc * 128
        if fc == 0:
            return wfcT[:, base:base + 128]
        return wfcT2[:, (fc - 1) * FF + base:(fc - 1) * FF + base + 128]


    trn5 = pb[0].bitcast(BF16)
    fcp = [pb[1], pb[2]]
    stp_s, stp_q = pb[3], pb[4]
    NT5 = OWN // 512
    m5 = {"wft_free": {0: 0, 1: 0}, "trn_free": 0, "hb_free": {0: 0, 1: 0}}
    for tt in range(NT5):
        i5 = tt % 2
        for hc in range(HCN):
            i = hc % 2
            if tt == 0:
                qv = E("sync", lambda e, hc=hc, i=i: e.dma_start(
                    out=wf[i][:], in_=Wfc[hc * 128:(hc + 1) * 128, :]),
                    waits=[("dv", M("dv"))], inc="q_wf0", incval=16)
                for hf in range(2):
                    c0_, c1_ = hf * (D // 2), (hf + 1) * (D // 2)
                    v1 = E("vector", lambda e, i=i, c0_=c0_, c1_=c1_: e.tensor_scalar(
                        out=wfs[:], in0=wf[i][:, c0_:c1_], scalar1=gam[:, 0:1], scalar2=0.5,
                        op0=ALU.mult, op1=ALU.is_gt),
                        waits=[("q_wf0", qv), ("dv", M("dv"))], inc="dv")
                    v2 = E("vector", lambda e, i=i, c0_=c0_, c1_=c1_: e.tensor_scalar(
                        out=wft[i][:, c0_:c1_], in0=wf[i][:, c0_:c1_], scalar1=gam[:, 0:1],
                        scalar2=-0.5, op0=ALU.mult, op1=ALU.is_lt), waits=[("q_wf0", qv)], inc="dv")
                    v3 = E("vector", lambda e, i=i, c0_=c0_, c1_=c1_: e.tensor_sub(
                        wft[i][:, c0_:c1_], wfs[:], wft[i][:, c0_:c1_]), waits=[("dv", v2)], inc="dv")
                m5["wft_free"][i] = v3
                pts = []
                for fc in range(8):
                    pt = E("tensor", lambda e, i=i, fc=fc: e.transpose(
                        trn5[:, fc * 128:(fc + 1) * 128], wft[i][:, fc * 128:(fc + 1) * 128],
                        identb[:]),
                        waits=([("dv", v3), ("ac", m5["trn_free"])] if fc == 0 else []), inc="pe")
                av = E("scalar", lambda e, hc=hc: e.copy(out=wfcT[:, hc * 128:(hc + 1) * 128],
                                                         in_=trn5[:, 0:128]),
                       waits=[("pe", M("pe"))], inc="ac")
                for fc in range(1, 8):
                    av = E("scalar", lambda e, hc=hc, fc=fc: e.copy(
                        out=wfcT2[:, (fc - 1) * FF + hc * 128:(fc - 1) * FF + hc * 128 + 128],
                        in_=trn5[:, fc * 128:(fc + 1) * 128]), inc="ac")
                m5["trn_free"] = av
                m5[f"wfcT_ready{hc}"] = av
            fi = hc % 2
            wfcq = [("ac", m5[f"wfcT_ready{hc}"])]
            for fc in range(8):
                E("tensor", lambda e, fc=fc, hc=hc, tt=tt, fi=fi: e.matmul(
                    fcp[fi][:, 0:512], wfcT_tile(fc, hc),
                    xn_sb[:, fc * OWN + tt * 512: fc * OWN + (tt + 1) * 512],
                    start=(fc == 0), stop=(fc == 7)),
                    waits=(wfcq + [("ac", m5[f"fcp_free{fi}"] if f"fcp_free{fi}" in m5 else 0)]) if fc == 0 else [],
                    inc="pe")
            pv = M("pe")
            hb = hc % 2
            a1 = E("scalar", lambda e, hb=hb, fi=fi: e.activation(
                out=hbuf[hb][:], in_=fcp[fi][:, 0:512], func=AF.Gelu),
                waits=[("pe", pv), (f"q_h{hb}", m5["hb_free"][hb]), ("dv", m5.get(f"h2_{hb}", 0))], inc="ac")
            m5[f"fcp_free{fi}"] = a1
            d1 = E("vector", lambda e, hb=hb: e.tensor_mul(h2buf[:], hbuf[hb][:], hbuf[hb][:]),
                   waits=[("ac", a1), ("pe", m5.get("st_pe", 0))], inc="dv")
            m5[f"h2_{hb}"] = d1
            E("tensor", lambda e, hc=hc, hb=hb: e.matmul(
                stp_s[0:1, 0:512], ones_bf[:, 0:1], hbuf[hb][:],
                start=(hc == 0), stop=(hc == HCN - 1)), waits=[("ac", a1)], inc="pe")
            E("tensor", lambda e, hc=hc: e.matmul(
                stp_q[0:1, 0:512], ones_bf[:, 0:1], h2buf[:],
                start=(hc == 0), stop=(hc == HCN - 1)), waits=[("dv", d1)], inc="pe")
            m5["st_pe"] = M("pe")
            qh = E("sync", lambda e, hc=hc, tt=tt, hb=hb: e.dma_start(
                out=h_dram[hc, :, tt * 512:(tt + 1) * 512], in_=hbuf[hb][:]),
                waits=[("ac", a1)], inc=f"q_h{hb}", incval=16)
            m5["hb_free"][hb] = qh
        pv = M("pe")
        a1 = E("scalar", lambda e, tt=tt: e.activation(
            out=statmu[0:1, 0:512], in_=stp_s[0:1, 0:512],
            func=AF.Identity, scale=-1.0 / FF),
            waits=[("pe", pv), ("dv", m5.get("stat_dv", 0))], inc="ac")
        d1 = E("vector", lambda e, tt=tt: e.scalar_tensor_tensor(
            out=srow[0:1, 0:512], in0=statmu[0:1, 0:512], scalar=1.0,
            in1=statmu[0:1, 0:512], op0=ALU.mult, op1=ALU.mult),
            waits=[("ac", a1)], inc="dv")
        d2 = E("vector", lambda e, tt=tt: e.scalar_tensor_tensor(
            out=srow[0:1, 512:1024], in0=stp_q[0:1, 0:512], scalar=1.0 / FF, in1=srow[0:1, 0:512],
            op0=ALU.mult, op1=ALU.subtract), waits=[("dv", d1), ("pe", pv)], inc="dv")
        a2 = E("scalar", lambda e: e.activation(out=srow[0:1, 1024:1536], in_=srow[0:1, 512:1024],
                                                func=AF.Ln, bias=epscol[0:1, 0:1]), waits=[("dv", d2)], inc="ac")
        a3 = E("scalar", lambda e, tt=tt: e.activation(
            out=statsv[0:1, 0:512], in_=srow[0:1, 1024:1536],
            func=AF.Exp, scale=0.5), waits=[("ac", a2)], inc="ac")
        a4 = E("scalar", lambda e, tt=tt: e.activation(
            out=statrs[0:1, 0:512], in_=srow[0:1, 1024:1536],
            func=AF.Exp, scale=-0.5),
            waits=[("ac", a2), ("q_rs", m5.get("rs_q", 0))], inc="ac")
        qrs = E("sync", lambda e, tt=tt: e.dma_start(
            out=statrs_dram[0:1, tt * 512:(tt + 1) * 512], in_=statrs[0:1, 0:512]),
            waits=[("ac", a4)], inc="q_rs", incval=16)
        m5["rs_q"] = qrs
        E("vector", lambda e, tt=tt: e.tensor_copy(
            out=statmu_b[0:1, tt * 512:(tt + 1) * 512], in_=statmu[0:1, 0:512]),
            waits=[("ac", a4)], inc="dv")
        E("vector", lambda e, tt=tt: e.tensor_copy(
            out=statsv_b[0:1, tt * 512:(tt + 1) * 512], in_=statsv[0:1, 0:512]),
            inc="dv")
        m5["stat_dv"] = M("dv")
        m5["st_pe_gate"] = M("ac")
    p5_done = M("dv")
    E("sync", lambda e: e.dma_start(out=statb_dram[0:1, :], in_=statmu_b[:]),
      waits=[("dv", p5_done)], inc="q_stat", incval=16)
    E("sync", lambda e: e.dma_start(out=statb_dram[1:2, :], in_=statsv_b[:]),
      inc="q_stat", incval=16)
    q_stat_done = M("q_stat")
    if debug:
        E("sync", lambda e: e.dma_start(out=dbg["h"], in_=h_dram),
          waits=[("q_h0", M("q_h0")), ("q_h1", M("q_h1")), ("dv", p5_done)],
          inc="q_dbg", incval=16)
        E("sync", lambda e: e.dma_start(out=dbg["stat"][0:1, :], in_=statmu[:]),
          waits=[("dv", p5_done), ("ac", M("ac"))], inc="q_dbg", incval=16)
        E("sync", lambda e: e.dma_start(out=dbg["stat"][1:2, :], in_=statsv[:]), inc="q_dbg", incval=16)
        E("sync", lambda e: e.dma_start(out=dbg["stat"][2:3, :], in_=statrs[:]), inc="q_dbg", incval=16)

    P.fence()
    esC.close()
    esD = ExitStack()
    scope[0] = esD
    wpp = sb("wpp", [128, HCN * D], BF16)
    borow = sb("borow", [1, D], BF16)
    statmu_b6 = sb("statmu_b6", [1, 512], BF16)
    statsv_b6 = sb("statsv_b6", [1, 512], BF16)
    statrs6 = sb("statrs6", [1, 512])
    wo_sb = sb("wo_sb", [128, 8 * D], BF16)
    hcol = [sb("hcol0", [128, HCN * 128], BF16)] * 2
    gbuf = sb("gbuf", [128, 512], BF16)
    rstdc = sb("rstdc", [128, 1])
    xf6 = [sb("xf60", [128, 512])] * 2
    out6 = sb("out6", [128, 512])
    E("gpsimd", lambda e: e.dma_start(out=borow[:], in_=bo[None, :]), inc="q_w6", incval=16)
    qw = E("gpsimd", lambda e: e.dma_start(out=wpp[:].rearrange("p (c n) -> p c n", c=HCN),
                                           in_=wp_dram.rearrange("c p n -> p c n")),
           waits=[("q_wp0", M("q_wp0")), ("q_wp1", M("q_wp1"))], inc="q_w6", incval=16)
    qw2 = E("gpsimd", lambda e: e.dma_start(out=wo_sb[:].rearrange("p (c n) -> p c n", c=8),
                                            in_=Wo_p.rearrange("(c p) n -> p c n", p=128)),
            inc="q_w6", incval=16)
    pjp = [pb[0], pb[1]]
    wop = [pb[2], pb[3]]
    rsp = pb[4]
    m6 = {}
    for tc in range(NT6):
        i = tc % 2
        qh = E("sync", lambda e, tc=tc, i=i: e.dma_start(
            out=hcol[i][:].rearrange("p (c t) -> p c t", c=HCN),
            in_=h_dram[:, :, tc * 128:(tc + 1) * 128].rearrange("c p t -> p c t")),
            waits=[("pe", m6.get(f"hcol_free{i}", 0))] +
                  ([("q_h0", M("q_h0")), ("q_h1", M("q_h1"))] if tc < 2 else []),
            inc=f"q_c{i}", incval=16)

        if tc % 4 == 0:
            stw = [("pe", M("pe"))] + ([("q_stat", q_stat_done)] if tc == 0 else [])
            qrs6 = E("sync", lambda e, tc=tc: e.dma_start(
                out=statrs6[0:1, 0:512], in_=statrs_dram[0:1, tc * 128:tc * 128 + 512]),
                waits=stw, inc="q_st6", incval=16)
            E("sync", lambda e, tc=tc: e.dma_start(
                out=statmu_b6[0:1, 0:512], in_=statb_dram[0:1, tc * 128:tc * 128 + 512]),
                inc="q_st6", incval=16)
            E("sync", lambda e, tc=tc: e.dma_start(
                out=statsv_b6[0:1, 0:512], in_=statb_dram[1:2, tc * 128:tc * 128 + 512]),
                inc="q_st6", incval=16)
            m6["rs6_q"] = M("q_st6")
        pr = E("tensor", lambda e, tc=tc: e.matmul(
            rsp[:, 0:1], statrs6[0:1, (tc % 4) * 128:(tc % 4 + 1) * 128], ones_f[0:1, 0:1],
            start=True, stop=True),
            waits=[("ac", m6.get("rst_free", 0)), ("q_st6", m6["rs6_q"])], inc="pe")
        m6["rsp_pe"] = pr
        ar = E("scalar", lambda e: e.copy(out=rstdc[:], in_=rsp[:, 0:1]), waits=[("pe", pr)], inc="ac")
        m6["rst_free"] = ar
        for half in range(2):
            for hc in range(HCN):
                E("tensor", lambda e, hc=hc, half=half, tc=tc, i=i: e.matmul(
                    pjp[half][:, 0:512], hcol[i][:, hc * 128:(hc + 1) * 128],
                    wpp[:, hc * D + half * 512:hc * D + (half + 1) * 512],
                    start=(hc == 0), stop=False),
                    waits=([(f"q_c{tc % 2}", qh), ("q_w6", 48), ("ac", m6.get(f"pj_free{half}", 0))]
                           if hc == 0 else []), inc="pe")
            E("tensor", lambda e, half=half, tc=tc: e.matmul(
                pjp[half][:, 0:512], statmu_b6[0:1, (tc % 4) * 128:(tc % 4 + 1) * 128],
                c1row[0:1, half * 512:(half + 1) * 512], start=False, stop=False),
                waits=[("q_st6", m6["rs6_q"])], inc="pe")
            E("tensor", lambda e, half=half, tc=tc: e.matmul(
                pjp[half][:, 0:512], statsv_b6[0:1, (tc % 4) * 128:(tc % 4 + 1) * 128],
                c0row[0:1, half * 512:(half + 1) * 512], start=False, stop=True), inc="pe")
            pv = M("pe")
            ag = E("scalar", lambda e, half=half: e.activation(
                out=gbuf[:, 0:512], in_=pjp[half][:, 0:512],
                func=AF.Silu, scale=rstdc[:]),
                waits=[("pe", pv), ("ac", ar), ("dv", m6.get("g_free", 0))], inc="ac")
            m6[f"pj_free{half}"] = ag
            for cc in range(8):
                E("tensor", lambda e, cc=cc, half=half, tc=tc: e.matmul(
                    wop[half][:, 0:512], ctx_sb[:, cc * OWN + tc * 128:cc * OWN + (tc + 1) * 128],
                    wo_sb[:, cc * D + half * 512:cc * D + (half + 1) * 512],
                    start=(cc == 0), stop=False),
                    waits=([("q_w6", 48), ("dv", m6.get(f"wo_free{half}", 0))] if cc == 0 else []),
                    inc="pe")
            E("tensor", lambda e, half=half: e.matmul(
                wop[half][:, 0:512], ones_bf[0:1, 0:128], borow[0:1, half * 512:(half + 1) * 512],
                start=False, stop=True), inc="pe")
            pv2 = M("pe")
            d1 = E("vector", lambda e, half=half: e.tensor_mul(
                out6[:, 0:512], wop[half][:, 0:512],
                gbuf[:, 0:512]),
                waits=[("pe", pv2), ("ac", ag), ("q_y", m6.get("y_q", 0))], inc="dv")
            m6[f"wo_free{half}"] = d1
            m6["g_free"] = d1
            qx = E("sync", lambda e, tc=tc, half=half: e.dma_start(
                out=xf6[0][:, 0:512],
                in_=x_own[tc * 128:(tc + 1) * 128, half * 512:(half + 1) * 512]),
                waits=[("dv", m6.get("x_free", 0))], inc="q_f0", incval=16)
            d2 = E("vector", lambda e, half=half: e.tensor_add(
                out6[:, 0:512], out6[:, 0:512], xf6[0][:, 0:512]),
                waits=[("q_f0", qx), ("dv", d1)], inc="dv")
            m6["x_free"] = d2
            yq = E("sync", lambda e, tc=tc, half=half: e.dma_start(
                out=y[tc * 128:(tc + 1) * 128, half * 512:(half + 1) * 512], in_=out6[:, 0:512]),
                waits=[("dv", d2)], inc="q_y", incval=16)
            m6["y_q"] = yq
        m6[f"hcol_free{i}"] = M("pe")
        m6["rst_free"] = M("ac")
    P.fence()
    P.realize()
    esD.close()
    esP.close()
    meta = dict(SEQ=SEQ, QTW=QTW, NK=NK, OWN=OWN, dbg=sorted(dbg.keys()))
    return nc, meta


BF = ml_dtypes.bfloat16




def eo_perm():
    perm = np.zeros(D, dtype=np.int64)
    for hp in range(HP):
        for i in range(64):
            perm[hp * 128 + i] = 128 * hp + 2 * i
            perm[hp * 128 + 64 + i] = 128 * hp + 2 * i + 1
    return perm


def wo_row_perm():
    perm = np.zeros(D, dtype=np.int64)
    for hp in range(HP):
        for i in range(64):
            perm[hp * 128 + i] = 64 * hp + i
            perm[hp * 128 + 64 + i] = 512 + 64 * hp + i
    return perm


def rope_tables(SEQ, positions):
    i = np.arange(D // 2, dtype=np.float32)
    fc = np.power(np.float32(10000.0), -2.0 * i / np.float32(D)).astype(np.float32)
    pos = positions.astype(np.float32)
    T1 = np.zeros((HP, 128, len(pos)), dtype=np.float32)
    T2 = np.zeros((HP, 128, len(pos)), dtype=np.float32)
    for hp in range(HP):
        f = fc[64 * hp:64 * hp + 64]
        ang = pos[None, :] * f[:, None]
        c, s = np.cos(ang), np.sin(ang)
        T1[hp, 0:64] = c
        T1[hp, 64:128] = s
        T2[hp, 0:64] = s
        T2[hp, 64:128] = c
    NT = len(pos) // 512
    T1 = T1.reshape(HP, 128, NT, 512).transpose(2, 1, 0, 3).reshape(NT, 128, HP * 512)
    T2 = T2.reshape(HP, 128, NT, 512).transpose(2, 1, 0, 3).reshape(NT, 128, HP * 512)
    return np.ascontiguousarray(T1).astype(BF), np.ascontiguousarray(T2).astype(BF)


def own_rows(SEQ, m):
    QTW = SEQ // 16
    rows = []
    for s in range(4):
        t = m + 4 * s
        rows.append(np.arange(t * QTW, (t + 1) * QTW))
    return np.concatenate(rows)


def prep_core(inp, c, SEQ):
    QTW = SEQ // 16
    NK = QTW // 128
    NAMB = 3 * NK
    NKS_T = SEQ // 128
    b, m = divmod(c, 4)
    x = np.asarray(inp["L0_attn_x"], dtype=np.float32)
    rows = own_rows(SEQ, m)
    perm = eo_perm()
    woperm = wo_row_perm()

    Wq = np.asarray(inp["L0_attn_Wq"], np.float32)
    Wk = np.asarray(inp["L0_attn_Wk"], np.float32)
    Wv = np.asarray(inp["L0_attn_Wv"], np.float32)
    Wo = np.asarray(inp["L0_attn_Wo"], np.float32)
    bq = np.asarray(inp["L0_attn_bq"], np.float32)
    bk = np.asarray(inp["L0_attn_bk"], np.float32)
    bv = np.asarray(inp["L0_attn_bv"], np.float32)
    bo = np.asarray(inp["L0_attn_bo"], np.float32)

    Wv_e = np.zeros((D, VROW), np.float32)
    bv_e = np.zeros(VROW, np.float32)
    for hp in range(HP):
        blk = hp * VW
        Wv_e[:, blk:blk + 64] = Wv[:, 64 * hp:64 * hp + 64]
        Wv_e[:, blk + 65:blk + 129] = Wv[:, 512 + 64 * hp:512 + 64 * hp + 64]
        bv_e[blk:blk + 64] = bv[64 * hp:64 * hp + 64]
        bv_e[blk + 64] = 1.0
        bv_e[blk + 65:blk + 129] = bv[512 + 64 * hp:512 + 64 * hp + 64]
        bv_e[blk + 129] = 1.0

    T1k, T2k = rope_tables(SEQ, np.arange(SEQ))
    T1q, T2q = rope_tables(SEQ, rows)

    imat = np.vstack([np.eye(64), -np.eye(64)]).astype(BF)
    ind = np.zeros((HP, 128, 16), np.float32)
    for hp in range(HP):
        ind[hp, 0:64, hp] = 1.0
        ind[hp, 64:128, 8 + hp] = 1.0

    p = np.arange(128, dtype=np.int64)
    kidx = np.zeros((128, 32), np.uint32)
    vidx = np.zeros((128, 32), np.uint32)
    for s in range(4):
        t = m + 4 * s
        for hp in range(HP):
            kidx[:, s * HP + hp] = hp * 128 * SEQ + p * SEQ + t * QTW
            vidx[:, s * HP + hp] = hp * 128 * (NKS_T * VW) + p * (NKS_T * VW) + t * NK * VW
    pflag = np.zeros((1, 4 * NAMB), np.float32)
    for s in range(4):
        for a in range(NAMB):
            pflag[0, s * NAMB + a] = 0.0 if a < m * NK else -60000.0

    return {
        "x_kv": np.ascontiguousarray(x[b]),
        "x_own": np.ascontiguousarray(x[b][rows]),
        "Wq_p": np.ascontiguousarray(Wq[:, perm]),
        "Wk_p": np.ascontiguousarray(Wk[:, perm]),
        "Wv_e": Wv_e,
        "bq_p": np.ascontiguousarray(bq[perm]).reshape(8, 128),
        "bk_p": np.ascontiguousarray(bk[perm]).reshape(8, 128),
        "bv_e": bv_e,
        "Wo_p": np.ascontiguousarray(Wo[woperm, :]),
        "bo": bo,
        "Wfc": np.asarray(inp["L0_ffn_W_fc"], np.float32),
        "Wpj": np.asarray(inp["L0_ffn_W_proj"], np.float32),
        "lnw": np.asarray(inp["L0_ffn_ln_w"], np.float32).reshape(32, 128),
        "lnb": np.asarray(inp["L0_ffn_ln_b"], np.float32).reshape(32, 128),
        "T1k": T1k, "T2k": T2k, "T1q": T1q, "T2q": T2q,
        "imat": imat,
        "ind": ind.astype(BF),
        "kidx": kidx, "vidx": vidx,
        "pflag": pflag,
    }




_CACHE = {}


def _get_nc():
    if "nc" not in _CACHE:
        _CACHE["nc"] = build(SEQ=8192, debug=False)[0]
    return _CACHE["nc"]


def kernel(**inputs):
    from concourse.bass_utils import run_bass_kernel_spmd
    SEQ = 8192
    nc = _get_nc()
    in_maps = [prep_core(inputs, c, SEQ) for c in range(8)]
    res = run_bass_kernel_spmd(nc, in_maps, core_ids=list(range(8)))
    out = np.zeros((2, SEQ, D), np.float32)
    for c in range(8):
        b, m = divmod(c, 4)
        out[b][own_rows(SEQ, m)] = res.results[c]["y"]
    return out


# version 05 (final)
# speedup vs baseline: 1.0667x; 1.0459x over previous
import numpy as np
from collections import defaultdict
from contextlib import ExitStack

import ml_dtypes
import concourse.bass as bass
import concourse.mybir as mybir
from concourse.bass import IndirectOffsetOnAxis

F32 = mybir.dt.float32
BF16 = mybir.dt.bfloat16
U32 = mybir.dt.uint32
AF = mybir.ActivationFunctionType
ALU = mybir.AluOpType
AXX = mybir.AxisListType.X

D = 1024
H = 16
HP = 8
FF = 4096
VW = 132
VROW = HP * VW
LN_EPS = 1e-5
ABS_EPS = 1e-6
NEG = -60000.0


class Prog:
    def __init__(self, nc):
        self.nc = nc
        self.ops = {e: [] for e in ("tensor", "vector", "scalar", "gpsimd", "sync")}
        self.sems = {}
        self.cnt = defaultdict(int)

    def sem(self, name):
        if name not in self.sems:
            self.sems[name] = self.nc.alloc_semaphore(name)
        return self.sems[name]

    def emit(self, eng, fn, waits=(), inc=None, incval=1):
        if inc is not None:
            self.sem(inc)
            self.cnt[inc] += incval
        for (s, v) in waits:
            self.sem(s)
        self.ops[eng].append((fn, tuple(waits), inc, incval))
        return self.cnt[inc] if inc is not None else None

    def mark(self, name):
        return self.cnt[name]

    def fence(self, engines=("tensor", "vector", "scalar", "gpsimd", "sync")):
        waits = tuple((s, self.cnt[s]) for s in self.sems)
        for e in engines:
            self.ops[e].append((None, waits, None, 1))

    def realize(self):
        with self.nc.Block() as block:
            for eng_name in ("sync", "tensor", "scalar", "vector", "gpsimd"):
                ops = self.ops[eng_name]
                if not ops:
                    continue

                def body(e, ops=ops):
                    for fn, waits, inc, incval in ops:
                        for (s, v) in waits:
                            if v > 0:
                                e.wait_ge(self.sems[s], v)
                        if fn is None:
                            continue
                        ins = fn(e)
                        if inc is not None:
                            ins.then_inc(self.sems[inc], incval)
                getattr(block, eng_name)(body)


def build(SEQ=8192, debug=False):
    QTW = SEQ // 16
    NK = QTW // 128
    assert QTW % 128 == 0 and QTW <= 512
    OWN = 4 * QTW
    NT1 = SEQ // 512
    NAMB = 3 * NK
    NKS_T = SEQ // 128
    NT6 = OWN // 128
    HCN = FF // 128

    nc = bass.Bass("TRN2", target_bir_lowering=False)
    P = Prog(nc)
    E, M = P.emit, P.mark
    for s in ("pe", "dv", "ac", "gp"):
        P.sem(s)

    def inp(name, shape, dt=F32):
        return nc.dram_tensor(name, shape, dt, kind="ExternalInput").ap()

    x_kv = inp("x_kv", [SEQ, D])
    x_own = inp("x_own", [OWN, D])
    Wq_p = inp("Wq_p", [D, D])
    Wk_p = inp("Wk_p", [D, D])
    Wv_e = inp("Wv_e", [D, VROW])
    bq_p = inp("bq_p", [8, 128])
    bk_p = inp("bk_p", [8, 128])
    bv_e = inp("bv_e", [VROW])
    Wo_p = inp("Wo_p", [D, D])
    bo = inp("bo", [D])
    Wfc = inp("Wfc", [FF, D])
    Wpj = inp("Wpj", [D, FF])
    lnw = inp("lnw", [HCN, 128])
    lnb = inp("lnb", [HCN, 128])
    Tk = inp("Tk", [SEQ // 512, 128, 2 * HP * 512], BF16)
    Tq = inp("Tq", [OWN // 512, 128, 2 * HP * 512], BF16)
    imat_i = inp("imat", [128, 64], BF16)
    ind_i = inp("ind", [HP, 128, 16], BF16)
    kidx_i = inp("kidx", [128, 32], U32)
    vidx_i = inp("vidx", [128, 32], U32)
    pflag_i = inp("pflag", [1, 4 * NAMB])

    y = nc.dram_tensor("y", [OWN, D], F32, kind="ExternalOutput").ap()

    K_dram = nc.dram_tensor("K_dram", [HP, 128, SEQ], BF16).ap()
    V_dram = nc.dram_tensor("V_dram", [HP, 128, NKS_T * VW], BF16).ap()
    h_dram = nc.dram_tensor("h_dram", [HCN, 128, OWN], BF16).ap()
    wp_dram = nc.dram_tensor("wp_dram", [HCN, 128, D], BF16).ap()
    statb_dram = nc.dram_tensor("statb_dram", [2, OWN], BF16).ap()
    statrs_dram = nc.dram_tensor("statrs_dram", [1, OWN], F32).ap()

    dbg = {}
    if debug:
        def dout(name, shape, dt=F32):
            dbg[name] = nc.dram_tensor("dbg_" + name, shape, dt, kind="ExternalOutput").ap()
        dout("K", [HP, 128, SEQ], BF16)
        dout("V", [HP, 128, NKS_T * VW], BF16)
        dout("Q", [128, HP * OWN], BF16)
        dout("xn", [8, 128, OWN], BF16)
        dout("ctx", [128, HP * OWN], BF16)
        dout("b16", [128, 16])
        dout("h", [HCN, 128, OWN], BF16)
        dout("wp", [HCN, 128, D], BF16)
        dout("gam", [128, 2])
        dout("c1", [1, D])
        dout("c0", [1, D])
        dout("stat", [3, OWN])

    esP = ExitStack()
    scope = [esP]
    sb = lambda name, shape, dt=F32: scope[0].enter_context(nc.sbuf_tensor(name, shape, dt))
    ps = lambda name, shape, dt=F32: esP.enter_context(nc.psum_tensor(name, shape, dt))

    ones_bf = sb("ones_bf", [128, 128], BF16)
    ones_f = sb("ones_f", [128, 128])
    identf = sb("identf", [128, 128])
    identb = sb("identb", [128, 128], BF16)
    ident64 = sb("ident64", [64, 64], BF16)
    epscol = sb("epscol", [128, 1])
    imat = sb("imat_s", [128, 64], BF16)
    ind16 = sb("ind16", [128, HP * 16], BF16)
    bqc = sb("bqc", [128, 8])
    bkc = sb("bkc", [128, 8])
    bvrow = sb("bvrow", [1, VROW], BF16)
    lnwc = sb("lnwc", [128, HCN])
    lnbc = sb("lnbc", [128, HCN])
    lnw_bf = sb("lnw_bf", [128, HCN], BF16)
    lnb_bf = sb("lnb_bf", [128, HCN], BF16)
    kidx = sb("kidx_s", [128, 32], U32)
    vidx = sb("vidx_s", [128, 32], U32)
    pflag = sb("pflag_s", [1, 4 * NAMB])
    bias16 = sb("bias16", [128, 8])
    biasamb = sb("biasamb", [128, 4 * NAMB * 8])
    uacc = sb("uacc", [16, 4])
    gam = sb("gam_s", [128, 2])
    c1row = sb("c1row", [1, D], BF16)
    c0row = sb("c0row", [1, D], BF16)
    Q_sb = sb("Q_sb", [128, HP * OWN], BF16)
    ctx_sb = sb("ctx_sb", [128, HP * OWN], BF16)
    xn_sb = sb("xn_sb", [128, 8 * OWN], BF16)
    urow_sc = sb("urow_sc", [16, 16])
    urow_r = sb("urow_r", [1, 16])
    pfcol = sb("pfcol", [128, 4 * NAMB])
    _rope_tab_gate = [("pe", 0)]

    E("gpsimd", lambda e: e.memset(ones_bf[:], 1.0))
    E("gpsimd", lambda e: e.memset(ones_f[:], 1.0))
    E("gpsimd", lambda e: e.memset(uacc[:], 0.0))
    E("gpsimd", lambda e: e.memset(urow_sc[:], 0.0))
    E("gpsimd", lambda e: e.memset(epscol[:], LN_EPS))
    for t, n in ((identf, 128), (identb, 128), (ident64, 64)):
        E("gpsimd", lambda e, t=t: e.memset(t[:], 0.0), inc="gp")
        E("gpsimd", lambda e, t=t, n=n: e.affine_select(
            out=t[:], in_=t[:], compare_op=ALU.not_equal, fill=1.0, base=0,
            pattern=[[-1, n]], channel_multiplier=1), waits=[("gp", M("gp"))], inc="gp")
    gp_consts = M("gp")

    E("sync", lambda e: e.dma_start(out=imat[:], in_=imat_i), inc="q_once", incval=16)
    E("sync", lambda e: e.dma_start(out=pflag[:], in_=pflag_i), inc="q_once", incval=16)
    E("sync", lambda e: e.dma_start(out=ind16[:].rearrange("p (h c) -> p h c", h=HP), in_=ind_i.rearrange("h p c -> p h c")), inc="q_once", incval=16)
    E("sync", lambda e: e.dma_start(out=kidx[:], in_=kidx_i), inc="q_once", incval=16)
    E("sync", lambda e: e.dma_start(out=vidx[:], in_=vidx_i), inc="q_once", incval=16)
    E("gpsimd", lambda e: e.dma_start(out=bvrow[:], in_=bv_e[None, :]), inc="q_onceg", incval=16)

    pd = [esP.enter_context(nc.psum_tensor(f"pd{i}", [128, 1024], F32)) for i in range(4)]
    pb = [pd[i // 2][:, (i % 2) * 512:(i % 2 + 1) * 512] for i in range(8)]
    btmp = sb("btmp", [HCN, 128])
    trb = pb[0]
    for (src, dst, rows) in ((bq_p, bqc, 8), (bk_p, bkc, 8), (lnw, lnwc, HCN), (lnb, lnbc, HCN)):
        qv = E("sync", lambda e, s=src, r=rows: e.dma_start(out=btmp[0:r, :], in_=s), waits=[("pe", M("pe"))], inc="q_btmp", incval=16)
        pv = E("tensor", lambda e, r=rows: e.transpose(trb[:, 0:r], btmp[0:r, :], identf[0:r, 0:r]),
               waits=[("q_btmp", qv), ("gp", gp_consts), ("ac", M("ac"))], inc="pe")
        E("scalar", lambda e, d=dst, r=rows: e.copy(out=d[:, 0:r], in_=trb[:, 0:r]),
          waits=[("pe", pv)], inc="ac")
    v = E("vector", lambda e: e.tensor_copy(out=lnw_bf[:], in_=lnwc[:]), waits=[("ac", M("ac"))], inc="dv")
    v = E("vector", lambda e: e.tensor_copy(out=lnb_bf[:], in_=lnbc[:]), waits=[("ac", M("ac"))], inc="dv")
    dv_ln = M("dv")

    esA = ExitStack()
    scope[0] = esA
    Wk_sb = sb("Wk_sb", [128, 8 * D], BF16)
    Wv_sb = sb("Wv_sb", [128, 8 * VROW], BF16)
    for (dsb, dsrc) in ((Wk_sb, Wk_p), (Wv_sb, Wv_e)):
        E("gpsimd", lambda e, a=dsb, b=dsrc: e.dma_start(
            out=a[:].rearrange("p (c n) -> p c n", c=8), in_=b.rearrange("(c p) n -> p c n", p=128)),
            inc="q_onceg", incval=16)
    qg_w_cell = [M("q_onceg")]

    esA0 = ExitStack()
    scope[0] = esA0
    gtile = [sb(f"gtile{i}", [128, D]) for i in range(2)]
    gabs = [sb(f"gabs{i}", [128, D], BF16) for i in range(2)]
    ginv = sb("ginv", [1, 4])
    gps_h = [pb[1], pb[2]]
    gbc = pb[3]

    def gamma_pass(view, nt, wi):
        pe_pair = [M("pe")]
        for t in range(nt):
            i = t % 2
            qv = E("sync", lambda e, t=t, i=i, view=view: e.dma_start(
                out=gtile[i][:], in_=view(t)), waits=[("pe", pe_pair[0])], inc=f"q_g{i}", incval=16)
            av = E("scalar", lambda e, i=i: e.activation(out=gabs[i][:], in_=gtile[i][:], func=AF.Abs),
                   waits=[(f"q_g{i}", qv)], inc="ac")
            for half in range(2):
                E("tensor", lambda e, i=i, t=t, half=half, nt=nt: e.matmul(
                    gps_h[half][0:1, 0:512],
                    ones_bf[:, 0:1], gabs[i][:, half * 512:(half + 1) * 512],
                    start=(t == 0), stop=(t == nt - 1)),
                    waits=[("ac", av)] + ([("dv", M("dv"))] if t == 0 else []), inc="pe")
            if t >= 1:
                pe_pair[0] = M("pe") - 2
        d = E("vector", lambda e, wi=wi: e.reduce_sum(
            ginv[0:1, wi:wi + 1], gps_h[0][0:1, 0:512], axis=AXX), waits=[("pe", M("pe"))], inc="dv")
        d = E("vector", lambda e, wi=wi: e.reduce_sum(
            ginv[0:1, wi + 2:wi + 3], gps_h[1][0:1, 0:512], axis=AXX), waits=[("pe", M("pe"))], inc="dv")
        return E("vector", lambda e, wi=wi: e.tensor_add(
            ginv[0:1, wi:wi + 1], ginv[0:1, wi:wi + 1], ginv[0:1, wi + 2:wi + 3]),
            waits=[("dv", d)], inc="dv")

    fcv = lambda t: Wfc[t * 128:(t + 1) * 128, :]
    pjv = lambda t: Wpj.rearrange("(g p) (t n) -> g t p n", p=128, n=D)[t // 4][t % 4]
    gamma_pass(fcv, FF // 128, 0)
    gamma_pass(pjv, (D // 128) * (FF // D), 1)
    dvv = E("vector", lambda e: e.tensor_scalar(out=ginv[0:1, 0:2], in0=ginv[0:1, 0:2],
                                                scalar1=1.0 / (D * FF),
                                                scalar2=ABS_EPS, op0=ALU.mult, op1=ALU.add),
            waits=[("dv", M("dv"))], inc="dv")
    dvv = E("vector", lambda e: e.reciprocal(out=ginv[0:1, 0:2], in_=ginv[0:1, 0:2]),
            waits=[("dv", dvv)], inc="dv")
    pv = E("tensor", lambda e: e.matmul(gbc[:, 0:2], ones_f[0:1, :], ginv[0:1, 0:2],
                                        start=True, stop=True), waits=[("dv", dvv)], inc="pe")
    ac_gam = E("scalar", lambda e: e.copy(out=gam[:], in_=gbc[:, 0:2]), waits=[("pe", pv)], inc="ac")
    if debug:
        E("sync", lambda e: e.dma_start(out=dbg["gam"], in_=gam[:]), waits=[("ac", ac_gam)],
          inc="q_dbg", incval=16)
    esA0.close()
    esA0 = ExitStack()
    scope[0] = esA0

    ac_tr_hold = [0]
    wp_hold = {0: 0, 1: 0}
    v4_hold = {0: 0, 1: 0}
    pj_f = [sb("pj_f0", [128, FF])] * 2
    pj_t = [sb("pj_t0", [128, FF], BF16)] * 2
    pj_s = sb("pj_s", [128, FF // 2], BF16)
    wpt = [sb(f"wpt{i}", [128, 128], BF16) for i in range(2)]
    wps = [sb(f"wps{i}", [128, 128], BF16) for i in range(2)]
    trn = pb[4].bitcast(BF16)
    c10 = [pb[5], pb[6]]
    c00 = [pb[7], pb[3]]
    qg_hold = [0]
    for dchunk in range(D // 128):
        i = dchunk % 2
        pe_gate = M("pe")
        qv = E("sync", lambda e, d=dchunk, i=i: e.dma_start(
            out=pj_f[i][:], in_=Wpj[d * 128:(d + 1) * 128, :]),
            waits=[("dv", M("dv"))], inc=f"q_pj{i}", incval=16)
        for hf in range(2):
            c0_, c1_ = hf * (FF // 2), (hf + 1) * (FF // 2)
            v1 = E("vector", lambda e, i=i, c0_=c0_, c1_=c1_: e.tensor_scalar(
                out=pj_s[:], in0=pj_f[i][:, c0_:c1_], scalar1=gam[:, 1:2], scalar2=0.5,
                op0=ALU.mult, op1=ALU.is_gt),
                waits=[(f"q_pj{i}", qv), ("ac", ac_gam), ("dv", M("dv"))], inc="dv")
            v2 = E("vector", lambda e, i=i, c0_=c0_, c1_=c1_: e.tensor_scalar(
                out=pj_t[i][:, c0_:c1_], in0=pj_f[i][:, c0_:c1_], scalar1=gam[:, 1:2], scalar2=-0.5,
                op0=ALU.mult, op1=ALU.is_lt),
                waits=[(f"q_pj{i}", qv), ("pe", pe_gate)], inc="dv")
            v3 = E("vector", lambda e, i=i, c0_=c0_, c1_=c1_: e.tensor_sub(
                pj_t[i][:, c0_:c1_], pj_s[:], pj_t[i][:, c0_:c1_]), waits=[("dv", v2)], inc="dv")
        for hc in range(HCN):
            j = hc % 2
            pv = E("tensor", lambda e, i=i, hc=hc, j=j: e.transpose(
                trn[:, j * 128:(j + 1) * 128], pj_t[i][:, hc * 128:(hc + 1) * 128], identb[:]),
                waits=[("dv", v3), ("ac", ac_tr_hold[0]), ("dv", dv_ln)], inc="pe")
            av = E("scalar", lambda e, j=j: e.copy(out=wpt[j][:], in_=trn[:, j * 128:(j + 1) * 128]),
                   waits=[("pe", pv), ("dv", v4_hold[j])], inc="ac")
            if hc >= 1:
                ac_tr_hold[0] = av - 1
            E("tensor", lambda e, j=j, hc=hc, d=dchunk: e.matmul(
                c10[d // 4][0:1, (d % 4) * 128:(d % 4 + 1) * 128], lnw_bf[:, hc:hc + 1], wpt[j][:],
                start=(hc == 0), stop=(hc == HCN - 1)), waits=[("ac", av)], inc="pe")
            E("tensor", lambda e, j=j, hc=hc, d=dchunk: e.matmul(
                c00[d // 4][0:1, (d % 4) * 128:(d % 4 + 1) * 128], lnb_bf[:, hc:hc + 1], wpt[j][:],
                start=(hc == 0), stop=(hc == HCN - 1)), inc="pe")
            v4 = E("vector", lambda e, j=j, hc=hc: e.tensor_scalar_mul(
                wps[j][:], wpt[j][:], lnwc[:, hc:hc + 1]),
                waits=[("ac", av), (f"q_wp{j}", wp_hold[j])], inc="dv")
            v4_hold[j] = v4
            qq = E("gpsimd", lambda e, j=j, hc=hc, d=dchunk: e.dma_start(
                out=wp_dram[hc, :, d * 128:(d + 1) * 128], in_=wps[j][:]),
                waits=[("dv", v4)], inc=f"q_wp{j}", incval=16)
            wp_hold[j] = qq
    pe_c = M("pe")
    av = E("scalar", lambda e: e.copy(out=c1row[0:1, 0:512], in_=c10[0][0:1, 0:512]),
           waits=[("pe", pe_c)], inc="ac")
    av = E("scalar", lambda e: e.copy(out=c1row[0:1, 512:1024], in_=c10[1][0:1, 0:512]), inc="ac")
    av = E("scalar", lambda e: e.copy(out=c0row[0:1, 0:512], in_=c00[0][0:1, 0:512]), inc="ac")
    av = E("scalar", lambda e: e.copy(out=c0row[0:1, 512:1024], in_=c00[1][0:1, 0:512]), inc="ac")
    ac_c10 = M("ac")
    if debug:
        cc1 = sb("cc1", [1, D])
        cc0 = sb("cc0", [1, D])
        E("vector", lambda e: e.tensor_copy(out=cc1[:], in_=c1row[:]), waits=[("ac", ac_c10)], inc="dv")
        E("vector", lambda e: e.tensor_copy(out=cc0[:], in_=c0row[:]), waits=[("ac", ac_c10)], inc="dv")
        E("sync", lambda e: e.dma_start(out=dbg["c1"], in_=cc1[:]), waits=[("dv", M("dv"))], inc="q_dbg", incval=16)
        E("sync", lambda e: e.dma_start(out=dbg["c0"], in_=cc0[:]), waits=[("dv", M("dv"))], inc="q_dbg", incval=16)
        E("sync", lambda e: e.dma_start(out=dbg["wp"], in_=wp_dram), waits=[("q_wp0", M("q_wp0")), ("q_wp1", M("q_wp1"))], inc="q_dbg", incval=16)

    esA0.close()
    scope[0] = esA
    xtile = [sb(f"xtile{i}", [128, D]) for i in range(2)]
    xnb = sb("xnb", [128, D], BF16)
    lncol = sb("lncol", [128, 8])
    x2s = sb("x2s", [128, D], BF16)
    xnT = [sb(f"xnT{i}", [128, 8 * 512], BF16) for i in range(2)]
    tabs = sb("tabs", [128, 2 * HP * 512], BF16)
    t1s = [tabs[:, 0:HP * 512]] * 2
    t2s = [tabs[:, HP * 512:2 * HP * 512]] * 2
    keo = sb("keo", [128, 512], BF16)
    m1 = sb("m1", [128, 512], BF16)
    m2 = sb("m2", [128, 512], BF16)
    kst = [sb(f"kst{i}", [128, 512], BF16) for i in range(2)]
    k2t = sb("k2t", [128, 512], BF16)
    vraw = [sb("vraw0", [128, 4 * VROW], BF16)] * 2
    tr_ps = pb[0].bitcast(BF16)
    eo_ps2 = [pb[1], pb[7]]
    ro_ps = pb[2]
    v_ps = [pb[3], pb[4], pb[5]]
    u_ps = pb[6]

    def ln_and_transpose(x_src, tt, xnT_buf, marks, tw, key):
        for sub in range(tw // 128):
            yield
            i = sub % 2
            r0 = tt * tw + sub * 128
            qv = E("sync", lambda e, i=i, r0=r0, x_src=x_src: e.dma_start(
                out=xtile[i][:], in_=x_src[r0:r0 + 128, :]),
                waits=[("ac", marks.get(f"x_reuse{i}", 0))], inc=f"q_x{i}", incval=16)
            av0 = E("scalar", lambda e, i=i: e.activation(
                out=x2s[:], in_=xtile[i][:], func=AF.Identity, accum_out=lncol[:, 0:1]),
                waits=[(f"q_x{i}", qv), ("dv", marks.get("x2_reuse", 0))], inc="ac")
            dv0 = E("vector", lambda e, i=i: e.scalar_tensor_tensor(
                out=x2s[:], in0=xtile[i][:], scalar=1.0, in1=xtile[i][:],
                op0=ALU.mult, op1=ALU.mult, accum_out=lncol[:, 1:2]),
                waits=[("ac", av0)], inc="dv")
            marks["x2_reuse"] = dv0
            dv1 = E("vector", lambda e: e.tensor_scalar_mul(lncol[:, 2:3], lncol[:, 0:1], 1.0 / D),
                    waits=[("dv", dv0)], inc="dv")
            dv2 = E("vector", lambda e: e.scalar_tensor_tensor(
                out=lncol[:, 3:4], in0=lncol[:, 2:3], scalar=-1.0, in1=lncol[:, 2:3],
                op0=ALU.mult, op1=ALU.mult), waits=[("dv", dv1)], inc="dv")
            dv3 = E("vector", lambda e: e.scalar_tensor_tensor(
                out=lncol[:, 3:4], in0=lncol[:, 1:2], scalar=1.0 / D, in1=lncol[:, 3:4],
                op0=ALU.mult, op1=ALU.add), waits=[("dv", dv2)], inc="dv")
            av1 = E("scalar", lambda e: e.activation(out=lncol[:, 4:5], in_=lncol[:, 3:4],
                                                     func=AF.Ln, bias=epscol[:]), waits=[("dv", dv3)], inc="ac")
            av2 = E("scalar", lambda e: e.activation(out=lncol[:, 4:5], in_=lncol[:, 4:5],
                                                     func=AF.Exp, scale=-0.5), waits=[("ac", av1)], inc="ac")
            dv4 = E("vector", lambda e: e.scalar_tensor_tensor(
                out=lncol[:, 5:6], in0=lncol[:, 2:3], scalar=-1.0, in1=lncol[:, 4:5],
                op0=ALU.mult, op1=ALU.mult), waits=[("ac", av2)], inc="dv")
            av3 = E("scalar", lambda e, i=i: e.activation(
                out=xnb[:], in_=xtile[i][:], func=AF.Identity,
                scale=lncol[:, 4:5], bias=lncol[:, 5:6]), waits=[("dv", dv4)], inc="ac")
            marks[f"x_reuse{i}"] = av3
            for fc in range(8):
                pv = E("tensor", lambda e, fc=fc: e.transpose(
                    tr_ps[:, fc * 128:(fc + 1) * 128], xnb[:, fc * 128:(fc + 1) * 128], identb[:]),
                    waits=[("ac", av3)] + ([("ac", marks.get("tr_evac", 0))] if fc == 0 else []),
                    inc="pe")
            av4 = E("scalar", lambda e, sub=sub, xnT_buf=xnT_buf, tw=tw: e.copy(
                out=xnT_buf.rearrange("p (f t) -> p f t", f=8)[:, :, sub * 128:(sub + 1) * 128],
                in_=tr_ps[:].rearrange("p (f t) -> p f t", f=8)),
                waits=[("pe", pv), ("pe", marks.get(key + "_pefree", 0)),
                       ("dv", marks.get(key + "_dvfree", 0))], inc="ac")
            marks["tr_evac"] = av4
            marks[key] = av4

    def rope_project(xnT_buf, W_sb, bias_c, T1buf, T2buf, dst_kind, tt, tw, marks, dbg_norm=None):
        def emit_proj(hp):
            ei = hp % 2
            for fc in range(8):
                E("tensor", lambda e, hp=hp, fc=fc, tw=tw, ei=ei: e.matmul(
                    eo_ps2[ei][:, 0:tw], W_sb[:, fc * D + hp * 128: fc * D + (hp + 1) * 128],
                    xnT_buf[:, fc * tw: fc * tw + tw], start=(fc == 0), stop=(fc == 7)),
                    waits=([("ac", marks.get(f"keo_reuse{ei}", 0)), ("ac", marks.get("tr_evac", 0))]
                           + ([("q_onceg", qg_w_cell[0])] if hp == 0 else []) if fc == 0 else []),
                    inc="pe")
            return M("pe")

        pmark = {0: emit_proj(0)}
        filler = marks.pop("filler", None)
        for hp in range(HP):
            ei = hp % 2
            if hp + 1 < HP:
                pmark[hp + 1] = emit_proj(hp + 1)
            if filler is not None:
                filler(hp)
            av = E("scalar", lambda e, hp=hp, tw=tw, bias_c=bias_c, ei=ei: e.activation(
                out=keo[:, 0:tw], in_=eo_ps2[ei][:, 0:tw], func=AF.Identity,
                bias=bias_c[:, hp:hp + 1]),
                waits=[("pe", pmark[hp]), ("dv", marks.get("keo_dv", 0))], inc="ac")
            marks[f"keo_reuse{ei}"] = av
            d1 = E("vector", lambda e, hp=hp, tw=tw, T1buf=T1buf: e.tensor_mul(
                m1[:, 0:tw], keo[:, 0:tw], T1buf[:, hp * tw: hp * tw + tw]),
                waits=[("ac", av), ("pe", marks.get("m_reuse", 0)),
                       _rope_tab_gate[0]], inc="dv")
            d2 = E("vector", lambda e, hp=hp, tw=tw, T2buf=T2buf: e.tensor_mul(
                m2[:, 0:tw], keo[:, 0:tw], T2buf[:, hp * tw: hp * tw + tw]), inc="dv")
            marks["keo_dv"] = d2
            pv1 = E("tensor", lambda e, tw=tw: e.matmul(
                ro_ps[0:64, 0:tw], imat[:], m1[:, 0:tw], start=True, stop=True),
                waits=[("dv", d2), ("ac", marks.get("ro_reuse", 0))], inc="pe")
            pv2 = E("tensor", lambda e, tw=tw: e.matmul(
                ro_ps[64:128, 0:tw], imat[:], m2[:, 0:tw], start=True, stop=True,
                tile_position=(0, 64)), inc="pe")
            marks["m_reuse"] = pv2
            if dst_kind[0] == 'K':
                i = hp % 2
                av2 = E("scalar", lambda e, i=i, tw=tw: e.copy(out=kst[i][:, 0:tw], in_=ro_ps[:, 0:tw]),
                        waits=[("pe", pv2), (f"q_ko{i}", marks.get(f"kst_reuse{i}", 0))], inc="ac")
                marks["ro_reuse"] = av2
                qv = E("sync", lambda e, i=i, hp=hp, tt=tt, tw=tw: e.dma_start(
                    out=K_dram[hp, :, tt * tw:(tt + 1) * tw], in_=kst[i][:, 0:tw]),
                    waits=[("ac", av2)], inc=f"q_ko{i}", incval=16)
                marks[f"kst_reuse{i}"] = qv
                ksrc = kst[i]
            else:
                av2 = E("scalar", lambda e, hp=hp, tt=tt, tw=tw: e.copy(
                    out=Q_sb[:, hp * OWN + tt * tw: hp * OWN + (tt + 1) * tw], in_=ro_ps[:, 0:tw]),
                    waits=[("pe", pv2)], inc="ac")
                marks["ro_reuse"] = av2
                ksrc = None
            src = ksrc[:, 0:tw] if ksrc is not None else Q_sb[:, hp * OWN + tt * tw: hp * OWN + (tt + 1) * tw]
            d3 = E("vector", lambda e, src=src, tw=tw: e.tensor_mul(k2t[:, 0:tw], src, src),
                   waits=[("ac", av2), ("pe", marks.get("k2_reuse", 0))], inc="dv")
            pu = E("tensor", lambda e, hp=hp, tw=tw: e.matmul(
                u_ps[0:16, 0:tw], ind16[:, hp * 16:(hp + 1) * 16], k2t[:, 0:tw],
                start=(hp == 0), stop=(hp == HP - 1)),
                waits=[("dv", d3)] + ([("dv", marks.get("u_dv", 0))] if hp == 0 else []), inc="pe")
            marks["k2_reuse"] = pu
        ucol = 0 if dst_kind[0] == 'K' else 1
        dm = E("vector", lambda e, tw=tw: e.reduce_max(uacc[:, 3:4], u_ps[0:16, 0:tw], axis=AXX),
               waits=[("pe", M("pe"))], inc="dv")
        dm = E("vector", lambda e, ucol=ucol: e.tensor_max(uacc[:, ucol:ucol + 1],
                                                           uacc[:, ucol:ucol + 1], uacc[:, 3:4]),
               waits=[("dv", dm)], inc="dv")
        marks["u_dv"] = dm
        return marks

    def v_build_sub(xnT_buf, tt, tw, marks, subs):
        i = 0
        segs = [(0, 396, 0, 3), (396, 792, 3, 6), (792, 1056, 6, 8)]
        for sub in subs:
            for fc in range(8):
                for bi, (c0, c1, h0, h1) in enumerate(segs):
                    E("tensor", lambda e, fc=fc, bi=bi, c0=c0, c1=c1, sub=sub, tw=tw: e.matmul(
                        v_ps[bi][:, 0:c1 - c0],
                        xnT_buf[:, fc * tw + sub * 128: fc * tw + (sub + 1) * 128],
                        Wv_sb[:, fc * VROW + c0: fc * VROW + c1],
                        start=(fc == 0), stop=False),
                        waits=([("ac", marks.get("vps_reuse", 0)),
                                ("ac", marks.get("tr_evac", 0))] if fc == 0 and bi == 0 else []),
                        inc="pe")
            for bi, (c0, c1, h0, h1) in enumerate(segs):
                E("tensor", lambda e, bi=bi, c0=c0, c1=c1: e.matmul(
                    v_ps[bi][:, 0:c1 - c0], ones_bf[0:1, :], bvrow[0:1, c0:c1],
                    start=False, stop=True), inc="pe")
            pv = M("pe")
            for bi, (c0, c1, h0, h1) in enumerate(segs):
                E("scalar", lambda e, bi=bi, c0=c0, c1=c1, h0=h0, h1=h1, sub=sub, i=i: e.copy(
                    out=vraw[i][:].rearrange("p (h s c) -> p h s c", h=HP, c=VW)[:, h0:h1, sub, :],
                    in_=v_ps[bi][:, 0:c1 - c0].rearrange("p (h c) -> p h c", c=VW)),
                    waits=[("pe", pv), (f"q_vo{i}", marks.get(f"vraw_q{i}", 0))], inc="ac")
            marks["vps_reuse"] = M("ac")

    def v_build_out(tt, tw, marks):
        i = 0
        av = M("ac")
        nsub = tw // 128
        for hp in range(HP):
            qv = E("gpsimd", lambda e, hp=hp, tt=tt, nsub=nsub, i=i: e.dma_start(
                out=V_dram[hp, :, tt * nsub * VW:(tt + 1) * nsub * VW],
                in_=vraw[i][:, hp * 4 * VW:(hp * 4 + nsub) * VW]),
                waits=[("ac", av)], inc=f"q_vo{i}", incval=16)
        marks[f"vraw_q{i}"] = M(f"q_vo{i}")
        return M("qs")

    P.fence()
    marks = {"tr_evac": M("ac"), "keo_reuse0": M("ac"), "keo_reuse1": M("ac"),
             "ro_reuse": M("ac"),
             "vps_reuse": M("ac"), "x2_reuse": M("dv"), "u_dv": M("dv")}

    def emit_tables(tt, src1, src2, tw):
        q1 = E("scalar", lambda e, tt=tt, src1=src1: e.dma_start(
            out=tabs[:], in_=src1[tt]),
            waits=[("dv", marks.get("tab_reuse", 0))], inc="q_t0", incval=16)
        marks["tab_q"] = ("q_t0", M("q_t0"))

    gen = ln_and_transpose(x_kv, 0, xnT[0], marks, 512, "lnk0")
    for _ in gen:
        pass
    for tt in range(NT1):
        i = tt % 2
        ni = (tt + 1) % 2
        emit_tables(tt, Tk, None, 512)
        _rope_tab_gate[0] = marks["tab_q"]
        if tt + 1 < NT1:
            marks[f"lnk{tt+1}_pefree"] = M("pe")
            nxt = ln_and_transpose(x_kv, tt + 1, xnT[ni], marks, 512, f"lnk{tt+1}")
        else:
            nxt = iter(())

        def filler(hp, tt=tt, nxt=nxt):
            if hp % 2 == 1:
                v_build_sub(xnT[tt % 2][:], tt, 512, marks, [(hp - 1) // 2])
            next(nxt, None)

        marks["filler"] = filler
        rope_project(xnT[i][:], Wk_sb, bkc, t1s[0][:], t2s[0][:], ('K', tt), tt, 512, marks)
        marks["tab_reuse"] = M("dv")
        for _ in nxt:
            pass
        v_build_out(tt, 512, marks)
    p1_done_q = M("qs")
    p1_done_pe = M("pe")

    E("gpsimd", lambda e: e.dma_start(
        out=Wk_sb[:].rearrange("p (c n) -> p c n", c=8),
        in_=Wq_p.rearrange("(c p) n -> p c n", p=128)),
        waits=[("pe", p1_done_pe)], inc="q_onceg", incval=16)
    qg_w_cell[0] = M("q_onceg")
    QW = min(512, OWN)
    for tt in range(OWN // QW):
        i = tt % 2
        emit_tables(tt, Tq, None, QW)
        _rope_tab_gate[0] = marks["tab_q"]
        marks[f"lnq{tt}_pefree"] = M("pe")
        for _ in ln_and_transpose(x_own, tt, xnT[i], marks, QW, f"lnq{tt}"):
            pass
        av = M("ac")
        for fc in range(8):
            E("vector", lambda e, fc=fc, tt=tt, i=i, QW=QW: e.tensor_copy(
                out=xn_sb[:, fc * OWN + tt * QW: fc * OWN + (tt + 1) * QW],
                in_=xnT[i][:, fc * QW:(fc + 1) * QW]), waits=([("ac", av)] if fc == 0 else []),
                inc="dv")
        marks[f"lnq{tt + 2}_dvfree"] = M("dv")
        rope_project(xnT[i][:], Wk_sb, bqc, t1s[0][:], t2s[0][:], ('Q', tt), tt, QW, marks)
        marks["tab_reuse"] = M("dv")
    p1b_done = M("ac")

    P.fence()
    d0 = E("vector", lambda e: e.tensor_mul(uacc[:, 2:3], uacc[:, 0:1], uacc[:, 1:2]),
           waits=[("dv", M("dv")), ("ac", p1b_done)], inc="dv")
    du = E("vector", lambda e: e.tensor_copy(out=urow_sc[:, 0:1], in_=uacc[:, 2:3]),
           waits=[("dv", d0)], inc="dv")
    pu = E("tensor", lambda e: e.transpose(pb[7][0:16, 0:16], urow_sc[0:16, 0:16],
                                           identf[0:16, 0:16]),
           waits=[("dv", du)], inc="pe")
    av = E("scalar", lambda e: e.copy(out=urow_r[0:1, 0:16], in_=pb[7][0:1, 0:16]),
           waits=[("pe", pu)], inc="ac")
    dmx = E("vector", lambda e: e.tensor_max(urow_r[0:1, 0:8], urow_r[0:1, 0:8],
                                             urow_r[0:1, 8:16]), waits=[("ac", av)], inc="dv")
    pu2 = E("tensor", lambda e: e.matmul(pb[7][:, 0:8], ones_f[0:1, :], urow_r[0:1, 0:8],
                                         start=True, stop=True), waits=[("dv", dmx)], inc="pe")
    av2 = E("scalar", lambda e: e.activation(out=bias16[:], in_=pb[7][:, 0:8], func=AF.Ln),
           waits=[("pe", pu2)], inc="ac")
    av3 = E("scalar", lambda e: e.activation(out=bias16[:], in_=bias16[:], func=AF.Exp, scale=0.5),
            waits=[("ac", av2)], inc="ac")
    d1 = E("vector", lambda e: e.tensor_scalar_mul(bias16[:], bias16[:], -0.125),
          waits=[("ac", av3)], inc="dv")
    pu3 = E("tensor", lambda e: e.matmul(pb[6][:, 0:4 * NAMB], ones_f[0:1, :], pflag[0:1, :],
                                         start=True, stop=True), inc="pe")
    av4 = E("scalar", lambda e: e.copy(out=pfcol[:], in_=pb[6][:, 0:4 * NAMB]),
           waits=[("pe", pu3)], inc="ac")
    for a in range(4 * NAMB):
        d2 = E("vector", lambda e, a=a: e.tensor_scalar_add(
            biasamb[:, a * 8:(a + 1) * 8], bias16[:], pfcol[:, a:a + 1]),
            waits=[("dv", d1), ("ac", av4)], inc="dv")
    u_done_dv = M("dv")
    if debug:
        E("sync", lambda e: e.dma_start(out=dbg["b16"], in_=bias16[:]), waits=[("dv", u_done_dv)],
          inc="q_dbg", incval=16)

    P.fence()
    esA.close()
    esB = ExitStack()
    scope[0] = esB
    KEYS = lambda s: (s + 1) * 4 * QTW
    Kbuf = [sb(f"Kbuf{j}", [128, 4 * QTW * 4], BF16) for j in range(2)]
    Vbuf = [sb(f"Vbuf{j}", [128, (4 * QTW * 4 // 128) * VW], BF16) for j in range(2)]
    Kd = [sb(f"Kd{j}", [128, NK * 128], BF16) for j in range(2)]
    Vd = [sb(f"Vd{j}", [128, NK * VW], BF16) for j in range(2)]
    Pbuf = [sb(f"Pbuf{j}", [128, 2 * QTW], BF16) for j in range(2)]
    rsc = sb("rsc", [128, 4 * QTW], BF16)
    ctmp = sb("ctmp", [128, 4 * QTW], BF16)
    s2 = [pd[0], pd[1]]
    cA, cB = pb[4], pb[5]
    dvl, dvh = pb[6], pb[7]

    def p3_loads(u, marks):
        s, hp = divmod(u, HP)
        j = u % 2
        kq = E("sync", lambda e, s=s, hp=hp, j=j: e.dma_start(
            out=Kbuf[j][:, 0:KEYS(s)], in_=K_dram[hp, :, 0:KEYS(s)]),
            waits=[("ac", marks.get(f"kbuf_free{j}", 0))] +
                  ([("q_ko0", M("q_ko0")), ("q_ko1", M("q_ko1"))] if (s, hp) == (0, 0) else []),
            inc=f"q_K{j}", incval=16)
        vq = E("scalar", lambda e, s=s, hp=hp, j=j: e.dma_start(
            out=Vbuf[j][:, 0:(KEYS(s) // 128) * VW], in_=V_dram[hp, :, 0:(KEYS(s) // 128) * VW]),
            waits=([("q_vo0", M("q_vo0")), ("q_vo1", M("q_vo1"))] if (s, hp) == (0, 0) else []),
            inc=f"q_V{j}", incval=16)
        col = s * HP + hp
        kdq = E("gpsimd", lambda e, col=col, j=j: e.indirect_dma_start(
            out=Kd[j][:], out_offset=None, in_=K_dram.rearrange("h p s -> (h p) s"),
            in_offset=IndirectOffsetOnAxis(ap=kidx[:, col:col + 1], axis=1)),
            waits=[("ac", marks.get(f"kbuf_free{j}", 0))], inc=f"q_d{j}", incval=16)
        vdq = E("gpsimd", lambda e, col=col, j=j: e.indirect_dma_start(
            out=Vd[j][:], out_offset=None, in_=V_dram.rearrange("h p s -> (h p) s"),
            in_offset=IndirectOffsetOnAxis(ap=vidx[:, col:col + 1], axis=1)),
            inc=f"q_d{j}", incval=16)
        marks[f"kload{j}"] = (kq, vq, kdq, vdq)

    NUNITS = 4 * HP
    m3 = {"exp": {}, "pv": {}, "ctx_free": 0, "dv_free": 0}
    p3_loads(0, m3)
    for u in range(NUNITS):
        s, hp = divmod(u, HP)
        j = u % 2
        if u + 1 < NUNITS:
            p3_loads(u + 1, m3)
        kq, vq, kdq, vdq = m3[f"kload{j}"]
        qcol = hp * OWN + s * QTW
        ntile = NK + (4 * s + 3) * NK

        def src_of(kc, j=j):
            if kc < NK:
                return (lambda kc=kc: Kd[j][0:64, kc * 128:(kc + 1) * 128],
                        lambda kc=kc: Kd[j][64:128, kc * 128:(kc + 1) * 128],
                        kc * VW, Vd[j])
            ki = kc - NK
            return (lambda ki=ki: Kbuf[j][0:64, ki * 128:(ki + 1) * 128],
                    lambda ki=ki: Kbuf[j][64:128, ki * 128:(ki + 1) * 128],
                    ki * VW, Vbuf[j])

        def emit_S(kc):
            it = kc % 2
            klo, khi, _, _ = src_of(kc)
            w_s = [("ac", m3["exp"].get((u, kc - 2), m3["exp"].get("prev_last", 0) if kc < 2 else 0))]
            if kc == 0:
                w_s += [(f"q_K{j}", kq), (f"q_d{j}", vdq), ("dv", m3["dv_free"])]
            E("tensor", lambda e, klo=klo, it=it, qcol=qcol: e.matmul(
                s2[it][:, 0:QTW], klo(), Q_sb[0:64, qcol:qcol + QTW], start=True, stop=True),
                waits=w_s, inc="pe")
            return E("tensor", lambda e, khi=khi, it=it, qcol=qcol: e.matmul(
                s2[it][:, QTW:2 * QTW], khi(), Q_sb[64:128, qcol:qcol + QTW], start=True, stop=True,
                tile_position=(64, 0)), inc="pe")

        m3["smark"] = {0: emit_S(0)}
        for kc in range(ntile):
            it = kc % 2
            diag = kc < NK
            _, _, vcb, vsrc = src_of(kc)
            if kc + 1 < ntile:
                m3["smark"][kc + 1] = emit_S(kc + 1)
            if diag or (kc - NK) < 4 * s * NK:
                blo = lambda hp=hp: bias16[:, hp:hp + 1]
            else:
                a = s * NAMB + (kc - NK - 4 * s * NK)
                blo = lambda a=a, hp=hp: biasamb[:, a * 8 + hp:a * 8 + hp + 1]
            wexp = [("pe", m3["smark"][kc])]
            if (u, kc - 2) in m3["pv"]:
                wexp.append(("pe", m3["pv"][(u, kc - 2)]))
            e2 = E("scalar", lambda e, it=it, blo=blo: e.activation(
                out=Pbuf[it][:, 0:2 * QTW], in_=s2[it][:, 0:2 * QTW], func=AF.Exp, scale=0.125,
                bias=blo()), waits=wexp, inc="ac")
            m3["exp"][(u, kc)] = e2
            wpv = [("ac", e2)]
            if diag:
                g1 = E("gpsimd", lambda e, it=it, kc=kc: e.affine_select(
                    out=Pbuf[it][:, 0:QTW], in_=Pbuf[it][:, 0:QTW], compare_op=ALU.is_ge,
                    fill=0.0, base=-(kc * 128), pattern=[[1, QTW]], channel_multiplier=-1),
                    waits=[("ac", e2)], inc="gp")
                g2 = E("gpsimd", lambda e, it=it, kc=kc: e.affine_select(
                    out=Pbuf[it][:, QTW:2 * QTW], in_=Pbuf[it][:, QTW:2 * QTW], compare_op=ALU.is_ge,
                    fill=0.0, base=-(kc * 128), pattern=[[1, QTW]], channel_multiplier=-1),
                    inc="gp")
                wpv = [("gp", g2)]
            if kc == 0:
                wpv.append((f"q_V{j}", vq))
                wpv.append(("dv", m3["ctx_free"]))
            p1_ = E("tensor", lambda e, vsrc=vsrc, vcb=vcb, it=it, kc=kc, ntile=ntile: e.matmul(
                cA[0:65, 0:QTW], vsrc[:, vcb:vcb + 65], Pbuf[it][:, 0:QTW],
                start=(kc == 0), stop=(kc == ntile - 1)), waits=wpv, inc="pe")
            p2_ = E("tensor", lambda e, vsrc=vsrc, vcb=vcb, it=it, kc=kc, ntile=ntile: e.matmul(
                cB[0:65, 0:QTW], vsrc[:, vcb + 65:vcb + 130], Pbuf[it][:, QTW:2 * QTW],
                start=(kc == 0), stop=(kc == ntile - 1)), inc="pe")
            m3["pv"][(u, kc)] = p2_
        m3["exp"]["prev_last"] = m3["exp"][(u, ntile - 1)]
        pv_done = m3["pv"][(u, ntile - 1)]
        a1 = E("scalar", lambda e: e.activation(out=rsc[64:65, 0:QTW], in_=cA[64:65, 0:QTW],
                                                func=AF.Ln), waits=[("pe", pv_done)], inc="ac")
        a2 = E("scalar", lambda e: e.activation(out=rsc[64:65, QTW:2 * QTW], in_=rsc[64:65, 0:QTW],
                                                func=AF.Exp, scale=-1.0), waits=[("ac", a1)], inc="ac")
        a3 = E("scalar", lambda e: e.activation(out=rsc[64:65, 2 * QTW:3 * QTW], in_=cB[64:65, 0:QTW],
                                                func=AF.Ln), waits=[("pe", pv_done)], inc="ac")
        a4 = E("scalar", lambda e: e.activation(out=rsc[64:65, 3 * QTW:4 * QTW], in_=rsc[64:65, 2 * QTW:3 * QTW],
                                                func=AF.Exp, scale=-1.0), waits=[("ac", a3)], inc="ac")
        pdl = E("tensor", lambda e: e.matmul(dvl[0:64, 0:QTW], ones_bf[64:65, 0:64],
                                             rsc[64:65, QTW:2 * QTW], start=True, stop=True,
                                             tile_position=(64, 0)),
                waits=[("ac", a2), ("dv", m3["dv_free"])], inc="pe")
        pdh = E("tensor", lambda e: e.matmul(dvh[0:64, 0:QTW], ones_bf[64:65, 0:64],
                                             rsc[64:65, 3 * QTW:4 * QTW], start=True, stop=True,
                                             tile_position=(64, 0)), waits=[("ac", a4)], inc="pe")
        d1 = E("vector", lambda e: e.tensor_copy(out=ctmp[0:64, 0:QTW], in_=cA[0:64, 0:QTW]),
               waits=[("pe", pdl)], inc="dv")
        d2 = E("vector", lambda e, qcol=qcol: e.tensor_mul(
            ctx_sb[0:64, qcol:qcol + QTW], ctmp[0:64, 0:QTW], dvl[0:64, 0:QTW]),
            waits=[("dv", d1)], inc="dv")
        d3 = E("vector", lambda e: e.tensor_copy(out=ctmp[0:64, QTW:2 * QTW], in_=cB[0:64, 0:QTW]),
               waits=[("pe", pdh)], inc="dv")
        d4 = E("vector", lambda e: e.tensor_mul(
            ctmp[0:64, 2 * QTW:3 * QTW], ctmp[0:64, QTW:2 * QTW], dvh[0:64, 0:QTW]),
            waits=[("dv", d3)], inc="dv")
        m3["ctx_free"] = d4
        ib = (ntile) % 2
        pid = E("tensor", lambda e, ib=ib: e.matmul(
            s2[ib][64:128, 0:QTW], ident64[:], ctmp[0:64, 2 * QTW:3 * QTW],
            start=True, stop=True, tile_position=(0, 64)),
            waits=[("dv", d4), ("ac", m3["exp"]["prev_last"])], inc="pe")
        a5 = E("scalar", lambda e, ib=ib, qcol=qcol: e.copy(
            out=ctx_sb[64:128, qcol:qcol + QTW], in_=s2[ib][64:128, 0:QTW]),
            waits=[("pe", pid)], inc="ac")
        m3["exp"]["prev_last"] = a5
        m3["dv_free"] = d4
        m3[f"kbuf_free{j}"] = a5
    p3_done_ac = M("ac")
    p3_done_dv = M("dv")
    if debug:
        E("sync", lambda e: e.dma_start(out=dbg["ctx"], in_=ctx_sb[:]),
          waits=[("ac", p3_done_ac), ("dv", p3_done_dv)], inc="q_dbg", incval=16)
        E("sync", lambda e: e.dma_start(out=dbg["Q"], in_=Q_sb[:]), inc="q_dbg", incval=16)
        E("sync", lambda e: e.dma_start(out=dbg["K"], in_=K_dram), inc="q_dbg", incval=16)
        E("sync", lambda e: e.dma_start(out=dbg["V"], in_=V_dram), inc="q_dbg", incval=16)

    P.fence()
    esB.close()
    esC = ExitStack()
    scope[0] = esC
    wfcT = sb("wfcT", [128, FF], BF16)
    wfcT2 = sb("wfcT2", [128, 7 * FF], BF16)
    wf = [sb("wf0", [128, D])] * 2
    wfs = sb("wfs", [128, D // 2], BF16)
    wft = [sb(f"wft{i}", [128, D], BF16) for i in range(2)]
    hbuf = [sb(f"hbuf{i}", [128, 512], BF16) for i in range(2)]
    h2buf = sb("h2buf", [128, 512], BF16)
    srow = sb("srow", [1, 1536])
    statmu = sb("statmu", [1, 512])
    statsv = sb("statsv", [1, 512])
    statrs = sb("statrs", [1, 512])
    statmu_b = sb("statmu_b", [1, OWN], BF16)
    statsv_b = sb("statsv_b", [1, OWN], BF16)

    def wfcT_tile(fc, hc):
        base = hc * 128
        if fc == 0:
            return wfcT[:, base:base + 128]
        return wfcT2[:, (fc - 1) * FF + base:(fc - 1) * FF + base + 128]


    trn5 = pb[0].bitcast(BF16)
    fcp = [pb[1], pb[2]]
    stp_s, stp_q = pb[3], pb[4]
    NT5 = OWN // 512
    m5 = {"wft_free": {0: 0, 1: 0}, "trn_free": 0, "hb_free": {0: 0, 1: 0}}
    for tt in range(NT5):
        i5 = tt % 2
        for hc in range(HCN):
            i = hc % 2
            if tt == 0:
                qv = E("sync", lambda e, hc=hc, i=i: e.dma_start(
                    out=wf[i][:], in_=Wfc[hc * 128:(hc + 1) * 128, :]),
                    waits=[("dv", M("dv"))], inc="q_wf0", incval=16)
                for hf in range(2):
                    c0_, c1_ = hf * (D // 2), (hf + 1) * (D // 2)
                    v1 = E("vector", lambda e, i=i, c0_=c0_, c1_=c1_: e.tensor_scalar(
                        out=wfs[:], in0=wf[i][:, c0_:c1_], scalar1=gam[:, 0:1], scalar2=0.5,
                        op0=ALU.mult, op1=ALU.is_gt),
                        waits=[("q_wf0", qv), ("dv", M("dv"))], inc="dv")
                    v2 = E("vector", lambda e, i=i, c0_=c0_, c1_=c1_: e.tensor_scalar(
                        out=wft[i][:, c0_:c1_], in0=wf[i][:, c0_:c1_], scalar1=gam[:, 0:1],
                        scalar2=-0.5, op0=ALU.mult, op1=ALU.is_lt), waits=[("q_wf0", qv)], inc="dv")
                    v3 = E("vector", lambda e, i=i, c0_=c0_, c1_=c1_: e.tensor_sub(
                        wft[i][:, c0_:c1_], wfs[:], wft[i][:, c0_:c1_]), waits=[("dv", v2)], inc="dv")
                m5["wft_free"][i] = v3
                pts = []
                for fc in range(8):
                    pt = E("tensor", lambda e, i=i, fc=fc: e.transpose(
                        trn5[:, fc * 128:(fc + 1) * 128], wft[i][:, fc * 128:(fc + 1) * 128],
                        identb[:]),
                        waits=([("dv", v3), ("ac", m5["trn_free"])] if fc == 0 else []), inc="pe")
                av = E("scalar", lambda e, hc=hc: e.copy(out=wfcT[:, hc * 128:(hc + 1) * 128],
                                                         in_=trn5[:, 0:128]),
                       waits=[("pe", M("pe"))], inc="ac")
                for fc in range(1, 8):
                    av = E("scalar", lambda e, hc=hc, fc=fc: e.copy(
                        out=wfcT2[:, (fc - 1) * FF + hc * 128:(fc - 1) * FF + hc * 128 + 128],
                        in_=trn5[:, fc * 128:(fc + 1) * 128]), inc="ac")
                m5["trn_free"] = av
                m5[f"wfcT_ready{hc}"] = av
            fi = hc % 2
            wfcq = [("ac", m5[f"wfcT_ready{hc}"])]
            for fc in range(8):
                E("tensor", lambda e, fc=fc, hc=hc, tt=tt, fi=fi: e.matmul(
                    fcp[fi][:, 0:512], wfcT_tile(fc, hc),
                    xn_sb[:, fc * OWN + tt * 512: fc * OWN + (tt + 1) * 512],
                    start=(fc == 0), stop=(fc == 7)),
                    waits=(wfcq + [("ac", m5[f"fcp_free{fi}"] if f"fcp_free{fi}" in m5 else 0)]) if fc == 0 else [],
                    inc="pe")
            pv = M("pe")
            hb = hc % 2
            a1 = E("scalar", lambda e, hb=hb, fi=fi: e.activation(
                out=hbuf[hb][:], in_=fcp[fi][:, 0:512], func=AF.Gelu),
                waits=[("pe", pv), (f"q_h{hb}", m5["hb_free"][hb]), ("dv", m5.get(f"h2_{hb}", 0))], inc="ac")
            m5[f"fcp_free{fi}"] = a1
            d1 = E("vector", lambda e, hb=hb: e.tensor_mul(h2buf[:], hbuf[hb][:], hbuf[hb][:]),
                   waits=[("ac", a1), ("pe", m5.get("st_pe", 0))], inc="dv")
            m5[f"h2_{hb}"] = d1
            E("tensor", lambda e, hc=hc, hb=hb: e.matmul(
                stp_s[0:1, 0:512], ones_bf[:, 0:1], hbuf[hb][:],
                start=(hc == 0), stop=(hc == HCN - 1)), waits=[("ac", a1)], inc="pe")
            E("tensor", lambda e, hc=hc: e.matmul(
                stp_q[0:1, 0:512], ones_bf[:, 0:1], h2buf[:],
                start=(hc == 0), stop=(hc == HCN - 1)), waits=[("dv", d1)], inc="pe")
            m5["st_pe"] = M("pe")
            qh = E("sync", lambda e, hc=hc, tt=tt, hb=hb: e.dma_start(
                out=h_dram[hc, :, tt * 512:(tt + 1) * 512], in_=hbuf[hb][:]),
                waits=[("ac", a1)], inc=f"q_h{hb}", incval=16)
            m5["hb_free"][hb] = qh
        pv = M("pe")
        a1 = E("scalar", lambda e, tt=tt: e.activation(
            out=statmu[0:1, 0:512], in_=stp_s[0:1, 0:512],
            func=AF.Identity, scale=-1.0 / FF),
            waits=[("pe", pv), ("dv", m5.get("stat_dv", 0))], inc="ac")
        d1 = E("vector", lambda e, tt=tt: e.scalar_tensor_tensor(
            out=srow[0:1, 0:512], in0=statmu[0:1, 0:512], scalar=1.0,
            in1=statmu[0:1, 0:512], op0=ALU.mult, op1=ALU.mult),
            waits=[("ac", a1)], inc="dv")
        d2 = E("vector", lambda e, tt=tt: e.scalar_tensor_tensor(
            out=srow[0:1, 512:1024], in0=stp_q[0:1, 0:512], scalar=1.0 / FF, in1=srow[0:1, 0:512],
            op0=ALU.mult, op1=ALU.subtract), waits=[("dv", d1), ("pe", pv)], inc="dv")
        a2 = E("scalar", lambda e: e.activation(out=srow[0:1, 1024:1536], in_=srow[0:1, 512:1024],
                                                func=AF.Ln, bias=epscol[0:1, 0:1]), waits=[("dv", d2)], inc="ac")
        a3 = E("scalar", lambda e, tt=tt: e.activation(
            out=statsv[0:1, 0:512], in_=srow[0:1, 1024:1536],
            func=AF.Exp, scale=0.5), waits=[("ac", a2)], inc="ac")
        a4 = E("scalar", lambda e, tt=tt: e.activation(
            out=statrs[0:1, 0:512], in_=srow[0:1, 1024:1536],
            func=AF.Exp, scale=-0.5),
            waits=[("ac", a2), ("q_rs", m5.get("rs_q", 0))], inc="ac")
        qrs = E("sync", lambda e, tt=tt: e.dma_start(
            out=statrs_dram[0:1, tt * 512:(tt + 1) * 512], in_=statrs[0:1, 0:512]),
            waits=[("ac", a4)], inc="q_rs", incval=16)
        m5["rs_q"] = qrs
        E("vector", lambda e, tt=tt: e.tensor_copy(
            out=statmu_b[0:1, tt * 512:(tt + 1) * 512], in_=statmu[0:1, 0:512]),
            waits=[("ac", a4)], inc="dv")
        E("vector", lambda e, tt=tt: e.tensor_copy(
            out=statsv_b[0:1, tt * 512:(tt + 1) * 512], in_=statsv[0:1, 0:512]),
            inc="dv")
        m5["stat_dv"] = M("dv")
        m5["st_pe_gate"] = M("ac")
    p5_done = M("dv")
    E("sync", lambda e: e.dma_start(out=statb_dram[0:1, :], in_=statmu_b[:]),
      waits=[("dv", p5_done)], inc="q_stat", incval=16)
    E("sync", lambda e: e.dma_start(out=statb_dram[1:2, :], in_=statsv_b[:]),
      inc="q_stat", incval=16)
    q_stat_done = M("q_stat")
    if debug:
        E("sync", lambda e: e.dma_start(out=dbg["h"], in_=h_dram),
          waits=[("q_h0", M("q_h0")), ("q_h1", M("q_h1")), ("dv", p5_done)],
          inc="q_dbg", incval=16)
        E("sync", lambda e: e.dma_start(out=dbg["stat"][0:1, :], in_=statmu[:]),
          waits=[("dv", p5_done), ("ac", M("ac"))], inc="q_dbg", incval=16)
        E("sync", lambda e: e.dma_start(out=dbg["stat"][1:2, :], in_=statsv[:]), inc="q_dbg", incval=16)
        E("sync", lambda e: e.dma_start(out=dbg["stat"][2:3, :], in_=statrs[:]), inc="q_dbg", incval=16)

    P.fence()
    esC.close()
    esD = ExitStack()
    scope[0] = esD
    wpp = sb("wpp", [128, HCN * D], BF16)
    borow = sb("borow", [1, D], BF16)
    statmu_b6 = sb("statmu_b6", [1, 512], BF16)
    statsv_b6 = sb("statsv_b6", [1, 512], BF16)
    statrs6 = sb("statrs6", [1, 512])
    wo_sb = sb("wo_sb", [128, 8 * D], BF16)
    hcol = [sb("hcol0", [128, HCN * 128], BF16)] * 2
    gbuf = sb("gbuf", [128, 512], BF16)
    rstdc = sb("rstdc", [128, 1])
    xf6 = [sb("xf60", [128, 512])] * 2
    out6 = sb("out6", [128, 512])
    E("gpsimd", lambda e: e.dma_start(out=borow[:], in_=bo[None, :]), inc="q_w6", incval=16)
    qw = E("gpsimd", lambda e: e.dma_start(out=wpp[:].rearrange("p (c n) -> p c n", c=HCN),
                                           in_=wp_dram.rearrange("c p n -> p c n")),
           waits=[("q_wp0", M("q_wp0")), ("q_wp1", M("q_wp1"))], inc="q_w6", incval=16)
    qw2 = E("gpsimd", lambda e: e.dma_start(out=wo_sb[:].rearrange("p (c n) -> p c n", c=8),
                                            in_=Wo_p.rearrange("(c p) n -> p c n", p=128)),
            inc="q_w6", incval=16)
    pjp = [pb[0], pb[1]]
    wop = [pb[2], pb[3]]
    rsp = pb[4]
    m6 = {}
    for tc in range(NT6):
        i = tc % 2
        qh = E("sync", lambda e, tc=tc, i=i: e.dma_start(
            out=hcol[i][:].rearrange("p (c t) -> p c t", c=HCN),
            in_=h_dram[:, :, tc * 128:(tc + 1) * 128].rearrange("c p t -> p c t")),
            waits=[("pe", m6.get(f"hcol_free{i}", 0))] +
                  ([("q_h0", M("q_h0")), ("q_h1", M("q_h1"))] if tc < 2 else []),
            inc=f"q_c{i}", incval=16)

        if tc % 4 == 0:
            stw = [("pe", M("pe"))] + ([("q_stat", q_stat_done)] if tc == 0 else [])
            qrs6 = E("sync", lambda e, tc=tc: e.dma_start(
                out=statrs6[0:1, 0:512], in_=statrs_dram[0:1, tc * 128:tc * 128 + 512]),
                waits=stw, inc="q_st6", incval=16)
            E("sync", lambda e, tc=tc: e.dma_start(
                out=statmu_b6[0:1, 0:512], in_=statb_dram[0:1, tc * 128:tc * 128 + 512]),
                inc="q_st6", incval=16)
            E("sync", lambda e, tc=tc: e.dma_start(
                out=statsv_b6[0:1, 0:512], in_=statb_dram[1:2, tc * 128:tc * 128 + 512]),
                inc="q_st6", incval=16)
            m6["rs6_q"] = M("q_st6")
        pr = E("tensor", lambda e, tc=tc: e.matmul(
            rsp[:, 0:1], statrs6[0:1, (tc % 4) * 128:(tc % 4 + 1) * 128], ones_f[0:1, 0:1],
            start=True, stop=True),
            waits=[("ac", m6.get("rst_free", 0)), ("q_st6", m6["rs6_q"])], inc="pe")
        m6["rsp_pe"] = pr
        ar = E("scalar", lambda e: e.copy(out=rstdc[:], in_=rsp[:, 0:1]), waits=[("pe", pr)], inc="ac")
        m6["rst_free"] = ar
        for half in range(2):
            for hc in range(HCN):
                E("tensor", lambda e, hc=hc, half=half, tc=tc, i=i: e.matmul(
                    pjp[half][:, 0:512], hcol[i][:, hc * 128:(hc + 1) * 128],
                    wpp[:, hc * D + half * 512:hc * D + (half + 1) * 512],
                    start=(hc == 0), stop=False),
                    waits=([(f"q_c{tc % 2}", qh), ("q_w6", 48), ("ac", m6.get(f"pj_free{half}", 0))]
                           if hc == 0 else []), inc="pe")
            E("tensor", lambda e, half=half, tc=tc: e.matmul(
                pjp[half][:, 0:512], statmu_b6[0:1, (tc % 4) * 128:(tc % 4 + 1) * 128],
                c1row[0:1, half * 512:(half + 1) * 512], start=False, stop=False),
                waits=[("q_st6", m6["rs6_q"])], inc="pe")
            E("tensor", lambda e, half=half, tc=tc: e.matmul(
                pjp[half][:, 0:512], statsv_b6[0:1, (tc % 4) * 128:(tc % 4 + 1) * 128],
                c0row[0:1, half * 512:(half + 1) * 512], start=False, stop=True), inc="pe")
            pv = M("pe")
            ag = E("scalar", lambda e, half=half: e.activation(
                out=gbuf[:, 0:512], in_=pjp[half][:, 0:512],
                func=AF.Silu, scale=rstdc[:]),
                waits=[("pe", pv), ("ac", ar), ("dv", m6.get("g_free", 0))], inc="ac")
            m6[f"pj_free{half}"] = ag
            for cc in range(8):
                E("tensor", lambda e, cc=cc, half=half, tc=tc: e.matmul(
                    wop[half][:, 0:512], ctx_sb[:, cc * OWN + tc * 128:cc * OWN + (tc + 1) * 128],
                    wo_sb[:, cc * D + half * 512:cc * D + (half + 1) * 512],
                    start=(cc == 0), stop=False),
                    waits=([("q_w6", 48), ("dv", m6.get(f"wo_free{half}", 0))] if cc == 0 else []),
                    inc="pe")
            E("tensor", lambda e, half=half: e.matmul(
                wop[half][:, 0:512], ones_bf[0:1, 0:128], borow[0:1, half * 512:(half + 1) * 512],
                start=False, stop=True), inc="pe")
            pv2 = M("pe")
            d1 = E("vector", lambda e, half=half: e.tensor_mul(
                out6[:, 0:512], wop[half][:, 0:512],
                gbuf[:, 0:512]),
                waits=[("pe", pv2), ("ac", ag), ("q_y", m6.get("y_q", 0))], inc="dv")
            m6[f"wo_free{half}"] = d1
            m6["g_free"] = d1
            qx = E("sync", lambda e, tc=tc, half=half: e.dma_start(
                out=xf6[0][:, 0:512],
                in_=x_own[tc * 128:(tc + 1) * 128, half * 512:(half + 1) * 512]),
                waits=[("dv", m6.get("x_free", 0))], inc="q_f0", incval=16)
            d2 = E("vector", lambda e, half=half: e.tensor_add(
                out6[:, 0:512], out6[:, 0:512], xf6[0][:, 0:512]),
                waits=[("q_f0", qx), ("dv", d1)], inc="dv")
            m6["x_free"] = d2
            yq = E("sync", lambda e, tc=tc, half=half: e.dma_start(
                out=y[tc * 128:(tc + 1) * 128, half * 512:(half + 1) * 512], in_=out6[:, 0:512]),
                waits=[("dv", d2)], inc="q_y", incval=16)
            m6["y_q"] = yq
        m6[f"hcol_free{i}"] = M("pe")
        m6["rst_free"] = M("ac")
    P.fence()
    P.realize()
    esD.close()
    esP.close()
    meta = dict(SEQ=SEQ, QTW=QTW, NK=NK, OWN=OWN, dbg=sorted(dbg.keys()))
    return nc, meta


BF = ml_dtypes.bfloat16




def eo_perm():
    perm = np.zeros(D, dtype=np.int64)
    for hp in range(HP):
        for i in range(64):
            perm[hp * 128 + i] = 128 * hp + 2 * i
            perm[hp * 128 + 64 + i] = 128 * hp + 2 * i + 1
    return perm


def wo_row_perm():
    perm = np.zeros(D, dtype=np.int64)
    for hp in range(HP):
        for i in range(64):
            perm[hp * 128 + i] = 64 * hp + i
            perm[hp * 128 + 64 + i] = 512 + 64 * hp + i
    return perm


def rope_tables(SEQ, positions):
    i = np.arange(D // 2, dtype=np.float32)
    fc = np.power(np.float32(10000.0), -2.0 * i / np.float32(D)).astype(np.float32)
    pos = positions.astype(np.float32)
    T1 = np.zeros((HP, 128, len(pos)), dtype=np.float32)
    T2 = np.zeros((HP, 128, len(pos)), dtype=np.float32)
    for hp in range(HP):
        f = fc[64 * hp:64 * hp + 64]
        ang = pos[None, :] * f[:, None]
        c, s = np.cos(ang), np.sin(ang)
        T1[hp, 0:64] = c
        T1[hp, 64:128] = s
        T2[hp, 0:64] = s
        T2[hp, 64:128] = c
    NT = len(pos) // 512
    T1 = T1.reshape(HP, 128, NT, 512).transpose(2, 1, 0, 3).reshape(NT, 128, HP * 512)
    T2 = T2.reshape(HP, 128, NT, 512).transpose(2, 1, 0, 3).reshape(NT, 128, HP * 512)
    return np.ascontiguousarray(T1).astype(BF), np.ascontiguousarray(T2).astype(BF)


def own_rows(SEQ, m):
    QTW = SEQ // 16
    rows = []
    for s in range(4):
        t = m + 4 * s
        rows.append(np.arange(t * QTW, (t + 1) * QTW))
    return np.concatenate(rows)


def prep_core(inp, c, SEQ):
    QTW = SEQ // 16
    NK = QTW // 128
    NAMB = 3 * NK
    NKS_T = SEQ // 128
    b, m = divmod(c, 4)
    x = np.asarray(inp["L0_attn_x"], dtype=np.float32)
    rows = own_rows(SEQ, m)
    perm = eo_perm()
    woperm = wo_row_perm()

    Wq = np.asarray(inp["L0_attn_Wq"], np.float32)
    Wk = np.asarray(inp["L0_attn_Wk"], np.float32)
    Wv = np.asarray(inp["L0_attn_Wv"], np.float32)
    Wo = np.asarray(inp["L0_attn_Wo"], np.float32)
    bq = np.asarray(inp["L0_attn_bq"], np.float32)
    bk = np.asarray(inp["L0_attn_bk"], np.float32)
    bv = np.asarray(inp["L0_attn_bv"], np.float32)
    bo = np.asarray(inp["L0_attn_bo"], np.float32)

    Wv_e = np.zeros((D, VROW), np.float32)
    bv_e = np.zeros(VROW, np.float32)
    for hp in range(HP):
        blk = hp * VW
        Wv_e[:, blk:blk + 64] = Wv[:, 64 * hp:64 * hp + 64]
        Wv_e[:, blk + 65:blk + 129] = Wv[:, 512 + 64 * hp:512 + 64 * hp + 64]
        bv_e[blk:blk + 64] = bv[64 * hp:64 * hp + 64]
        bv_e[blk + 64] = 1.0
        bv_e[blk + 65:blk + 129] = bv[512 + 64 * hp:512 + 64 * hp + 64]
        bv_e[blk + 129] = 1.0

    T1k, T2k = rope_tables(SEQ, np.arange(SEQ))
    T1q, T2q = rope_tables(SEQ, rows)
    Tk = np.ascontiguousarray(np.concatenate([T1k, T2k], axis=2))
    Tq = np.ascontiguousarray(np.concatenate([T1q, T2q], axis=2))

    imat = np.vstack([np.eye(64), -np.eye(64)]).astype(BF)
    ind = np.zeros((HP, 128, 16), np.float32)
    for hp in range(HP):
        ind[hp, 0:64, hp] = 1.0
        ind[hp, 64:128, 8 + hp] = 1.0

    p = np.arange(128, dtype=np.int64)
    kidx = np.zeros((128, 32), np.uint32)
    vidx = np.zeros((128, 32), np.uint32)
    for s in range(4):
        t = m + 4 * s
        for hp in range(HP):
            kidx[:, s * HP + hp] = hp * 128 * SEQ + p * SEQ + t * QTW
            vidx[:, s * HP + hp] = hp * 128 * (NKS_T * VW) + p * (NKS_T * VW) + t * NK * VW
    pflag = np.zeros((1, 4 * NAMB), np.float32)
    for s in range(4):
        for a in range(NAMB):
            pflag[0, s * NAMB + a] = 0.0 if a < m * NK else -60000.0

    return {
        "x_kv": np.ascontiguousarray(x[b]),
        "x_own": np.ascontiguousarray(x[b][rows]),
        "Wq_p": np.ascontiguousarray(Wq[:, perm]),
        "Wk_p": np.ascontiguousarray(Wk[:, perm]),
        "Wv_e": Wv_e,
        "bq_p": np.ascontiguousarray(bq[perm]).reshape(8, 128),
        "bk_p": np.ascontiguousarray(bk[perm]).reshape(8, 128),
        "bv_e": bv_e,
        "Wo_p": np.ascontiguousarray(Wo[woperm, :]),
        "bo": bo,
        "Wfc": np.asarray(inp["L0_ffn_W_fc"], np.float32),
        "Wpj": np.asarray(inp["L0_ffn_W_proj"], np.float32),
        "lnw": np.asarray(inp["L0_ffn_ln_w"], np.float32).reshape(32, 128),
        "lnb": np.asarray(inp["L0_ffn_ln_b"], np.float32).reshape(32, 128),
        "Tk": Tk, "Tq": Tq,
        "imat": imat,
        "ind": ind.astype(BF),
        "kidx": kidx, "vidx": vidx,
        "pflag": pflag,
    }




_CACHE = {}


def _get_nc():
    if "nc" not in _CACHE:
        _CACHE["nc"] = build(SEQ=8192, debug=False)[0]
    return _CACHE["nc"]


def kernel(**inputs):
    from concourse.bass_utils import run_bass_kernel_spmd
    SEQ = 8192
    nc = _get_nc()
    in_maps = [prep_core(inputs, c, SEQ) for c in range(8)]
    res = run_bass_kernel_spmd(nc, in_maps, core_ids=list(range(8)))
    out = np.zeros((2, SEQ, D), np.float32)
    for c in range(8):
        b, m = divmod(c, 4)
        out[b][own_rows(SEQ, m)] = res.results[c]["y"]
    return out


# version 06
# speedup vs baseline: 1.0916x; 1.0233x over previous
import numpy as np
from collections import defaultdict
from contextlib import ExitStack

import ml_dtypes
import concourse.bass as bass
import concourse.mybir as mybir
from concourse.bass import IndirectOffsetOnAxis

F32 = mybir.dt.float32
BF16 = mybir.dt.bfloat16
U32 = mybir.dt.uint32
AF = mybir.ActivationFunctionType
ALU = mybir.AluOpType
AXX = mybir.AxisListType.X

D = 1024
H = 16
HP = 8
FF = 4096
VW = 132
VROW = HP * VW
LN_EPS = 1e-5
ABS_EPS = 1e-6
NEG = -60000.0


class Prog:
    def __init__(self, nc):
        self.nc = nc
        self.ops = {e: [] for e in ("tensor", "vector", "scalar", "gpsimd", "sync")}
        self.sems = {}
        self.cnt = defaultdict(int)

    def sem(self, name):
        if name not in self.sems:
            self.sems[name] = self.nc.alloc_semaphore(name)
        return self.sems[name]

    def emit(self, eng, fn, waits=(), inc=None, incval=1):
        if inc is not None:
            self.sem(inc)
            self.cnt[inc] += incval
        for (s, v) in waits:
            self.sem(s)
        self.ops[eng].append((fn, tuple(waits), inc, incval))
        return self.cnt[inc] if inc is not None else None

    def mark(self, name):
        return self.cnt[name]

    def fence(self, engines=("tensor", "vector", "scalar", "gpsimd", "sync")):
        waits = tuple((s, self.cnt[s]) for s in self.sems)
        for e in engines:
            self.ops[e].append((None, waits, None, 1))

    def realize(self):
        with self.nc.Block() as block:
            for eng_name in ("sync", "tensor", "scalar", "vector", "gpsimd"):
                ops = self.ops[eng_name]
                if not ops:
                    continue

                def body(e, ops=ops):
                    for fn, waits, inc, incval in ops:
                        for (s, v) in waits:
                            if v > 0:
                                e.wait_ge(self.sems[s], v)
                        if fn is None:
                            continue
                        ins = fn(e)
                        if inc is not None:
                            ins.then_inc(self.sems[inc], incval)
                getattr(block, eng_name)(body)


def build(SEQ=8192, debug=False):
    QTW = SEQ // 16
    NK = QTW // 128
    assert QTW % 128 == 0 and QTW <= 512
    OWN = 4 * QTW
    NT1 = SEQ // 512
    NAMB = 3 * NK
    NKS_T = SEQ // 128
    NT6 = OWN // 128
    HCN = FF // 128

    nc = bass.Bass("TRN2", target_bir_lowering=False)
    P = Prog(nc)
    E, M = P.emit, P.mark
    for s in ("pe", "dv", "ac", "gp"):
        P.sem(s)

    def inp(name, shape, dt=F32):
        return nc.dram_tensor(name, shape, dt, kind="ExternalInput").ap()

    x_kv = inp("x_kv", [SEQ, D])
    x_own = inp("x_own", [OWN, D])
    Wq_p = inp("Wq_p", [D, D])
    Wk_p = inp("Wk_p", [D, D])
    Wv_e = inp("Wv_e", [D, VROW])
    bq_p = inp("bq_p", [8, 128])
    bk_p = inp("bk_p", [8, 128])
    bv_e = inp("bv_e", [VROW])
    Wo_p = inp("Wo_p", [D, D])
    bo = inp("bo", [D])
    Wfc = inp("Wfc", [FF, D])
    Wpj = inp("Wpj", [D, FF])
    lnw = inp("lnw", [HCN, 128])
    lnb = inp("lnb", [HCN, 128])
    Tk = inp("Tk", [SEQ // 512, 128, 2 * HP * 512], BF16)
    Tq = inp("Tq", [OWN // 512, 128, 2 * HP * 512], BF16)
    imat_i = inp("imat", [128, 64], BF16)
    ind_i = inp("ind", [HP, 128, 16], BF16)
    kidx_i = inp("kidx", [128, 32], U32)
    vidx_i = inp("vidx", [128, 32], U32)
    pflag_i = inp("pflag", [1, 4 * NAMB])

    y = nc.dram_tensor("y", [OWN, D], F32, kind="ExternalOutput").ap()

    K_dram = nc.dram_tensor("K_dram", [HP, 128, SEQ], BF16).ap()
    V_dram = nc.dram_tensor("V_dram", [HP, 128, NKS_T * VW], BF16).ap()
    h_dram = nc.dram_tensor("h_dram", [HCN, 128, OWN], BF16).ap()
    wp_dram = nc.dram_tensor("wp_dram", [HCN, 128, D], BF16).ap()
    statb_dram = nc.dram_tensor("statb_dram", [2, OWN], BF16).ap()
    statrs_dram = nc.dram_tensor("statrs_dram", [1, OWN], F32).ap()

    dbg = {}
    if debug:
        def dout(name, shape, dt=F32):
            dbg[name] = nc.dram_tensor("dbg_" + name, shape, dt, kind="ExternalOutput").ap()
        dout("K", [HP, 128, SEQ], BF16)
        dout("V", [HP, 128, NKS_T * VW], BF16)
        dout("Q", [128, HP * OWN], BF16)
        dout("xn", [8, 128, OWN], BF16)
        dout("ctx", [128, HP * OWN], BF16)
        dout("b16", [128, 16])
        dout("h", [HCN, 128, OWN], BF16)
        dout("wp", [HCN, 128, D], BF16)
        dout("gam", [128, 2])
        dout("c1", [1, D])
        dout("c0", [1, D])
        dout("stat", [3, OWN])

    esP = ExitStack()
    scope = [esP]
    sb = lambda name, shape, dt=F32: scope[0].enter_context(nc.sbuf_tensor(name, shape, dt))
    ps = lambda name, shape, dt=F32: esP.enter_context(nc.psum_tensor(name, shape, dt))

    ones_bf = sb("ones_bf", [128, 128], BF16)
    ones_f = sb("ones_f", [128, 128])
    identf = sb("identf", [128, 128])
    identb = sb("identb", [128, 128], BF16)
    ident64 = sb("ident64", [64, 64], BF16)
    epscol = sb("epscol", [128, 1])
    imat = sb("imat_s", [128, 64], BF16)
    ind16 = sb("ind16", [128, HP * 16], BF16)

    bvrow = sb("bvrow", [1, VROW], BF16)
    lnwc = sb("lnwc", [128, HCN])
    lnbc = sb("lnbc", [128, HCN])
    lnw_bf = sb("lnw_bf", [128, HCN], BF16)
    lnb_bf = sb("lnb_bf", [128, HCN], BF16)
    kidx = sb("kidx_s", [128, 32], U32)
    vidx = sb("vidx_s", [128, 32], U32)
    pflag = sb("pflag_s", [1, 4 * NAMB])
    bias16 = sb("bias16", [128, 8])
    biasamb = sb("biasamb", [128, 4 * NAMB * 8])
    uacc = sb("uacc", [16, 4])
    gam = sb("gam_s", [128, 2])
    c1row = sb("c1row", [1, D], BF16)
    c0row = sb("c0row", [1, D], BF16)
    Q_sb = sb("Q_sb", [128, HP * OWN], BF16)
    ctx_sb = sb("ctx_sb", [128, HP * OWN], BF16)
    xn_sb = sb("xn_sb", [128, 8 * OWN], BF16)
    urow_sc = sb("urow_sc", [16, 16])
    urow_r = sb("urow_r", [1, 16])
    pfcol = sb("pfcol", [128, 4 * NAMB])
    _rope_tab_gate = [("pe", 0)]

    E("gpsimd", lambda e: e.memset(ones_bf[:], 1.0))
    E("gpsimd", lambda e: e.memset(ones_f[:], 1.0))
    E("gpsimd", lambda e: e.memset(uacc[:], 0.0))
    E("gpsimd", lambda e: e.memset(urow_sc[:], 0.0))
    E("gpsimd", lambda e: e.memset(epscol[:], LN_EPS))
    for t, n in ((identf, 128), (identb, 128), (ident64, 64)):
        E("gpsimd", lambda e, t=t: e.memset(t[:], 0.0), inc="gp")
        E("gpsimd", lambda e, t=t, n=n: e.affine_select(
            out=t[:], in_=t[:], compare_op=ALU.not_equal, fill=1.0, base=0,
            pattern=[[-1, n]], channel_multiplier=1), waits=[("gp", M("gp"))], inc="gp")
    gp_consts = M("gp")

    E("sync", lambda e: e.dma_start(out=imat[:], in_=imat_i), inc="q_once", incval=16)
    E("sync", lambda e: e.dma_start(out=pflag[:], in_=pflag_i), inc="q_once", incval=16)
    E("sync", lambda e: e.dma_start(out=ind16[:].rearrange("p (h c) -> p h c", h=HP), in_=ind_i.rearrange("h p c -> p h c")), inc="q_once", incval=16)
    E("sync", lambda e: e.dma_start(out=kidx[:], in_=kidx_i), inc="q_once", incval=16)
    E("sync", lambda e: e.dma_start(out=vidx[:], in_=vidx_i), inc="q_once", incval=16)
    E("gpsimd", lambda e: e.dma_start(out=bvrow[:], in_=bv_e[None, :]), inc="q_onceg", incval=16)
    E("gpsimd", lambda e: e.dma_start(out=bqr[:], in_=bq_p.rearrange("a b -> (a b)")[None, :]),
      inc="q_onceg", incval=16)
    E("gpsimd", lambda e: e.dma_start(out=bkr[:], in_=bk_p.rearrange("a b -> (a b)")[None, :]),
      inc="q_onceg", incval=16)

    pd = [esP.enter_context(nc.psum_tensor(f"pd{i}", [128, 1024], F32)) for i in range(4)]
    pb = [pd[i // 2][:, (i % 2) * 512:(i % 2 + 1) * 512] for i in range(8)]
    btmp = sb("btmp", [HCN, 128])
    trb = pb[0]
    for (src, dst, rows) in ((lnw, lnwc, HCN), (lnb, lnbc, HCN)):
        qv = E("sync", lambda e, s=src, r=rows: e.dma_start(out=btmp[0:r, :], in_=s), waits=[("pe", M("pe"))], inc="q_btmp", incval=16)
        pv = E("tensor", lambda e, r=rows: e.transpose(trb[:, 0:r], btmp[0:r, :], identf[0:r, 0:r]),
               waits=[("q_btmp", qv), ("gp", gp_consts), ("ac", M("ac"))], inc="pe")
        E("scalar", lambda e, d=dst, r=rows: e.copy(out=d[:, 0:r], in_=trb[:, 0:r]),
          waits=[("pe", pv)], inc="ac")
    v = E("vector", lambda e: e.tensor_copy(out=lnw_bf[:], in_=lnwc[:]), waits=[("ac", M("ac"))], inc="dv")
    v = E("vector", lambda e: e.tensor_copy(out=lnb_bf[:], in_=lnbc[:]), waits=[("ac", M("ac"))], inc="dv")
    dv_ln = M("dv")

    esA = ExitStack()
    scope[0] = esA
    Wk_sb = sb("Wk_sb", [128, 8 * D], BF16)
    Wv_sb = sb("Wv_sb", [128, 8 * VROW], BF16)
    ones_row = sb("ones_row", [1, 512], BF16)
    bqr = sb("bqr", [1, D], BF16)
    bkr = sb("bkr", [1, D], BF16)
    E("gpsimd", lambda e: e.memset(ones_row[:], 1.0))
    for (dsb, dsrc) in ((Wk_sb, Wk_p), (Wv_sb, Wv_e)):
        E("gpsimd", lambda e, a=dsb, b=dsrc: e.dma_start(
            out=a[:].rearrange("p (c n) -> p c n", c=8), in_=b.rearrange("(c p) n -> p c n", p=128)),
            inc="q_onceg", incval=16)
    qg_w_cell = [M("q_onceg")]

    esA0 = ExitStack()
    scope[0] = esA0
    gtile = [sb(f"gtile{i}", [128, D]) for i in range(2)]
    gabs = [sb(f"gabs{i}", [128, D], BF16) for i in range(2)]
    ginv = sb("ginv", [1, 4])
    gps_h = [pb[1], pb[2]]
    gbc = pb[3]

    def gamma_pass(view, nt, wi):
        pe_pair = [M("pe")]
        for t in range(nt):
            i = t % 2
            qv = E("sync", lambda e, t=t, i=i, view=view: e.dma_start(
                out=gtile[i][:], in_=view(t)), waits=[("pe", pe_pair[0])], inc=f"q_g{i}", incval=16)
            av = E("scalar", lambda e, i=i: e.activation(out=gabs[i][:], in_=gtile[i][:], func=AF.Abs),
                   waits=[(f"q_g{i}", qv)], inc="ac")
            for half in range(2):
                E("tensor", lambda e, i=i, t=t, half=half, nt=nt: e.matmul(
                    gps_h[half][0:1, 0:512],
                    ones_bf[:, 0:1], gabs[i][:, half * 512:(half + 1) * 512],
                    start=(t == 0), stop=(t == nt - 1)),
                    waits=[("ac", av)] + ([("dv", M("dv"))] if t == 0 else []), inc="pe")
            if t >= 1:
                pe_pair[0] = M("pe") - 2
        d = E("vector", lambda e, wi=wi: e.reduce_sum(
            ginv[0:1, wi:wi + 1], gps_h[0][0:1, 0:512], axis=AXX), waits=[("pe", M("pe"))], inc="dv")
        d = E("vector", lambda e, wi=wi: e.reduce_sum(
            ginv[0:1, wi + 2:wi + 3], gps_h[1][0:1, 0:512], axis=AXX), waits=[("pe", M("pe"))], inc="dv")
        return E("vector", lambda e, wi=wi: e.tensor_add(
            ginv[0:1, wi:wi + 1], ginv[0:1, wi:wi + 1], ginv[0:1, wi + 2:wi + 3]),
            waits=[("dv", d)], inc="dv")

    fcv = lambda t: Wfc[t * 128:(t + 1) * 128, :]
    pjv = lambda t: Wpj.rearrange("(g p) (t n) -> g t p n", p=128, n=D)[t // 4][t % 4]
    gamma_pass(fcv, FF // 128, 0)
    gamma_pass(pjv, (D // 128) * (FF // D), 1)
    dvv = E("vector", lambda e: e.tensor_scalar(out=ginv[0:1, 0:2], in0=ginv[0:1, 0:2],
                                                scalar1=1.0 / (D * FF),
                                                scalar2=ABS_EPS, op0=ALU.mult, op1=ALU.add),
            waits=[("dv", M("dv"))], inc="dv")
    dvv = E("vector", lambda e: e.reciprocal(out=ginv[0:1, 0:2], in_=ginv[0:1, 0:2]),
            waits=[("dv", dvv)], inc="dv")
    pv = E("tensor", lambda e: e.matmul(gbc[:, 0:2], ones_f[0:1, :], ginv[0:1, 0:2],
                                        start=True, stop=True), waits=[("dv", dvv)], inc="pe")
    ac_gam = E("scalar", lambda e: e.copy(out=gam[:], in_=gbc[:, 0:2]), waits=[("pe", pv)], inc="ac")
    if debug:
        E("sync", lambda e: e.dma_start(out=dbg["gam"], in_=gam[:]), waits=[("ac", ac_gam)],
          inc="q_dbg", incval=16)
    esA0.close()
    esA0 = ExitStack()
    scope[0] = esA0

    ac_tr_hold = [0]
    wp_hold = {0: 0, 1: 0}
    v4_hold = {0: 0, 1: 0}
    pj_f = [sb("pj_f0", [128, FF])] * 2
    pj_t = [sb("pj_t0", [128, FF], BF16)] * 2
    pj_s = sb("pj_s", [128, FF // 2], BF16)
    wpt = [sb(f"wpt{i}", [128, 128], BF16) for i in range(2)]
    wps = [sb(f"wps{i}", [128, 128], BF16) for i in range(2)]
    trn = pb[4].bitcast(BF16)
    c10 = [pb[5], pb[6]]
    c00 = [pb[7], pb[3]]
    qg_hold = [0]
    for dchunk in range(D // 128):
        i = dchunk % 2
        pe_gate = M("pe")
        qv = E("sync", lambda e, d=dchunk, i=i: e.dma_start(
            out=pj_f[i][:], in_=Wpj[d * 128:(d + 1) * 128, :]),
            waits=[("dv", M("dv"))], inc=f"q_pj{i}", incval=16)
        for hf in range(2):
            c0_, c1_ = hf * (FF // 2), (hf + 1) * (FF // 2)
            v1 = E("vector", lambda e, i=i, c0_=c0_, c1_=c1_: e.tensor_scalar(
                out=pj_s[:], in0=pj_f[i][:, c0_:c1_], scalar1=gam[:, 1:2], scalar2=0.5,
                op0=ALU.mult, op1=ALU.is_gt),
                waits=[(f"q_pj{i}", qv), ("ac", ac_gam), ("dv", M("dv"))], inc="dv")
            v2 = E("vector", lambda e, i=i, c0_=c0_, c1_=c1_: e.tensor_scalar(
                out=pj_t[i][:, c0_:c1_], in0=pj_f[i][:, c0_:c1_], scalar1=gam[:, 1:2], scalar2=-0.5,
                op0=ALU.mult, op1=ALU.is_lt),
                waits=[(f"q_pj{i}", qv), ("pe", pe_gate)], inc="dv")
            v3 = E("vector", lambda e, i=i, c0_=c0_, c1_=c1_: e.tensor_sub(
                pj_t[i][:, c0_:c1_], pj_s[:], pj_t[i][:, c0_:c1_]), waits=[("dv", v2)], inc="dv")
        for hc in range(HCN):
            j = hc % 2
            pv = E("tensor", lambda e, i=i, hc=hc, j=j: e.transpose(
                trn[:, j * 128:(j + 1) * 128], pj_t[i][:, hc * 128:(hc + 1) * 128], identb[:]),
                waits=[("dv", v3), ("ac", ac_tr_hold[0]), ("dv", dv_ln)], inc="pe")
            av = E("scalar", lambda e, j=j: e.copy(out=wpt[j][:], in_=trn[:, j * 128:(j + 1) * 128]),
                   waits=[("pe", pv), ("dv", v4_hold[j])], inc="ac")
            if hc >= 1:
                ac_tr_hold[0] = av - 1
            E("tensor", lambda e, j=j, hc=hc, d=dchunk: e.matmul(
                c10[d // 4][0:1, (d % 4) * 128:(d % 4 + 1) * 128], lnw_bf[:, hc:hc + 1], wpt[j][:],
                start=(hc == 0), stop=(hc == HCN - 1)), waits=[("ac", av)], inc="pe")
            E("tensor", lambda e, j=j, hc=hc, d=dchunk: e.matmul(
                c00[d // 4][0:1, (d % 4) * 128:(d % 4 + 1) * 128], lnb_bf[:, hc:hc + 1], wpt[j][:],
                start=(hc == 0), stop=(hc == HCN - 1)), inc="pe")
            v4 = E("vector", lambda e, j=j, hc=hc: e.tensor_scalar_mul(
                wps[j][:], wpt[j][:], lnwc[:, hc:hc + 1]),
                waits=[("ac", av), (f"q_wp{j}", wp_hold[j])], inc="dv")
            v4_hold[j] = v4
            qq = E("gpsimd", lambda e, j=j, hc=hc, d=dchunk: e.dma_start(
                out=wp_dram[hc, :, d * 128:(d + 1) * 128], in_=wps[j][:]),
                waits=[("dv", v4)], inc=f"q_wp{j}", incval=16)
            wp_hold[j] = qq
    pe_c = M("pe")
    av = E("scalar", lambda e: e.copy(out=c1row[0:1, 0:512], in_=c10[0][0:1, 0:512]),
           waits=[("pe", pe_c)], inc="ac")
    av = E("scalar", lambda e: e.copy(out=c1row[0:1, 512:1024], in_=c10[1][0:1, 0:512]), inc="ac")
    av = E("scalar", lambda e: e.copy(out=c0row[0:1, 0:512], in_=c00[0][0:1, 0:512]), inc="ac")
    av = E("scalar", lambda e: e.copy(out=c0row[0:1, 512:1024], in_=c00[1][0:1, 0:512]), inc="ac")
    ac_c10 = M("ac")
    if debug:
        cc1 = sb("cc1", [1, D])
        cc0 = sb("cc0", [1, D])
        E("vector", lambda e: e.tensor_copy(out=cc1[:], in_=c1row[:]), waits=[("ac", ac_c10)], inc="dv")
        E("vector", lambda e: e.tensor_copy(out=cc0[:], in_=c0row[:]), waits=[("ac", ac_c10)], inc="dv")
        E("sync", lambda e: e.dma_start(out=dbg["c1"], in_=cc1[:]), waits=[("dv", M("dv"))], inc="q_dbg", incval=16)
        E("sync", lambda e: e.dma_start(out=dbg["c0"], in_=cc0[:]), waits=[("dv", M("dv"))], inc="q_dbg", incval=16)
        E("sync", lambda e: e.dma_start(out=dbg["wp"], in_=wp_dram), waits=[("q_wp0", M("q_wp0")), ("q_wp1", M("q_wp1"))], inc="q_dbg", incval=16)

    esA0.close()
    scope[0] = esA
    xtile = [sb(f"xtile{i}", [128, D]) for i in range(2)]
    xnb = sb("xnb", [128, D], BF16)
    lncol = sb("lncol", [128, 8])
    x2s = sb("x2s", [128, D], BF16)
    xnT = [sb(f"xnT{i}", [128, 8 * 512], BF16) for i in range(2)]
    tabs = sb("tabs", [128, 2 * HP * 512], BF16)
    t1s = [tabs[:, 0:HP * 512]] * 2
    t2s = [tabs[:, HP * 512:2 * HP * 512]] * 2
    m1 = [sb(f"m1_{i}", [128, 512], BF16) for i in range(2)]
    m2 = [sb(f"m2_{i}", [128, 512], BF16) for i in range(2)]
    kst = [sb(f"kst{i}", [128, 512], BF16) for i in range(2)]
    k2t = sb("k2t", [128, 512], BF16)
    vraw = [sb("vraw0", [128, 4 * VROW], BF16)] * 2
    tr_ps = pb[0].bitcast(BF16)
    eo_ps2 = [pb[1], pb[7]]
    ro_ps2 = [pb[2], pb[5]]
    v_ps = [pb[3], pb[4]]
    u_ps = pb[6]

    def ln_and_transpose(x_src, tt, xnT_buf, marks, tw, key):
        for sub in range(tw // 128):
            yield
            i = sub % 2
            r0 = tt * tw + sub * 128
            qv = E("sync", lambda e, i=i, r0=r0, x_src=x_src: e.dma_start(
                out=xtile[i][:], in_=x_src[r0:r0 + 128, :]),
                waits=[("ac", marks.get(f"x_reuse{i}", 0))], inc=f"q_x{i}", incval=16)
            av0 = E("scalar", lambda e, i=i: e.activation(
                out=x2s[:], in_=xtile[i][:], func=AF.Identity, accum_out=lncol[:, 0:1]),
                waits=[(f"q_x{i}", qv), ("dv", marks.get("x2_reuse", 0))], inc="ac")
            dv0 = E("vector", lambda e, i=i: e.scalar_tensor_tensor(
                out=x2s[:], in0=xtile[i][:], scalar=1.0, in1=xtile[i][:],
                op0=ALU.mult, op1=ALU.mult, accum_out=lncol[:, 1:2]),
                waits=[("ac", av0)], inc="dv")
            marks["x2_reuse"] = dv0
            dv1 = E("vector", lambda e: e.tensor_scalar_mul(lncol[:, 2:3], lncol[:, 0:1], 1.0 / D),
                    waits=[("dv", dv0)], inc="dv")
            dv2 = E("vector", lambda e: e.scalar_tensor_tensor(
                out=lncol[:, 3:4], in0=lncol[:, 2:3], scalar=-1.0, in1=lncol[:, 2:3],
                op0=ALU.mult, op1=ALU.mult), waits=[("dv", dv1)], inc="dv")
            dv3 = E("vector", lambda e: e.scalar_tensor_tensor(
                out=lncol[:, 3:4], in0=lncol[:, 1:2], scalar=1.0 / D, in1=lncol[:, 3:4],
                op0=ALU.mult, op1=ALU.add), waits=[("dv", dv2)], inc="dv")
            av1 = E("scalar", lambda e: e.activation(out=lncol[:, 4:5], in_=lncol[:, 3:4],
                                                     func=AF.Ln, bias=epscol[:]), waits=[("dv", dv3)], inc="ac")
            av2 = E("scalar", lambda e: e.activation(out=lncol[:, 4:5], in_=lncol[:, 4:5],
                                                     func=AF.Exp, scale=-0.5), waits=[("ac", av1)], inc="ac")
            dv4 = E("vector", lambda e: e.scalar_tensor_tensor(
                out=lncol[:, 5:6], in0=lncol[:, 2:3], scalar=-1.0, in1=lncol[:, 4:5],
                op0=ALU.mult, op1=ALU.mult), waits=[("ac", av2)], inc="dv")
            av3 = E("scalar", lambda e, i=i: e.activation(
                out=xnb[:], in_=xtile[i][:], func=AF.Identity,
                scale=lncol[:, 4:5], bias=lncol[:, 5:6]), waits=[("dv", dv4)], inc="ac")
            marks[f"x_reuse{i}"] = av3
            for fc in range(8):
                pv = E("tensor", lambda e, fc=fc: e.transpose(
                    tr_ps[:, fc * 128:(fc + 1) * 128], xnb[:, fc * 128:(fc + 1) * 128], identb[:]),
                    waits=[("ac", av3)] + ([("ac", marks.get("tr_evac", 0))] if fc == 0 else []),
                    inc="pe")
            av4 = E("scalar", lambda e, sub=sub, xnT_buf=xnT_buf, tw=tw: e.copy(
                out=xnT_buf.rearrange("p (f t) -> p f t", f=8)[:, :, sub * 128:(sub + 1) * 128],
                in_=tr_ps[:].rearrange("p (f t) -> p f t", f=8)),
                waits=[("pe", pv), ("pe", marks.get(key + "_pefree", 0)),
                       ("dv", marks.get(key + "_dvfree", 0))], inc="ac")
            marks["tr_evac"] = av4
            marks[key] = av4

    def rope_project(xnT_buf, W_sb, brow, T1buf, T2buf, dst_kind, tt, tw, marks, dbg_norm=None):
        def emit_proj(hp):
            ei = hp % 2
            for fc in range(8):
                E("tensor", lambda e, hp=hp, fc=fc, tw=tw, ei=ei: e.matmul(
                    eo_ps2[ei][:, 0:tw], W_sb[:, fc * D + hp * 128: fc * D + (hp + 1) * 128],
                    xnT_buf[:, fc * tw: fc * tw + tw], start=(fc == 0), stop=False),
                    waits=([("dv", marks.get(f"eo_used{ei}", 0)), ("ac", marks.get("tr_evac", 0))]
                           + ([("q_onceg", qg_w_cell[0])] if hp == 0 else []) if fc == 0 else []),
                    inc="pe")
            E("tensor", lambda e, hp=hp, tw=tw, ei=ei, brow=brow: e.matmul(
                eo_ps2[ei][:, 0:tw], brow[0:1, hp * 128:(hp + 1) * 128], ones_row[0:1, 0:tw],
                start=False, stop=True), inc="pe")
            return M("pe")

        pmark = {0: emit_proj(0)}
        filler = marks.pop("filler", None)
        for hp in range(HP):
            ei = hp % 2
            ri = hp % 2
            if hp + 1 < HP:
                pmark[hp + 1] = emit_proj(hp + 1)
            if filler is not None:
                filler(hp)
            w1 = [("pe", pmark[hp]), ("pe", marks.get(f"m_reuse{ri}", 0))]
            if hp == 0:
                w1.append(_rope_tab_gate[0])
            d1 = E("vector", lambda e, hp=hp, tw=tw, ri=ri, ei=ei, T1buf=T1buf: e.tensor_mul(
                m1[ri][:, 0:tw], eo_ps2[ei][:, 0:tw], T1buf[:, hp * tw: hp * tw + tw]),
                waits=w1, inc="dv")
            d2 = E("vector", lambda e, hp=hp, tw=tw, ri=ri, ei=ei, T2buf=T2buf: e.tensor_mul(
                m2[ri][:, 0:tw], eo_ps2[ei][:, 0:tw], T2buf[:, hp * tw: hp * tw + tw]), inc="dv")
            marks[f"eo_used{ei}"] = d2
            pv1 = E("tensor", lambda e, tw=tw, ri=ri: e.matmul(
                ro_ps2[ri][0:64, 0:tw], imat[:], m1[ri][:, 0:tw], start=True, stop=True),
                waits=[("dv", d2), ("ac", marks.get(f"ro_used{ri}", 0))], inc="pe")
            pv2 = E("tensor", lambda e, tw=tw, ri=ri: e.matmul(
                ro_ps2[ri][64:128, 0:tw], imat[:], m2[ri][:, 0:tw], start=True, stop=True,
                tile_position=(0, 64)), inc="pe")
            marks[f"m_reuse{ri}"] = pv2
            if dst_kind[0] == 'K':
                i = hp % 2
                av2 = E("scalar", lambda e, i=i, tw=tw, ri=ri: e.copy(
                    out=kst[i][:, 0:tw], in_=ro_ps2[ri][:, 0:tw]),
                    waits=[("pe", pv2), (f"q_ko{i}", marks.get(f"kst_reuse{i}", 0))], inc="ac")
                marks[f"ro_used{ri}"] = av2
                qv = E("sync", lambda e, i=i, hp=hp, tt=tt, tw=tw: e.dma_start(
                    out=K_dram[hp, :, tt * tw:(tt + 1) * tw], in_=kst[i][:, 0:tw]),
                    waits=[("ac", av2)], inc=f"q_ko{i}", incval=16)
                marks[f"kst_reuse{i}"] = qv
                ksrc = kst[i]
            else:
                av2 = E("scalar", lambda e, hp=hp, tt=tt, tw=tw, ri=ri: e.copy(
                    out=Q_sb[:, hp * OWN + tt * tw: hp * OWN + (tt + 1) * tw],
                    in_=ro_ps2[ri][:, 0:tw]),
                    waits=[("pe", pv2)], inc="ac")
                marks[f"ro_used{ri}"] = av2
                ksrc = None
            src = ksrc[:, 0:tw] if ksrc is not None else Q_sb[:, hp * OWN + tt * tw: hp * OWN + (tt + 1) * tw]
            d3 = E("vector", lambda e, src=src, tw=tw: e.tensor_mul(k2t[:, 0:tw], src, src),
                   waits=[("ac", av2), ("pe", marks.get("k2_reuse", 0))], inc="dv")
            pu = E("tensor", lambda e, hp=hp, tw=tw: e.matmul(
                u_ps[0:16, 0:tw], ind16[:, hp * 16:(hp + 1) * 16], k2t[:, 0:tw],
                start=(hp == 0), stop=(hp == HP - 1)),
                waits=[("dv", d3)] + ([("dv", marks.get("u_dv", 0))] if hp == 0 else []), inc="pe")
            marks["k2_reuse"] = pu
        ucol = 0 if dst_kind[0] == 'K' else 1
        dm = E("vector", lambda e, tw=tw: e.reduce_max(uacc[:, 3:4], u_ps[0:16, 0:tw], axis=AXX),
               waits=[("pe", M("pe"))], inc="dv")
        dm = E("vector", lambda e, ucol=ucol: e.tensor_max(uacc[:, ucol:ucol + 1],
                                                           uacc[:, ucol:ucol + 1], uacc[:, 3:4]),
               waits=[("dv", dm)], inc="dv")
        marks["u_dv"] = dm
        return marks

    def v_build_sub(xnT_buf, tt, tw, marks, subs):
        i = 0
        passes = [[(0, 396, 0, 3), (396, 792, 3, 6)], [(792, 1056, 6, 8)]]
        for sub in subs:
            for segs in passes:
                for fc in range(8):
                    for bi, (c0, c1, h0, h1) in enumerate(segs):
                        E("tensor", lambda e, fc=fc, bi=bi, c0=c0, c1=c1, sub=sub, tw=tw: e.matmul(
                            v_ps[bi][:, 0:c1 - c0],
                            xnT_buf[:, fc * tw + sub * 128: fc * tw + (sub + 1) * 128],
                            Wv_sb[:, fc * VROW + c0: fc * VROW + c1],
                            start=(fc == 0), stop=False),
                            waits=([("ac", marks.get(f"vps_reuse{bi}", 0)),
                                    ("ac", marks.get("tr_evac", 0))] if fc == 0 else []),
                            inc="pe")
                for bi, (c0, c1, h0, h1) in enumerate(segs):
                    E("tensor", lambda e, bi=bi, c0=c0, c1=c1: e.matmul(
                        v_ps[bi][:, 0:c1 - c0], ones_bf[0:1, :], bvrow[0:1, c0:c1],
                        start=False, stop=True), inc="pe")
                pv = M("pe")
                for bi, (c0, c1, h0, h1) in enumerate(segs):
                    E("scalar", lambda e, bi=bi, c0=c0, c1=c1, h0=h0, h1=h1, sub=sub, i=i: e.copy(
                        out=vraw[i][:].rearrange("p (h s c) -> p h s c", h=HP, c=VW)[:, h0:h1, sub, :],
                        in_=v_ps[bi][:, 0:c1 - c0].rearrange("p (h c) -> p h c", c=VW)),
                        waits=[("pe", pv), (f"q_vo{i}", marks.get(f"vraw_q{i}", 0))], inc="ac")
                    marks[f"vps_reuse{bi}"] = M("ac")

    def v_build_out(tt, tw, marks):
        i = 0
        av = M("ac")
        nsub = tw // 128
        for hp in range(HP):
            qv = E("gpsimd", lambda e, hp=hp, tt=tt, nsub=nsub, i=i: e.dma_start(
                out=V_dram[hp, :, tt * nsub * VW:(tt + 1) * nsub * VW],
                in_=vraw[i][:, hp * 4 * VW:(hp * 4 + nsub) * VW]),
                waits=[("ac", av)], inc=f"q_vo{i}", incval=16)
        marks[f"vraw_q{i}"] = M(f"q_vo{i}")
        return M("qs")

    P.fence()
    marks = {"tr_evac": M("ac"), "x2_reuse": M("dv"), "u_dv": M("dv")}

    def emit_tables(tt, src1, src2, tw):
        q1 = E("scalar", lambda e, tt=tt, src1=src1: e.dma_start(
            out=tabs[:], in_=src1[tt]),
            waits=[("dv", marks.get("tab_reuse", 0))], inc="q_t0", incval=16)
        marks["tab_q"] = ("q_t0", M("q_t0"))

    gen = ln_and_transpose(x_kv, 0, xnT[0], marks, 512, "lnk0")
    for _ in gen:
        pass
    for tt in range(NT1):
        i = tt % 2
        ni = (tt + 1) % 2
        emit_tables(tt, Tk, None, 512)
        _rope_tab_gate[0] = marks["tab_q"]
        if tt + 1 < NT1:
            marks[f"lnk{tt+1}_pefree"] = M("pe")
            nxt = ln_and_transpose(x_kv, tt + 1, xnT[ni], marks, 512, f"lnk{tt+1}")
        else:
            nxt = iter(())

        def filler(hp, tt=tt, nxt=nxt):
            if hp % 2 == 1:
                v_build_sub(xnT[tt % 2][:], tt, 512, marks, [(hp - 1) // 2])
            next(nxt, None)

        marks["filler"] = filler
        rope_project(xnT[i][:], Wk_sb, bkr, t1s[0][:], t2s[0][:], ('K', tt), tt, 512, marks)
        marks["tab_reuse"] = M("dv")
        for _ in nxt:
            pass
        v_build_out(tt, 512, marks)
    p1_done_q = M("qs")
    p1_done_pe = M("pe")

    E("gpsimd", lambda e: e.dma_start(
        out=Wk_sb[:].rearrange("p (c n) -> p c n", c=8),
        in_=Wq_p.rearrange("(c p) n -> p c n", p=128)),
        waits=[("pe", p1_done_pe)], inc="q_onceg", incval=16)
    qg_w_cell[0] = M("q_onceg")
    QW = min(512, OWN)
    for tt in range(OWN // QW):
        i = tt % 2
        emit_tables(tt, Tq, None, QW)
        _rope_tab_gate[0] = marks["tab_q"]
        marks[f"lnq{tt}_pefree"] = M("pe")
        for _ in ln_and_transpose(x_own, tt, xnT[i], marks, QW, f"lnq{tt}"):
            pass
        av = M("ac")
        for fc in range(8):
            E("vector", lambda e, fc=fc, tt=tt, i=i, QW=QW: e.tensor_copy(
                out=xn_sb[:, fc * OWN + tt * QW: fc * OWN + (tt + 1) * QW],
                in_=xnT[i][:, fc * QW:(fc + 1) * QW]), waits=([("ac", av)] if fc == 0 else []),
                inc="dv")
        marks[f"lnq{tt + 2}_dvfree"] = M("dv")
        rope_project(xnT[i][:], Wk_sb, bqr, t1s[0][:], t2s[0][:], ('Q', tt), tt, QW, marks)
        marks["tab_reuse"] = M("dv")
    p1b_done = M("ac")

    P.fence()
    d0 = E("vector", lambda e: e.tensor_mul(uacc[:, 2:3], uacc[:, 0:1], uacc[:, 1:2]),
           waits=[("dv", M("dv")), ("ac", p1b_done)], inc="dv")
    du = E("vector", lambda e: e.tensor_copy(out=urow_sc[:, 0:1], in_=uacc[:, 2:3]),
           waits=[("dv", d0)], inc="dv")
    pu = E("tensor", lambda e: e.transpose(pb[7][0:16, 0:16], urow_sc[0:16, 0:16],
                                           identf[0:16, 0:16]),
           waits=[("dv", du)], inc="pe")
    av = E("scalar", lambda e: e.copy(out=urow_r[0:1, 0:16], in_=pb[7][0:1, 0:16]),
           waits=[("pe", pu)], inc="ac")
    dmx = E("vector", lambda e: e.tensor_max(urow_r[0:1, 0:8], urow_r[0:1, 0:8],
                                             urow_r[0:1, 8:16]), waits=[("ac", av)], inc="dv")
    pu2 = E("tensor", lambda e: e.matmul(pb[7][:, 0:8], ones_f[0:1, :], urow_r[0:1, 0:8],
                                         start=True, stop=True), waits=[("dv", dmx)], inc="pe")
    av2 = E("scalar", lambda e: e.activation(out=bias16[:], in_=pb[7][:, 0:8], func=AF.Ln),
           waits=[("pe", pu2)], inc="ac")
    av3 = E("scalar", lambda e: e.activation(out=bias16[:], in_=bias16[:], func=AF.Exp, scale=0.5),
            waits=[("ac", av2)], inc="ac")
    d1 = E("vector", lambda e: e.tensor_scalar_mul(bias16[:], bias16[:], -0.125),
          waits=[("ac", av3)], inc="dv")
    pu3 = E("tensor", lambda e: e.matmul(pb[6][:, 0:4 * NAMB], ones_f[0:1, :], pflag[0:1, :],
                                         start=True, stop=True), inc="pe")
    av4 = E("scalar", lambda e: e.copy(out=pfcol[:], in_=pb[6][:, 0:4 * NAMB]),
           waits=[("pe", pu3)], inc="ac")
    for a in range(4 * NAMB):
        d2 = E("vector", lambda e, a=a: e.tensor_scalar_add(
            biasamb[:, a * 8:(a + 1) * 8], bias16[:], pfcol[:, a:a + 1]),
            waits=[("dv", d1), ("ac", av4)], inc="dv")
    u_done_dv = M("dv")
    if debug:
        E("sync", lambda e: e.dma_start(out=dbg["b16"], in_=bias16[:]), waits=[("dv", u_done_dv)],
          inc="q_dbg", incval=16)

    P.fence()
    esA.close()
    esB = ExitStack()
    scope[0] = esB
    KEYS = lambda s: (s + 1) * 4 * QTW
    Kbuf = [sb(f"Kbuf{j}", [128, 4 * QTW * 4], BF16) for j in range(2)]
    Vbuf = [sb(f"Vbuf{j}", [128, (4 * QTW * 4 // 128) * VW], BF16) for j in range(2)]
    Kd = [sb(f"Kd{j}", [128, NK * 128], BF16) for j in range(2)]
    Vd = [sb(f"Vd{j}", [128, NK * VW], BF16) for j in range(2)]
    Pbuf = [sb(f"Pbuf{j}", [128, 2 * QTW], BF16) for j in range(2)]
    rsc = sb("rsc", [128, 4 * QTW], BF16)
    ctmp = sb("ctmp", [128, 4 * QTW], BF16)
    s2 = [pd[0], pd[1]]
    cA, cB = pb[4], pb[5]
    dvl, dvh = pb[6], pb[7]

    def p3_loads(u, marks):
        s, hp = divmod(u, HP)
        j = u % 2
        kq = E("sync", lambda e, s=s, hp=hp, j=j: e.dma_start(
            out=Kbuf[j][:, 0:KEYS(s)], in_=K_dram[hp, :, 0:KEYS(s)]),
            waits=[("ac", marks.get(f"kbuf_free{j}", 0))] +
                  ([("q_ko0", M("q_ko0")), ("q_ko1", M("q_ko1"))] if (s, hp) == (0, 0) else []),
            inc=f"q_K{j}", incval=16)
        vq = E("scalar", lambda e, s=s, hp=hp, j=j: e.dma_start(
            out=Vbuf[j][:, 0:(KEYS(s) // 128) * VW], in_=V_dram[hp, :, 0:(KEYS(s) // 128) * VW]),
            waits=([("q_vo0", M("q_vo0")), ("q_vo1", M("q_vo1"))] if (s, hp) == (0, 0) else []),
            inc=f"q_V{j}", incval=16)
        col = s * HP + hp
        kdq = E("gpsimd", lambda e, col=col, j=j: e.indirect_dma_start(
            out=Kd[j][:], out_offset=None, in_=K_dram.rearrange("h p s -> (h p) s"),
            in_offset=IndirectOffsetOnAxis(ap=kidx[:, col:col + 1], axis=1)),
            waits=[("ac", marks.get(f"kbuf_free{j}", 0))], inc=f"q_d{j}", incval=16)
        vdq = E("gpsimd", lambda e, col=col, j=j: e.indirect_dma_start(
            out=Vd[j][:], out_offset=None, in_=V_dram.rearrange("h p s -> (h p) s"),
            in_offset=IndirectOffsetOnAxis(ap=vidx[:, col:col + 1], axis=1)),
            inc=f"q_d{j}", incval=16)
        marks[f"kload{j}"] = (kq, vq, kdq, vdq)

    NUNITS = 4 * HP
    m3 = {"exp": {}, "pv": {}, "ctx_free": 0, "dv_free": 0}
    p3_loads(0, m3)
    for u in range(NUNITS):
        s, hp = divmod(u, HP)
        j = u % 2
        if u + 1 < NUNITS:
            p3_loads(u + 1, m3)
        kq, vq, kdq, vdq = m3[f"kload{j}"]
        qcol = hp * OWN + s * QTW
        ntile = NK + (4 * s + 3) * NK

        def src_of(kc, j=j):
            if kc < NK:
                return (lambda kc=kc: Kd[j][0:64, kc * 128:(kc + 1) * 128],
                        lambda kc=kc: Kd[j][64:128, kc * 128:(kc + 1) * 128],
                        kc * VW, Vd[j])
            ki = kc - NK
            return (lambda ki=ki: Kbuf[j][0:64, ki * 128:(ki + 1) * 128],
                    lambda ki=ki: Kbuf[j][64:128, ki * 128:(ki + 1) * 128],
                    ki * VW, Vbuf[j])

        def emit_S(kc):
            it = kc % 2
            klo, khi, _, _ = src_of(kc)
            w_s = [("ac", m3["exp"].get((u, kc - 2), m3["exp"].get("prev_last", 0) if kc < 2 else 0))]
            if kc == 0:
                w_s += [(f"q_K{j}", kq), (f"q_d{j}", vdq), ("dv", m3["dv_free"])]
            E("tensor", lambda e, klo=klo, it=it, qcol=qcol: e.matmul(
                s2[it][:, 0:QTW], klo(), Q_sb[0:64, qcol:qcol + QTW], start=True, stop=True),
                waits=w_s, inc="pe")
            return E("tensor", lambda e, khi=khi, it=it, qcol=qcol: e.matmul(
                s2[it][:, QTW:2 * QTW], khi(), Q_sb[64:128, qcol:qcol + QTW], start=True, stop=True,
                tile_position=(64, 0)), inc="pe")

        m3["smark"] = {0: emit_S(0)}
        for kc in range(ntile):
            it = kc % 2
            diag = kc < NK
            _, _, vcb, vsrc = src_of(kc)
            if kc + 1 < ntile:
                m3["smark"][kc + 1] = emit_S(kc + 1)
            if diag or (kc - NK) < 4 * s * NK:
                blo = lambda hp=hp: bias16[:, hp:hp + 1]
            else:
                a = s * NAMB + (kc - NK - 4 * s * NK)
                blo = lambda a=a, hp=hp: biasamb[:, a * 8 + hp:a * 8 + hp + 1]
            wexp = [("pe", m3["smark"][kc])]
            if (u, kc - 2) in m3["pv"]:
                wexp.append(("pe", m3["pv"][(u, kc - 2)]))
            e2 = E("scalar", lambda e, it=it, blo=blo: e.activation(
                out=Pbuf[it][:, 0:2 * QTW], in_=s2[it][:, 0:2 * QTW], func=AF.Exp, scale=0.125,
                bias=blo()), waits=wexp, inc="ac")
            m3["exp"][(u, kc)] = e2
            wpv = [("ac", e2)]
            if diag:
                g1 = E("gpsimd", lambda e, it=it, kc=kc: e.affine_select(
                    out=Pbuf[it][:, 0:QTW], in_=Pbuf[it][:, 0:QTW], compare_op=ALU.is_ge,
                    fill=0.0, base=-(kc * 128), pattern=[[1, QTW]], channel_multiplier=-1),
                    waits=[("ac", e2)], inc="gp")
                g2 = E("gpsimd", lambda e, it=it, kc=kc: e.affine_select(
                    out=Pbuf[it][:, QTW:2 * QTW], in_=Pbuf[it][:, QTW:2 * QTW], compare_op=ALU.is_ge,
                    fill=0.0, base=-(kc * 128), pattern=[[1, QTW]], channel_multiplier=-1),
                    inc="gp")
                wpv = [("gp", g2)]
            if kc == 0:
                wpv.append((f"q_V{j}", vq))
                wpv.append(("dv", m3["ctx_free"]))
            p1_ = E("tensor", lambda e, vsrc=vsrc, vcb=vcb, it=it, kc=kc, ntile=ntile: e.matmul(
                cA[0:65, 0:QTW], vsrc[:, vcb:vcb + 65], Pbuf[it][:, 0:QTW],
                start=(kc == 0), stop=(kc == ntile - 1)), waits=wpv, inc="pe")
            p2_ = E("tensor", lambda e, vsrc=vsrc, vcb=vcb, it=it, kc=kc, ntile=ntile: e.matmul(
                cB[0:65, 0:QTW], vsrc[:, vcb + 65:vcb + 130], Pbuf[it][:, QTW:2 * QTW],
                start=(kc == 0), stop=(kc == ntile - 1)), inc="pe")
            m3["pv"][(u, kc)] = p2_
        m3["exp"]["prev_last"] = m3["exp"][(u, ntile - 1)]
        pv_done = m3["pv"][(u, ntile - 1)]
        a1 = E("scalar", lambda e: e.activation(out=rsc[64:65, 0:QTW], in_=cA[64:65, 0:QTW],
                                                func=AF.Ln), waits=[("pe", pv_done)], inc="ac")
        a2 = E("scalar", lambda e: e.activation(out=rsc[64:65, QTW:2 * QTW], in_=rsc[64:65, 0:QTW],
                                                func=AF.Exp, scale=-1.0), waits=[("ac", a1)], inc="ac")
        a3 = E("scalar", lambda e: e.activation(out=rsc[64:65, 2 * QTW:3 * QTW], in_=cB[64:65, 0:QTW],
                                                func=AF.Ln), waits=[("pe", pv_done)], inc="ac")
        a4 = E("scalar", lambda e: e.activation(out=rsc[64:65, 3 * QTW:4 * QTW], in_=rsc[64:65, 2 * QTW:3 * QTW],
                                                func=AF.Exp, scale=-1.0), waits=[("ac", a3)], inc="ac")
        pdl = E("tensor", lambda e: e.matmul(dvl[0:64, 0:QTW], ones_bf[64:65, 0:64],
                                             rsc[64:65, QTW:2 * QTW], start=True, stop=True,
                                             tile_position=(64, 0)),
                waits=[("ac", a2), ("dv", m3["dv_free"])], inc="pe")
        pdh = E("tensor", lambda e: e.matmul(dvh[0:64, 0:QTW], ones_bf[64:65, 0:64],
                                             rsc[64:65, 3 * QTW:4 * QTW], start=True, stop=True,
                                             tile_position=(64, 0)), waits=[("ac", a4)], inc="pe")
        d1 = E("vector", lambda e: e.tensor_copy(out=ctmp[0:64, 0:QTW], in_=cA[0:64, 0:QTW]),
               waits=[("pe", pdl)], inc="dv")
        d2 = E("vector", lambda e, qcol=qcol: e.tensor_mul(
            ctx_sb[0:64, qcol:qcol + QTW], ctmp[0:64, 0:QTW], dvl[0:64, 0:QTW]),
            waits=[("dv", d1)], inc="dv")
        d3 = E("vector", lambda e: e.tensor_copy(out=ctmp[0:64, QTW:2 * QTW], in_=cB[0:64, 0:QTW]),
               waits=[("pe", pdh)], inc="dv")
        d4 = E("vector", lambda e: e.tensor_mul(
            ctmp[0:64, 2 * QTW:3 * QTW], ctmp[0:64, QTW:2 * QTW], dvh[0:64, 0:QTW]),
            waits=[("dv", d3)], inc="dv")
        m3["ctx_free"] = d4
        ib = (ntile) % 2
        pid = E("tensor", lambda e, ib=ib: e.matmul(
            s2[ib][64:128, 0:QTW], ident64[:], ctmp[0:64, 2 * QTW:3 * QTW],
            start=True, stop=True, tile_position=(0, 64)),
            waits=[("dv", d4), ("ac", m3["exp"]["prev_last"])], inc="pe")
        a5 = E("scalar", lambda e, ib=ib, qcol=qcol: e.copy(
            out=ctx_sb[64:128, qcol:qcol + QTW], in_=s2[ib][64:128, 0:QTW]),
            waits=[("pe", pid)], inc="ac")
        m3["exp"]["prev_last"] = a5
        m3["dv_free"] = d4
        m3[f"kbuf_free{j}"] = a5
    p3_done_ac = M("ac")
    p3_done_dv = M("dv")
    if debug:
        E("sync", lambda e: e.dma_start(out=dbg["ctx"], in_=ctx_sb[:]),
          waits=[("ac", p3_done_ac), ("dv", p3_done_dv)], inc="q_dbg", incval=16)
        E("sync", lambda e: e.dma_start(out=dbg["Q"], in_=Q_sb[:]), inc="q_dbg", incval=16)
        E("sync", lambda e: e.dma_start(out=dbg["K"], in_=K_dram), inc="q_dbg", incval=16)
        E("sync", lambda e: e.dma_start(out=dbg["V"], in_=V_dram), inc="q_dbg", incval=16)

    P.fence()
    esB.close()
    esC = ExitStack()
    scope[0] = esC
    wfcT = sb("wfcT", [128, FF], BF16)
    wfcT2 = sb("wfcT2", [128, 7 * FF], BF16)
    wf = [sb("wf0", [128, D])] * 2
    wfs = sb("wfs", [128, D // 2], BF16)
    wft = [sb(f"wft{i}", [128, D], BF16) for i in range(2)]
    hbuf = [sb(f"hbuf{i}", [128, 512], BF16) for i in range(2)]
    h2buf = sb("h2buf", [128, 512], BF16)
    srow = sb("srow", [1, 1536])
    statmu = sb("statmu", [1, 512])
    statsv = sb("statsv", [1, 512])
    statrs = sb("statrs", [1, 512])
    statmu_b = sb("statmu_b", [1, OWN], BF16)
    statsv_b = sb("statsv_b", [1, OWN], BF16)

    def wfcT_tile(fc, hc):
        base = hc * 128
        if fc == 0:
            return wfcT[:, base:base + 128]
        return wfcT2[:, (fc - 1) * FF + base:(fc - 1) * FF + base + 128]


    trn5 = pb[0].bitcast(BF16)
    fcp = [pb[1], pb[2]]
    stp_s, stp_q = pb[3], pb[4]
    NT5 = OWN // 512
    m5 = {"wft_free": {0: 0, 1: 0}, "trn_free": 0, "hb_free": {0: 0, 1: 0}}
    for tt in range(NT5):
        i5 = tt % 2
        for hc in range(HCN):
            i = hc % 2
            if tt == 0:
                qv = E("sync", lambda e, hc=hc, i=i: e.dma_start(
                    out=wf[i][:], in_=Wfc[hc * 128:(hc + 1) * 128, :]),
                    waits=[("dv", M("dv"))], inc="q_wf0", incval=16)
                for hf in range(2):
                    c0_, c1_ = hf * (D // 2), (hf + 1) * (D // 2)
                    v1 = E("vector", lambda e, i=i, c0_=c0_, c1_=c1_: e.tensor_scalar(
                        out=wfs[:], in0=wf[i][:, c0_:c1_], scalar1=gam[:, 0:1], scalar2=0.5,
                        op0=ALU.mult, op1=ALU.is_gt),
                        waits=[("q_wf0", qv), ("dv", M("dv"))], inc="dv")
                    v2 = E("vector", lambda e, i=i, c0_=c0_, c1_=c1_: e.tensor_scalar(
                        out=wft[i][:, c0_:c1_], in0=wf[i][:, c0_:c1_], scalar1=gam[:, 0:1],
                        scalar2=-0.5, op0=ALU.mult, op1=ALU.is_lt), waits=[("q_wf0", qv)], inc="dv")
                    v3 = E("vector", lambda e, i=i, c0_=c0_, c1_=c1_: e.tensor_sub(
                        wft[i][:, c0_:c1_], wfs[:], wft[i][:, c0_:c1_]), waits=[("dv", v2)], inc="dv")
                m5["wft_free"][i] = v3
                pts = []
                for fc in range(8):
                    pt = E("tensor", lambda e, i=i, fc=fc: e.transpose(
                        trn5[:, fc * 128:(fc + 1) * 128], wft[i][:, fc * 128:(fc + 1) * 128],
                        identb[:]),
                        waits=([("dv", v3), ("ac", m5["trn_free"])] if fc == 0 else []), inc="pe")
                av = E("scalar", lambda e, hc=hc: e.copy(out=wfcT[:, hc * 128:(hc + 1) * 128],
                                                         in_=trn5[:, 0:128]),
                       waits=[("pe", M("pe"))], inc="ac")
                for fc in range(1, 8):
                    av = E("scalar", lambda e, hc=hc, fc=fc: e.copy(
                        out=wfcT2[:, (fc - 1) * FF + hc * 128:(fc - 1) * FF + hc * 128 + 128],
                        in_=trn5[:, fc * 128:(fc + 1) * 128]), inc="ac")
                m5["trn_free"] = av
                m5[f"wfcT_ready{hc}"] = av
            fi = hc % 2
            wfcq = [("ac", m5[f"wfcT_ready{hc}"])]
            for fc in range(8):
                E("tensor", lambda e, fc=fc, hc=hc, tt=tt, fi=fi: e.matmul(
                    fcp[fi][:, 0:512], wfcT_tile(fc, hc),
                    xn_sb[:, fc * OWN + tt * 512: fc * OWN + (tt + 1) * 512],
                    start=(fc == 0), stop=(fc == 7)),
                    waits=(wfcq + [("ac", m5[f"fcp_free{fi}"] if f"fcp_free{fi}" in m5 else 0)]) if fc == 0 else [],
                    inc="pe")
            pv = M("pe")
            hb = hc % 2
            a1 = E("scalar", lambda e, hb=hb, fi=fi: e.activation(
                out=hbuf[hb][:], in_=fcp[fi][:, 0:512], func=AF.Gelu),
                waits=[("pe", pv), (f"q_h{hb}", m5["hb_free"][hb]), ("dv", m5.get(f"h2_{hb}", 0))], inc="ac")
            m5[f"fcp_free{fi}"] = a1
            d1 = E("vector", lambda e, hb=hb: e.tensor_mul(h2buf[:], hbuf[hb][:], hbuf[hb][:]),
                   waits=[("ac", a1), ("pe", m5.get("st_pe", 0))], inc="dv")
            m5[f"h2_{hb}"] = d1
            E("tensor", lambda e, hc=hc, hb=hb: e.matmul(
                stp_s[0:1, 0:512], ones_bf[:, 0:1], hbuf[hb][:],
                start=(hc == 0), stop=(hc == HCN - 1)), waits=[("ac", a1)], inc="pe")
            E("tensor", lambda e, hc=hc: e.matmul(
                stp_q[0:1, 0:512], ones_bf[:, 0:1], h2buf[:],
                start=(hc == 0), stop=(hc == HCN - 1)), waits=[("dv", d1)], inc="pe")
            m5["st_pe"] = M("pe")
            qh = E("sync", lambda e, hc=hc, tt=tt, hb=hb: e.dma_start(
                out=h_dram[hc, :, tt * 512:(tt + 1) * 512], in_=hbuf[hb][:]),
                waits=[("ac", a1)], inc=f"q_h{hb}", incval=16)
            m5["hb_free"][hb] = qh
        pv = M("pe")
        a1 = E("scalar", lambda e, tt=tt: e.activation(
            out=statmu[0:1, 0:512], in_=stp_s[0:1, 0:512],
            func=AF.Identity, scale=-1.0 / FF),
            waits=[("pe", pv), ("dv", m5.get("stat_dv", 0))], inc="ac")
        d1 = E("vector", lambda e, tt=tt: e.scalar_tensor_tensor(
            out=srow[0:1, 0:512], in0=statmu[0:1, 0:512], scalar=1.0,
            in1=statmu[0:1, 0:512], op0=ALU.mult, op1=ALU.mult),
            waits=[("ac", a1)], inc="dv")
        d2 = E("vector", lambda e, tt=tt: e.scalar_tensor_tensor(
            out=srow[0:1, 512:1024], in0=stp_q[0:1, 0:512], scalar=1.0 / FF, in1=srow[0:1, 0:512],
            op0=ALU.mult, op1=ALU.subtract), waits=[("dv", d1), ("pe", pv)], inc="dv")
        a2 = E("scalar", lambda e: e.activation(out=srow[0:1, 1024:1536], in_=srow[0:1, 512:1024],
                                                func=AF.Ln, bias=epscol[0:1, 0:1]), waits=[("dv", d2)], inc="ac")
        a3 = E("scalar", lambda e, tt=tt: e.activation(
            out=statsv[0:1, 0:512], in_=srow[0:1, 1024:1536],
            func=AF.Exp, scale=0.5), waits=[("ac", a2)], inc="ac")
        a4 = E("scalar", lambda e, tt=tt: e.activation(
            out=statrs[0:1, 0:512], in_=srow[0:1, 1024:1536],
            func=AF.Exp, scale=-0.5),
            waits=[("ac", a2), ("q_rs", m5.get("rs_q", 0))], inc="ac")
        qrs = E("sync", lambda e, tt=tt: e.dma_start(
            out=statrs_dram[0:1, tt * 512:(tt + 1) * 512], in_=statrs[0:1, 0:512]),
            waits=[("ac", a4)], inc="q_rs", incval=16)
        m5["rs_q"] = qrs
        E("vector", lambda e, tt=tt: e.tensor_copy(
            out=statmu_b[0:1, tt * 512:(tt + 1) * 512], in_=statmu[0:1, 0:512]),
            waits=[("ac", a4)], inc="dv")
        E("vector", lambda e, tt=tt: e.tensor_copy(
            out=statsv_b[0:1, tt * 512:(tt + 1) * 512], in_=statsv[0:1, 0:512]),
            inc="dv")
        m5["stat_dv"] = M("dv")
        m5["st_pe_gate"] = M("ac")
    p5_done = M("dv")
    E("sync", lambda e: e.dma_start(out=statb_dram[0:1, :], in_=statmu_b[:]),
      waits=[("dv", p5_done)], inc="q_stat", incval=16)
    E("sync", lambda e: e.dma_start(out=statb_dram[1:2, :], in_=statsv_b[:]),
      inc="q_stat", incval=16)
    q_stat_done = M("q_stat")
    if debug:
        E("sync", lambda e: e.dma_start(out=dbg["h"], in_=h_dram),
          waits=[("q_h0", M("q_h0")), ("q_h1", M("q_h1")), ("dv", p5_done)],
          inc="q_dbg", incval=16)
        E("sync", lambda e: e.dma_start(out=dbg["stat"][0:1, :], in_=statmu[:]),
          waits=[("dv", p5_done), ("ac", M("ac"))], inc="q_dbg", incval=16)
        E("sync", lambda e: e.dma_start(out=dbg["stat"][1:2, :], in_=statsv[:]), inc="q_dbg", incval=16)
        E("sync", lambda e: e.dma_start(out=dbg["stat"][2:3, :], in_=statrs[:]), inc="q_dbg", incval=16)

    P.fence()
    esC.close()
    esD = ExitStack()
    scope[0] = esD
    wpp = sb("wpp", [128, HCN * D], BF16)
    borow = sb("borow", [1, D], BF16)
    statmu_b6 = sb("statmu_b6", [1, 512], BF16)
    statsv_b6 = sb("statsv_b6", [1, 512], BF16)
    statrs6 = sb("statrs6", [1, 512])
    wo_sb = sb("wo_sb", [128, 8 * D], BF16)
    hcol = [sb("hcol0", [128, HCN * 128], BF16)] * 2
    gbuf = sb("gbuf", [128, 512], BF16)
    rstdc = sb("rstdc", [128, 1])
    xf6 = [sb("xf60", [128, 512])] * 2
    out6 = sb("out6", [128, 512])
    E("gpsimd", lambda e: e.dma_start(out=borow[:], in_=bo[None, :]), inc="q_w6", incval=16)
    qw = E("gpsimd", lambda e: e.dma_start(out=wpp[:].rearrange("p (c n) -> p c n", c=HCN),
                                           in_=wp_dram.rearrange("c p n -> p c n")),
           waits=[("q_wp0", M("q_wp0")), ("q_wp1", M("q_wp1"))], inc="q_w6", incval=16)
    qw2 = E("gpsimd", lambda e: e.dma_start(out=wo_sb[:].rearrange("p (c n) -> p c n", c=8),
                                            in_=Wo_p.rearrange("(c p) n -> p c n", p=128)),
            inc="q_w6", incval=16)
    pjp = [pb[0], pb[1]]
    wop = [pb[2], pb[3]]
    rsp = pb[4]
    m6 = {}
    for tc in range(NT6):
        i = tc % 2
        qh = E("sync", lambda e, tc=tc, i=i: e.dma_start(
            out=hcol[i][:].rearrange("p (c t) -> p c t", c=HCN),
            in_=h_dram[:, :, tc * 128:(tc + 1) * 128].rearrange("c p t -> p c t")),
            waits=[("pe", m6.get(f"hcol_free{i}", 0))] +
                  ([("q_h0", M("q_h0")), ("q_h1", M("q_h1"))] if tc < 2 else []),
            inc=f"q_c{i}", incval=16)

        if tc % 4 == 0:
            stw = [("pe", M("pe"))] + ([("q_stat", q_stat_done)] if tc == 0 else [])
            qrs6 = E("sync", lambda e, tc=tc: e.dma_start(
                out=statrs6[0:1, 0:512], in_=statrs_dram[0:1, tc * 128:tc * 128 + 512]),
                waits=stw, inc="q_st6", incval=16)
            E("sync", lambda e, tc=tc: e.dma_start(
                out=statmu_b6[0:1, 0:512], in_=statb_dram[0:1, tc * 128:tc * 128 + 512]),
                inc="q_st6", incval=16)
            E("sync", lambda e, tc=tc: e.dma_start(
                out=statsv_b6[0:1, 0:512], in_=statb_dram[1:2, tc * 128:tc * 128 + 512]),
                inc="q_st6", incval=16)
            m6["rs6_q"] = M("q_st6")
        pr = E("tensor", lambda e, tc=tc: e.matmul(
            rsp[:, 0:1], statrs6[0:1, (tc % 4) * 128:(tc % 4 + 1) * 128], ones_f[0:1, 0:1],
            start=True, stop=True),
            waits=[("ac", m6.get("rst_free", 0)), ("q_st6", m6["rs6_q"])], inc="pe")
        m6["rsp_pe"] = pr
        ar = E("scalar", lambda e: e.copy(out=rstdc[:], in_=rsp[:, 0:1]), waits=[("pe", pr)], inc="ac")
        m6["rst_free"] = ar
        for half in range(2):
            for hc in range(HCN):
                E("tensor", lambda e, hc=hc, half=half, tc=tc, i=i: e.matmul(
                    pjp[half][:, 0:512], hcol[i][:, hc * 128:(hc + 1) * 128],
                    wpp[:, hc * D + half * 512:hc * D + (half + 1) * 512],
                    start=(hc == 0), stop=False),
                    waits=([(f"q_c{tc % 2}", qh), ("q_w6", 48), ("ac", m6.get(f"pj_free{half}", 0))]
                           if hc == 0 else []), inc="pe")
            E("tensor", lambda e, half=half, tc=tc: e.matmul(
                pjp[half][:, 0:512], statmu_b6[0:1, (tc % 4) * 128:(tc % 4 + 1) * 128],
                c1row[0:1, half * 512:(half + 1) * 512], start=False, stop=False),
                waits=[("q_st6", m6["rs6_q"])], inc="pe")
            E("tensor", lambda e, half=half, tc=tc: e.matmul(
                pjp[half][:, 0:512], statsv_b6[0:1, (tc % 4) * 128:(tc % 4 + 1) * 128],
                c0row[0:1, half * 512:(half + 1) * 512], start=False, stop=True), inc="pe")
            pv = M("pe")
            ag = E("scalar", lambda e, half=half: e.activation(
                out=gbuf[:, 0:512], in_=pjp[half][:, 0:512],
                func=AF.Silu, scale=rstdc[:]),
                waits=[("pe", pv), ("ac", ar), ("dv", m6.get("g_free", 0))], inc="ac")
            m6[f"pj_free{half}"] = ag
            for cc in range(8):
                E("tensor", lambda e, cc=cc, half=half, tc=tc: e.matmul(
                    wop[half][:, 0:512], ctx_sb[:, cc * OWN + tc * 128:cc * OWN + (tc + 1) * 128],
                    wo_sb[:, cc * D + half * 512:cc * D + (half + 1) * 512],
                    start=(cc == 0), stop=False),
                    waits=([("q_w6", 48), ("dv", m6.get(f"wo_free{half}", 0))] if cc == 0 else []),
                    inc="pe")
            E("tensor", lambda e, half=half: e.matmul(
                wop[half][:, 0:512], ones_bf[0:1, 0:128], borow[0:1, half * 512:(half + 1) * 512],
                start=False, stop=True), inc="pe")
            pv2 = M("pe")
            d1 = E("vector", lambda e, half=half: e.tensor_mul(
                out6[:, 0:512], wop[half][:, 0:512],
                gbuf[:, 0:512]),
                waits=[("pe", pv2), ("ac", ag), ("q_y", m6.get("y_q", 0))], inc="dv")
            m6[f"wo_free{half}"] = d1
            m6["g_free"] = d1
            qx = E("sync", lambda e, tc=tc, half=half: e.dma_start(
                out=xf6[0][:, 0:512],
                in_=x_own[tc * 128:(tc + 1) * 128, half * 512:(half + 1) * 512]),
                waits=[("dv", m6.get("x_free", 0))], inc="q_f0", incval=16)
            d2 = E("vector", lambda e, half=half: e.tensor_add(
                out6[:, 0:512], out6[:, 0:512], xf6[0][:, 0:512]),
                waits=[("q_f0", qx), ("dv", d1)], inc="dv")
            m6["x_free"] = d2
            yq = E("sync", lambda e, tc=tc, half=half: e.dma_start(
                out=y[tc * 128:(tc + 1) * 128, half * 512:(half + 1) * 512], in_=out6[:, 0:512]),
                waits=[("dv", d2)], inc="q_y", incval=16)
            m6["y_q"] = yq
        m6[f"hcol_free{i}"] = M("pe")
        m6["rst_free"] = M("ac")
    P.fence()
    P.realize()
    esD.close()
    esP.close()
    meta = dict(SEQ=SEQ, QTW=QTW, NK=NK, OWN=OWN, dbg=sorted(dbg.keys()))
    return nc, meta


BF = ml_dtypes.bfloat16




def eo_perm():
    perm = np.zeros(D, dtype=np.int64)
    for hp in range(HP):
        for i in range(64):
            perm[hp * 128 + i] = 128 * hp + 2 * i
            perm[hp * 128 + 64 + i] = 128 * hp + 2 * i + 1
    return perm


def wo_row_perm():
    perm = np.zeros(D, dtype=np.int64)
    for hp in range(HP):
        for i in range(64):
            perm[hp * 128 + i] = 64 * hp + i
            perm[hp * 128 + 64 + i] = 512 + 64 * hp + i
    return perm


def rope_tables(SEQ, positions):
    i = np.arange(D // 2, dtype=np.float32)
    fc = np.power(np.float32(10000.0), -2.0 * i / np.float32(D)).astype(np.float32)
    pos = positions.astype(np.float32)
    T1 = np.zeros((HP, 128, len(pos)), dtype=np.float32)
    T2 = np.zeros((HP, 128, len(pos)), dtype=np.float32)
    for hp in range(HP):
        f = fc[64 * hp:64 * hp + 64]
        ang = pos[None, :] * f[:, None]
        c, s = np.cos(ang), np.sin(ang)
        T1[hp, 0:64] = c
        T1[hp, 64:128] = s
        T2[hp, 0:64] = s
        T2[hp, 64:128] = c
    NT = len(pos) // 512
    T1 = T1.reshape(HP, 128, NT, 512).transpose(2, 1, 0, 3).reshape(NT, 128, HP * 512)
    T2 = T2.reshape(HP, 128, NT, 512).transpose(2, 1, 0, 3).reshape(NT, 128, HP * 512)
    return np.ascontiguousarray(T1).astype(BF), np.ascontiguousarray(T2).astype(BF)


def own_rows(SEQ, m):
    QTW = SEQ // 16
    rows = []
    for s in range(4):
        t = m + 4 * s
        rows.append(np.arange(t * QTW, (t + 1) * QTW))
    return np.concatenate(rows)


def prep_core(inp, c, SEQ):
    QTW = SEQ // 16
    NK = QTW // 128
    NAMB = 3 * NK
    NKS_T = SEQ // 128
    b, m = divmod(c, 4)
    x = np.asarray(inp["L0_attn_x"], dtype=np.float32)
    rows = own_rows(SEQ, m)
    perm = eo_perm()
    woperm = wo_row_perm()

    Wq = np.asarray(inp["L0_attn_Wq"], np.float32)
    Wk = np.asarray(inp["L0_attn_Wk"], np.float32)
    Wv = np.asarray(inp["L0_attn_Wv"], np.float32)
    Wo = np.asarray(inp["L0_attn_Wo"], np.float32)
    bq = np.asarray(inp["L0_attn_bq"], np.float32)
    bk = np.asarray(inp["L0_attn_bk"], np.float32)
    bv = np.asarray(inp["L0_attn_bv"], np.float32)
    bo = np.asarray(inp["L0_attn_bo"], np.float32)

    Wv_e = np.zeros((D, VROW), np.float32)
    bv_e = np.zeros(VROW, np.float32)
    for hp in range(HP):
        blk = hp * VW
        Wv_e[:, blk:blk + 64] = Wv[:, 64 * hp:64 * hp + 64]
        Wv_e[:, blk + 65:blk + 129] = Wv[:, 512 + 64 * hp:512 + 64 * hp + 64]
        bv_e[blk:blk + 64] = bv[64 * hp:64 * hp + 64]
        bv_e[blk + 64] = 1.0
        bv_e[blk + 65:blk + 129] = bv[512 + 64 * hp:512 + 64 * hp + 64]
        bv_e[blk + 129] = 1.0

    T1k, T2k = rope_tables(SEQ, np.arange(SEQ))
    T1q, T2q = rope_tables(SEQ, rows)
    Tk = np.ascontiguousarray(np.concatenate([T1k, T2k], axis=2))
    Tq = np.ascontiguousarray(np.concatenate([T1q, T2q], axis=2))

    imat = np.vstack([np.eye(64), -np.eye(64)]).astype(BF)
    ind = np.zeros((HP, 128, 16), np.float32)
    for hp in range(HP):
        ind[hp, 0:64, hp] = 1.0
        ind[hp, 64:128, 8 + hp] = 1.0

    p = np.arange(128, dtype=np.int64)
    kidx = np.zeros((128, 32), np.uint32)
    vidx = np.zeros((128, 32), np.uint32)
    for s in range(4):
        t = m + 4 * s
        for hp in range(HP):
            kidx[:, s * HP + hp] = hp * 128 * SEQ + p * SEQ + t * QTW
            vidx[:, s * HP + hp] = hp * 128 * (NKS_T * VW) + p * (NKS_T * VW) + t * NK * VW
    pflag = np.zeros((1, 4 * NAMB), np.float32)
    for s in range(4):
        for a in range(NAMB):
            pflag[0, s * NAMB + a] = 0.0 if a < m * NK else -60000.0

    return {
        "x_kv": np.ascontiguousarray(x[b]),
        "x_own": np.ascontiguousarray(x[b][rows]),
        "Wq_p": np.ascontiguousarray(Wq[:, perm]),
        "Wk_p": np.ascontiguousarray(Wk[:, perm]),
        "Wv_e": Wv_e,
        "bq_p": np.ascontiguousarray(bq[perm]).reshape(8, 128),
        "bk_p": np.ascontiguousarray(bk[perm]).reshape(8, 128),
        "bv_e": bv_e,
        "Wo_p": np.ascontiguousarray(Wo[woperm, :]),
        "bo": bo,
        "Wfc": np.asarray(inp["L0_ffn_W_fc"], np.float32),
        "Wpj": np.asarray(inp["L0_ffn_W_proj"], np.float32),
        "lnw": np.asarray(inp["L0_ffn_ln_w"], np.float32).reshape(32, 128),
        "lnb": np.asarray(inp["L0_ffn_ln_b"], np.float32).reshape(32, 128),
        "Tk": Tk, "Tq": Tq,
        "imat": imat,
        "ind": ind.astype(BF),
        "kidx": kidx, "vidx": vidx,
        "pflag": pflag,
    }




_CACHE = {}


def _get_nc():
    if "nc" not in _CACHE:
        _CACHE["nc"] = build(SEQ=8192, debug=False)[0]
    return _CACHE["nc"]


def kernel(**inputs):
    from concourse.bass_utils import run_bass_kernel_spmd
    SEQ = 8192
    nc = _get_nc()
    in_maps = [prep_core(inputs, c, SEQ) for c in range(8)]
    res = run_bass_kernel_spmd(nc, in_maps, core_ids=list(range(8)))
    out = np.zeros((2, SEQ, D), np.float32)
    for c in range(8):
        b, m = divmod(c, 4)
        out[b][own_rows(SEQ, m)] = res.results[c]["y"]
    return out
